# version 22
# baseline (speedup 1.0000x reference)
import numpy as np
import ml_dtypes
import concourse.bass as bass
import concourse.mybir as mybir
from concourse.bass_utils import run_bass_kernel_spmd
from concourse.alu_op_type import AluOpType as OP

F32 = mybir.dt.float32
BF16 = mybir.dt.bfloat16
AF = mybir.ActivationFunctionType
AXX = mybir.AxisListType.X

DM = 1024
DI = 2048
DS = 64
NH = 32
HD = 64
CDIM = 2176
CH = 128
NFC = 17
EPS = 1e-5

ENGS = ("pe", "act", "dve", "pool", "sp")
ENG_ATTR = {"pe": "tensor", "act": "scalar", "dve": "vector", "pool": "gpsimd", "sp": "sync"}


class Builder:
    def __init__(self):
        self.ops = []
        self.eng_n = {e: 0 for e in ENGS}
        self.dma_n = {}
        self.last_w = {}
        self.readers = {}

    def _dep_waits(self, eng, reads, writes):
        waits = []
        def need(tag):
            if tag is None:
                return
            if tag[0] == "e":
                if tag[1] != eng or eng != "pe":
                    waits.append(tag)
            else:
                waits.append(tag)
        for k in reads:
            need(self.last_w.get(k))
        for k in writes:
            need(self.last_w.get(k))
            for t in self.readers.get(k, ()):
                need(t)
        return waits

    def op(self, eng, fn, reads=(), writes=(), dma=None, tag=""):
        reads = list(reads); writes = list(writes)
        waits = self._dep_waits(eng, reads, writes)
        if dma is None:
            self.eng_n[eng] += 1
            mytag = ("e", eng, self.eng_n[eng])
        else:
            k = self.dma_n.get(dma, 0)
            if k > 0:
                waits.append(("d", dma, k))
            self.dma_n[dma] = k + 1
            mytag = ("d", dma, k + 1)
        for k2 in reads:
            self.readers.setdefault(k2, []).append(mytag)
        for k2 in writes:
            self.last_w[k2] = mytag
            self.readers[k2] = []
        self.ops.append(dict(eng=eng, fn=fn, waits=waits, tag=mytag, dbg=tag))
        return mytag

    def barrier(self, tag="bar"):
        marks = {}
        for e in ENGS:
            fn, rd, wr = self.nopfns[e]
            marks[e] = self.op(e, fn, reads=rd, writes=wr, tag=f"{tag}-mark")
        dma_marks = [("d", q, self.dma_n[q]) for q in self.dma_n if self.dma_n[q] > 0]
        marks2 = {}
        for e in ENGS:
            fn, rd, wr = self.nopfns[e]
            marks2[e] = self.op(e, fn, reads=rd, writes=wr, tag=f"{tag}-wait")
            self.ops[-1]["waits"].extend(
                [marks[o] for o in ENGS if o != e] + dma_marks)
        for k in list(self.last_w):
            t = self.last_w[k]
            if t[0] == "e":
                self.last_w[k] = marks2[t[1]]
        self.readers = {k: [] for k in self.readers}

    def emit(self, nc, engines, esems, dsems):
        awaited = {e: set() for e in ENGS}
        for o in self.ops:
            for t in o["waits"]:
                if t[0] == "e":
                    awaited[t[1]].add(t[2])
        rank = {}
        inc_set = {}
        for e in ENGS:
            s = sorted(awaited[e])
            inc_set[e] = set(s)
            rank[e] = {idx: i + 1 for i, idx in enumerate(s)}
        self.final_counts = {e: len(awaited[e]) for e in ENGS}

        streams = {e: [o for o in self.ops if o["eng"] == e] for e in ENGS}
        wm = {}

        def emit_stream(e, eng_h):
            for o in streams[e]:
                for t in o["waits"]:
                    if t[0] == "e":
                        if t[2] not in rank[t[1]]:
                            continue
                        v = rank[t[1]][t[2]]
                        kk = (e, "e" + t[1])
                        if wm.get(kk, 0) >= v:
                            continue
                        wm[kk] = v
                        eng_h.wait_ge(esems[t[1]], v)
                    else:
                        v = 16 * t[2]
                        kk = (e, "d" + t[1])
                        if wm.get(kk, 0) >= v:
                            continue
                        wm[kk] = v
                        eng_h.wait_ge(dsems[t[1]], v)
                instr = o["fn"](eng_h)
                t = o["tag"]
                if t[0] == "d":
                    instr.then_inc(dsems[t[1]], 16)
                elif t[2] in inc_set[t[1]]:
                    instr.then_inc(esems[t[1]], 1)
        return emit_stream


class Arena:
    def __init__(self, nc, budget_bytes=206000):
        self.nc = nc
        self.budget = budget_bytes
        self.tiles = {}
        self.handle = None
        self.regions = {}
        self.pending = []

    def tile(self, name, shape, dtype, offset):
        per_part = int(np.prod(shape[1:])) * mybir.dt.size(dtype)
        assert offset % 32 == 0, (name, offset)
        assert offset + per_part <= self.budget, (name, offset, per_part, self.budget)
        h = self.nc.virtual_sbuf_tensor(name, list(shape), dtype)
        self.pending.append((h, offset))
        self.tiles[name] = (h, offset, per_part)
        return h

    def allocate(self):
        self.handle = self.nc.sbuf_tensor("arena", [128, self.budget], mybir.dt.uint8)
        self.handle_cm = self.handle
        h = self.handle.__enter__()
        base = self.nc.lookup_mloc(h).addr
        for hnd, off in self.pending:
            self.nc._tensor(hnd.name, hnd.shape, hnd.dtype, type="SB")
            mloc = self.nc.lookup_mloc(hnd)
            mloc.allocated = True
            mloc.addr = base + off
        return h

    def free(self):
        self.handle_cm.__exit__(None, None, None)


def ceil32(x):
    return (x + 31) // 32 * 32


def build(S=2048, budget=212832):
    NCH = S // CH
    SG = min(512, S)
    NSG = S // SG
    nc = bass.Bass("TRN2", target_bir_lowering=False)

    d = {}
    def di(name, shape, dtype=F32, kind="ExternalInput"):
        d[name] = nc.dram_tensor(name, list(shape), dtype, kind=kind).ap()
        return d[name]
    u_tm_d = di("u_tm", [S, DM])
    u_fmT_d = di("u_fmT", [DM, S])
    winT_d = di("winT", [DM, 4256])
    woutT_d = di("woutT", [DI, DM])
    convw_d = di("convw", [128, NFC * 4])
    convb_d = di("convb", [128, NFC])
    dtb_d = di("dtb", [128, 1])
    alog_d = di("alog", [128, 1])
    nwpk_d = di("nwpk", [128, 16])
    dflat_d = di("dflat", [1, DI])
    identf_d = di("identf", [128, 128])
    triu_d = di("triu", [128, 128])
    maskT_d = di("maskT", [128, 128])
    onesf_d = di("onesf", [1, 128])
    wfc_d = di("wfc", [4096, DM])
    wproj_d = di("wproj", [DM, 4096])
    vT_d = di("vT", [DM, 16])
    fbT_d = di("fbT", [16, 16])
    uTlr_d = di("uTlr", [16, 4096])
    gater_d = di("gater", [16, 1])
    out_d = di("out", [S, DM], F32, "ExternalOutput")
    aflat_d = di("acumflat", [NCH, NH * CH], F32, "Internal")
    x1_d = di("x1d", [S, DM], F32, "Internal")
    ternfc_d = di("ternfc", [4096, DM], BF16, "Internal")
    ternproj_d = di("ternproj", [DM, 4096], BF16, "Internal")
    cdf_d = di("cdf", [8, 128], F32, "Internal")

    PERS = 4096
    ar = Arena(nc, budget)
    off = [0]
    def T(name, shape, dtype, at=None):
        if at is not None:
            off[0] = at
        t = ar.tile(name, shape, dtype, off[0])
        off[0] = ceil32(off[0] + int(np.prod(shape[1:])) * mybir.dt.size(dtype))
        return t
    off[0] = budget - PERS
    identf_s = T("identf_s", [128, 128], F32)
    identb_s = T("identb_s", [128, 128], BF16)
    triu_s = T("triu_s", [128, 128], F32)
    maskT_s = T("maskT_s", [128, 128], F32)
    convw_s = T("convw_s", [128, NFC * 4], F32)
    convb_s = T("convb_s", [128, NFC], F32)
    dtb_s = T("dtb_s", [128, 1], F32)
    alog_s = T("alog_s", [128, 1], F32)
    nwpk_s = T("nwpk_s", [128, 16], F32)
    onesf_s = T("onesf_s", [1, 128], F32)
    dumact = T("dumact", [1, 32], F32)
    dumdve = T("dumdve", [1, 32], F32)
    dumpool = T("dumpool", [1, 32], F32)
    cbias = T("cbias", [128, 4], F32)
    ssum = T("ssum", [128, 2], F32)
    xsum = T("xsum", [128, 2], F32)
    x2sum = T("x2sum", [128, 2], F32)
    mstat = T("mstat", [128, 2], F32)
    lnt = T("lnt", [128, 2], F32)
    mm_t = T("mm_t", [128, 2], F32)
    vstat = T("vstat", [128, 2], F32)
    rstd = T("rstd", [128, 2], F32)
    b2 = T("b2", [128, 2], F32)
    nalog = T("nalog", [128, 1], F32)
    scs = T("scs", [128, 2], F32)
    rcs = T("rcs", [128, 1], F32)
    hscs = T("hscs", [128, 1], F32)
    gate_s = T("gate_s", [16, 1], F32)
    assert off[0] <= budget, off[0]

    off[0] = 0
    winT = T("winT_s", [128, 8 * 4256], BF16)
    woutT = T("woutT_s", [128, 16 * 1024], BF16)
    u_tm = T("u_tm_s", [128, DM], F32)
    u_fm = T("u_fm_s", [128, 8 * 256], BF16)
    sz = [T("sz0", [128, DI], BF16), T("sz1", [128, DI], BF16)]
    xbcraw = T("xbcraw", [128, NFC * 260], BF16)
    convtmp = [T(f"convtmp{i}", [128, 512], F32) for i in range(2)]
    xact_pair = T("xact_pair", [128, 16 * 256], BF16)
    B_s2 = T("B_s2", [64, 256], BF16)
    C_s2 = T("C_s2", [64, 256], BF16)
    B_tm2 = T("B_tm2", [128, 128], BF16)
    subl = T("subl", [128, NCH * 32], F32)
    sdol = T("sdol", [128, NCH * 32], F32)
    coefl = T("coefl", [128, NCH * 32], F32)
    crepl = T("crepl", [128, NCH * 32], F32)
    ddb = T("ddb_t", [128, 16 * CH], F32)
    lt = T("lt", [128, 16 * CH], BF16)
    gtm = T("gtm", [128, 128], BF16)
    xt = T("xt", [128, 16 * 64], BF16)
    xtc = T("xtc", [128, 16 * 64], BF16)
    srun_bf = T("srun_bf", [64, DI], BF16)
    srun = T("srun", [128, 1024], F32)
    y1 = T("y1", [128, DI], F32)
    ynorm = T("ynorm", [128, DI], BF16)
    yt = T("yt", [128, DI], BF16)
    drep = T("drep", [128, DI], BF16)
    x1s = T("x1s", [128, DM], F32)
    x1n = T("x1n", [128, DM], F32)
    cdrow8 = T("cdrow8", [1, 128], F32)
    CEND = off[0]
    acum_pk = T("acum_pk", [128, 1024], F32, at=ar.tiles["u_tm_s"][1])
    subt_pk = T("subt_pk", [128, 1024], F32, at=ar.tiles["sz0"][1])
    coeft_pk = T("coeft_pk", [128, 1024], F32, at=ar.tiles["drep"][1])
    adlh = T("adlh", [128, 256], F32, at=ar.tiles["convtmp0"][1])
    wdt_s = T("wdt_s", [128, 8 * 32], BF16, at=ar.tiles["convtmp1"][1])
    y1_off = ar.tiles["y1"][1]
    dt_pk = T("dt_pk", [128, 1024], F32, at=y1_off)
    lndt_pk = T("lndt_pk", [128, 1024], F32)
    ad_pk = T("ad_pk", [128, 1024], F32)
    assert off[0] <= ar.tiles["yt"][1], (off[0], ar.tiles["yt"][1])
    assert CEND <= budget - PERS, (CEND, budget - PERS)

    off[0] = 0
    wfcT = T("wfcT", [128, 8 * 4096], BF16, at=0)
    wprT = T("wprT", [128, 32 * 1024], BF16)
    EW = off[0]
    x1tm = T("x1tm", [128, 4 * 1024], F32)
    x1bf = T("x1bf", [128, 1024], BF16)
    x1fm = T("x1fm", [128, 8 * SG], BF16)
    r2 = T("r2", [128, 32 * SG], BF16)
    hr = [T(f"hr{i}", [128, SG], BF16) for i in range(2)]
    t1b = T("t1b", [16, SG], BF16)
    t2b = T("t2b", [16, SG], BF16)
    vtb = T("vtb", [128, 8 * 16], BF16)
    fbtb = T("fbtb", [16, 16], BF16)
    uTg = T("uTg", [16, 4096], BF16)
    outst = T("outst", [128, DM], F32)
    EEND = off[0]
    assert EEND <= budget - PERS, EEND
    wst = [T("wst0", [128, 2048], F32, at=ar.tiles["ddb_t"][1]),
           T("wst1", [128, 2048], F32, at=ar.tiles["lt"][1]),
           T("wst2", [128, 2048], F32, at=ar.tiles["xbcraw"][1]),
           T("wst3", [128, 2048], F32, at=ar.tiles["xact_pair"][1])]
    trow = [T("trow0", [128, 2048], BF16, at=ar.tiles["srun_bf"][1]),
            T("trow1", [128, 2048], BF16, at=ar.tiles["x1s"][1])]
    trow2 = T("trow2", [128, 2048], BF16, at=ar.tiles["x1n"][1])
    part1 = T("part1", [128, 256], F32, at=ar.tiles["yt"][1])
    part2 = T("part2", [128, 8], F32)
    chs = T("chs", [128, 2], F32)
    cls = T("cls", [128, 2], F32)
    ncls = T("ncls", [128, 2], F32)
    assert off[0] <= ar.tiles["yt"][1] + 4096

    ar.allocate()

    B = Builder()
    import contextlib
    es = contextlib.ExitStack()
    with es:
        pa = es.enter_context(nc.psum_tensor("pa", [128, 1024], F32))
        pb = es.enter_context(nc.psum_tensor("pb", [128, 1024], F32))
        pt = es.enter_context(nc.psum_tensor("pt", [128, 2048], BF16))
        pe_ = es.enter_context(nc.psum_tensor("pe_", [128, 512], F32))
        pf = es.enter_context(nc.psum_tensor("pf", [128, 512], F32))

        B.nopfns = {
            "pe": (lambda e: e.matmul(pa[0:1, 0:8], identb_s[0:1, 0:1],
                                      identb_s[0:1, 0:8], start=True, stop=True),
                   ("identb",), ("pa0",)),
            "act": (lambda e: e.memzero(dumact[:]), ("dumact",), ("dumact",)),
            "dve": (lambda e: e.memset(dumdve[:], 0.0), ("dumdve",), ("dumdve",)),
            "pool": (lambda e: e.memset(dumpool[:], 0.0), ("dumpool",), ("dumpool",)),
            "sp": (lambda e: e.nop(), (), ()),
        }

        op = B.op

        op("pool", lambda e: e.dma_start(
            out=wdt_s[:].rearrange("p (kc c) -> p kc c", c=32),
            in_=winT_d[:, 4224:4256].rearrange("(kc p) c -> p kc c", p=128)),
           writes=("wdt",), dma="wdt", tag="ld-wdt")
        for nm, dst in [("identf", identf_s), ("triu", triu_s), ("maskT", maskT_s),
                        ("convw", convw_s), ("convb", convb_s), ("dtb", dtb_s),
                        ("alog", alog_s), ("nwpk", nwpk_s), ("onesf", onesf_s)]:
            op("sp", lambda e, a=d[nm], b=dst: e.dma_start(out=b[:], in_=a),
               writes=(nm,), dma="cst", tag=f"ld-{nm}")
        op("pool", lambda e: e.dma_start(out=identb_s[:], in_=d["identf"]),
           writes=("identb",), dma="cst2", tag="ld-identb")
        def emit_bigw():
            for kc in range(8):
                op("pool", lambda e, kc=kc: e.dma_start(
                    out=winT[:, kc * 4256:(kc + 1) * 4256],
                    in_=winT_d.rearrange("(kc p) c -> p kc c", p=128)[:, kc]),
                   writes=(f"winT{kc}",), dma="winT", tag=f"ld-winT{kc}")
            for q in range(4):
                op("pool", lambda e, q=q: e.dma_start(
                    out=woutT[:, q * 4096:(q + 1) * 4096],
                    in_=woutT_d.rearrange("(kc p) c -> p kc c", p=128)
                        [:, 4 * q:4 * (q + 1)]),
                   writes=(f"woutT{q}",), dma="woutT", tag=f"ld-woutT{q}")
        op("act", lambda e: e.activation(out=nalog[:], in_=alog_s[:], func=AF.Exp),
           reads=("alog",), writes=("nalog",), tag="expA")
        op("pool", lambda e: e.tensor_scalar(out=nalog[:], in0=nalog[:], scalar1=-1.0,
                                            scalar2=None, op0=OP.mult),
           writes=("nalog",), tag="negA")
        op("pool", lambda e: e.memset(cbias[:, 0:1], EPS), writes=("cbias",), tag="cb0")
        op("pool", lambda e: e.memset(cbias[:, 1:2], 1.0), writes=("cbias",), tag="cb1")
        op("pool", lambda e: e.memset(cbias[:, 2:3], -0.5), writes=("cbias",), tag="cb2")
        op("pool", lambda e: e.memset(cbias[:, 3:4], 0.5), writes=("cbias",), tag="cb3")

        d_items = script_D(B, nc, d, locals())
        di = [0]

        def emitD(n):
            while n > 0 and di[0] < len(d_items):
                d_items[di[0]]()
                di[0] += 1
                n -= 1

        NPAIR = NCH // 2
        op("pool", lambda e: e.memset(dt_pk[:], 0.0), writes=("dt_pk",), tag="z-dtpk")
        for c in range(NCH):
            bse = 64 * (c % 2)
            col = 128 * (c // 2)
            cp = c % 2
            pp = pe_ if cp == 0 else pf
            bk = "pe" if cp == 0 else "pf"
            op("pool", lambda e, c=c: e.dma_start(
                out=u_fm[:, 0:1024].rearrange("p (kc t) -> p kc t", t=128),
                in_=u_fmT_d[:, c * CH:(c + 1) * CH].rearrange("(kc p) t -> p kc t", p=128)),
               writes=("u_fm",), dma=f"ufm{cp}", tag=f"B-ufm{c}")
            for kc in range(8):
                op("pe", lambda e, kc=kc, pp=pp: e.matmul(
                    pp[0:32, 0:128], wdt_s[:, kc * 32:(kc + 1) * 32],
                    u_fm[:, kc * 128:(kc + 1) * 128],
                    start=(kc == 0), stop=(kc == 7)),
                   reads=("wdt", "u_fm"), writes=(bk,), tag=f"B-dt{c}.{kc}")
            op("act", lambda e, bse=bse, col=col, pp=pp: e.copy(
                out=dt_pk[bse:bse + 32, col:col + 128], in_=pp[0:32, 0:128]),
               reads=(bk,), writes=("dt_pk",), tag=f"B-ev{c}")
            if c == 1:
                emit_bigw()
            emitD(2)
        op("act", lambda e: e.activation(out=lndt_pk[:], in_=dt_pk[:], func=AF.Relu,
                                         bias=dtb_s[:, 0:1], scale=1.0),
           reads=("dt_pk", "dtb"), writes=("lndt_pk",), tag="sp-relu")
        op("act", lambda e: e.activation(out=ad_pk[:], in_=dt_pk[:], func=AF.Abs,
                                         bias=dtb_s[:, 0:1], scale=1.0),
           reads=("dt_pk", "dtb"), writes=("ad_pk",), tag="sp-abs")
        op("act", lambda e: e.activation(out=ad_pk[:], in_=ad_pk[:], func=AF.Exp,
                                         scale=-1.0),
           reads=("ad_pk",), writes=("ad_pk",), tag="sp-exp")
        op("act", lambda e: e.activation(out=ad_pk[:], in_=ad_pk[:], func=AF.Ln,
                                         bias=cbias[:, 1:2], scale=1.0),
           reads=("ad_pk", "cbias"), writes=("ad_pk",), tag="sp-ln")
        op("pool", lambda e: e.tensor_tensor(out=dt_pk[:], in0=lndt_pk[:], in1=ad_pk[:],
                                            op=OP.add),
           reads=("lndt_pk", "ad_pk"), writes=("dt_pk",), tag="sp-add")
        op("act", lambda e: e.activation(out=lndt_pk[:], in_=dt_pk[:], func=AF.Ln),
           reads=("dt_pk",), writes=("lndt_pk",), tag="lndt")
        op("pool", lambda e: e.tensor_tensor(out=ad_pk[:], in0=dt_pk[:],
                                            in1=nalog[:, 0:1].broadcast_to([128, 1024]),
                                            op=OP.mult),
           reads=("dt_pk", "nalog"), writes=("ad_pk",), tag="Ad")
        for j in range(NPAIR):
            col = 128 * j
            jp = j % 2
            ao = 128 * jp
            pp = pe_ if jp == 0 else pf
            bk = "pe" if jp == 0 else "pf"
            op("pe", lambda e, col=col, pp=pp: e.transpose(
                pp[:, 0:128], ad_pk[:, col:col + 128], identf_s[:]),
               reads=("ad_pk", "identf"), writes=(bk,), tag=f"B-tAd{j}")
            op("act", lambda e, ao=ao, pp=pp: e.copy(
                out=adlh[:, ao:ao + 128], in_=pp[:, 0:128]),
               reads=(bk,), writes=(f"adlh{jp}",), tag=f"B-evAd{j}")
            op("pe", lambda e, ao=ao, pp=pp: e.matmul(
                pp[:, 128:256], adlh[:, ao:ao + 128], triu_s[:],
                start=True, stop=True),
               reads=(f"adlh{jp}", "triu"), writes=(bk,), tag=f"B-mh{j}")
            op("act", lambda e, col=col, pp=pp: e.copy(
                out=acum_pk[:, col:col + 128], in_=pp[:, 128:256]),
               reads=(bk,), writes=("acum_pk",), tag=f"B-evh{j}")
            op("pe", lambda e, ao=ao, pp=pp: e.matmul(
                pp[:, 256:384], triu_s[:], adlh[:, ao:ao + 128],
                start=True, stop=True),
               reads=(f"adlh{jp}", "triu"), writes=(bk,), tag=f"B-ml{j}")
            for q2 in range(2):
                op("act", lambda e, j=j, q2=q2, pp=pp: e.activation(
                    out=sdol[:, (2 * j + q2) * 32:(2 * j + q2) * 32 + 32],
                    in_=pp[:, 256 + 64 * q2:256 + 64 * q2 + 32], func=AF.Exp),
                   reads=(bk,), writes=("sdol",), tag=f"B-sdo{j}.{q2}")
            emitD(1)
        op("pool", lambda e: e.tensor_tensor(out=subt_pk[:, 0:NPAIR * 128],
                                            in0=acum_pk[:, 0:NPAIR * 128],
                                            in1=lndt_pk[:, 0:NPAIR * 128],
                                            op=OP.subtract),
           reads=("acum_pk", "lndt_pk"), writes=("subt_pk",), tag="B-sub")
        for j in range(NPAIR):
            col = 128 * j
            jp = j % 2
            pp = pe_ if jp == 0 else pf
            bk = "pe" if jp == 0 else "pf"
            op("act", lambda e, col=col: e.activation(
                out=coeft_pk[:, col:col + 128], in_=subt_pk[:, col:col + 128],
                func=AF.Exp, scale=-1.0, bias=acum_pk[:, col + 127:col + 128]),
               reads=("subt_pk", "acum_pk"), writes=("coeft_pk",), tag=f"B-coef{j}")
            for q2 in range(2):
                c = 2 * j + q2
                op("act", lambda e, c=c, col=col, q2=q2: e.dma_start(
                    out=aflat_d[c:c + 1, :].rearrange("o (h l) -> (o h) l", h=32),
                    in_=acum_pk[64 * q2:64 * q2 + 32, col:col + 128]),
                   reads=("acum_pk",), writes=("aflat",), dma=f"aflat{c % 2}",
                   tag=f"B-fl{c}")
            op("pe", lambda e, col=col, pp=pp: e.transpose(
                pp[:, 0:128], subt_pk[:, col:col + 128], identf_s[:]),
               reads=("subt_pk", "identf"), writes=(bk,), tag=f"B-tsub{j}")
            for q2 in range(2):
                c = 2 * j + q2
                op("act", lambda e, c=c, q2=q2, pp=pp: e.copy(
                    out=subl[:, c * 32:c * 32 + 32],
                    in_=pp[:, 64 * q2:64 * q2 + 32]),
                   reads=(bk,), writes=("subl",), tag=f"B-evsub{c}")
            op("pe", lambda e, col=col, pp=pp: e.transpose(
                pp[:, 128:256], coeft_pk[:, col:col + 128], identf_s[:]),
               reads=("coeft_pk", "identf"), writes=(bk,), tag=f"B-tcoef{j}")
            for q2 in range(2):
                c = 2 * j + q2
                op("act", lambda e, c=c, q2=q2, pp=pp: e.copy(
                    out=coefl[:, c * 32:c * 32 + 32],
                    in_=pp[:, 128 + 64 * q2:128 + 64 * q2 + 32]),
                   reads=(bk,), writes=("coefl",), tag=f"B-evcoef{c}")
            emitD(1)
        op("act", lambda e: e.activation(
            out=coeft_pk[:, 0:NPAIR],
            in_=acum_pk[:, 0:NPAIR * 128].rearrange("p (j l) -> p j l", l=128)[:, :, 127:128],
            func=AF.Exp),
           reads=("acum_pk",), writes=("cdall",), tag="B-cdall")
        op("pe", lambda e: e.transpose(pe_[0:NPAIR, 0:128], coeft_pk[:, 0:NPAIR],
                                       identf_s[:]),
           reads=("cdall", "identf"), writes=("pe",), tag="B-tcd")
        op("act", lambda e: e.copy(out=adlh[0:NPAIR, 0:128], in_=pe_[0:NPAIR, 0:128]),
           reads=("pe",), writes=("cdsb",), tag="B-evcd")
        for j in range(NPAIR):
            jp = j % 2
            pp = pe_ if jp == 0 else pf
            bk = "pe" if jp == 0 else "pf"
            op("act", lambda e, j=j: e.dma_start(out=cdrow8[:],
                                                   in_=adlh[j:j + 1, 0:128]),
               reads=("cdsb",), writes=("cdrow8",), dma="cdf", tag=f"B-cdmv{j}")
            op("pe", lambda e, pp=pp: e.matmul(pp[:, 384:512], onesf_s[:], cdrow8[:],
                                               start=True, stop=True),
               reads=("cdrow8", "onesf"), writes=(bk,), tag=f"B-mcd{j}")
            for q2 in range(2):
                c = 2 * j + q2
                op("act", lambda e, c=c, q2=q2, pp=pp: e.copy(
                    out=crepl[:, c * 32:c * 32 + 32],
                    in_=pp[:, 384 + 64 * q2:384 + 64 * q2 + 32]),
                   reads=(bk,), writes=("crepl",), tag=f"B-evcrep{c}")
        emitD(len(d_items))
        op("dve", lambda e: e.tensor_tensor(
            out=woutT[:].rearrange("p (kc c) -> p kc c", c=1024),
            in0=woutT[:].rearrange("p (kc c) -> p kc c", c=1024),
            in1=nwpk_s[:, :, None].broadcast_to([128, 16, 1024]),
            op=OP.mult),
           reads=("woutT0", "woutT1", "woutT2", "woutT3", "nwpk"),
           writes=("woutT0", "woutT1", "woutT2", "woutT3"), tag="nwfold")

        B.barrier("pre-C")
        op("sp", lambda e: e.dma_start(out=ddb[:, 0:DI],
                                       in_=dflat_d.broadcast_to([128, DI])),
           writes=("ddb",), dma="ddb0", tag="dbcast")
        op("dve", lambda e: e.tensor_copy(out=drep[:], in_=ddb[:, 0:DI]),
           reads=("ddb",), writes=("drep",), tag="drep")
        op("dve", lambda e: e.memset(srun[:], 0.0), writes=("srun",), tag="z-srun")
        op("dve", lambda e: e.memset(
            xbcraw[:].rearrange("p (fc c) -> p fc c", c=260)[:, :, 0:4], 0.0),
           writes=('xbc0', 'xbc1', 'xbc2', 'xbc3', 'xbc4', 'xbc5', 'xbc6', 'xbc7', 'xbc8'), tag="z-halo")
        script_C(B, nc, d, locals(), NCH)
        B.barrier("C-E")
        script_E(B, nc, d, locals(), S, SG, NSG)
        B.barrier("end")

        esems = {e: es.enter_context(nc.semaphore(f"s_{e}")) for e in ENGS}
        dsems = {ch: es.enter_context(nc.semaphore(f"d_{ch}")) for ch in B.dma_n}
        block = es.enter_context(nc.Block())

        emit_stream = B.emit(nc, None, esems, dsems)

        @block.tensor
        def _(e):
            emit_stream("pe", e)

        @block.scalar
        def _(e):
            emit_stream("act", e)

        @block.vector
        def _(e):
            emit_stream("dve", e)

        @block.gpsimd
        def _(e):
            emit_stream("pool", e)

        @block.sync
        def _(e):
            emit_stream("sp", e)

    ar.free()
    return nc


def script_C(B, nc, d, V, NCH):
    op = B.op
    g = lambda n: V[n]
    (pa, pb, pt, pe_, pf) = (g("pa"), g("pb"), g("pt"), g("pe_"), g("pf"))
    winT, woutT, u_tm, u_fm = g("winT"), g("woutT"), g("u_tm"), g("u_fm")
    sz, xbcraw, convtmp = g("sz"), g("xbcraw"), g("convtmp")
    xact_pair, B_s2, C_s2, B_tm2 = g("xact_pair"), g("B_s2"), g("C_s2"), g("B_tm2")
    subl, sdol, coefl, crepl = g("subl"), g("sdol"), g("coefl"), g("crepl")
    ddb, lt, gtm, xt, xtc = g("ddb"), g("lt"), g("gtm"), g("xt"), g("xtc")
    srun_bf, srun, y1, ynorm, yt = g("srun_bf"), g("srun"), g("y1"), g("ynorm"), g("yt")
    drep, x1s, x1n = g("drep"), g("x1s"), g("x1n")
    identb_s, identf_s, maskT_s = g("identb_s"), g("identf_s"), g("maskT_s")
    convw_s, convb_s = g("convw_s"), g("convb_s")
    ssum, xsum, x2sum = g("ssum"), g("xsum"), g("x2sum")
    mstat, lnt, mm_t, vstat, rstd, b2 = (g("mstat"), g("lnt"), g("mm_t"),
                                          g("vstat"), g("rstd"), g("b2"))
    cbias = g("cbias")
    u_fmT_d, u_tm_d, aflat_d, x1_d = d["u_fmT"], d["u_tm"], d["acumflat"], d["x1d"]
    xr3 = xbcraw[:].rearrange("p (fc c) -> p fc c", c=260)
    PABANK = [("pa0", pa, 0), ("pa1", pa, 512), ("pb0", pb, 0), ("pb1", pb, 512)]
    WK = [f"winT{kc}" for kc in range(8)]
    WOK = ["woutT0", "woutT1", "woutT2", "woutT3"]

    def inproj_xbc_pair(c0):
        for w in range(2):
            for q in range(4):
                fc0 = w * 8 + q * 2
                bk, pp, co = PABANK[q]
                for fc in (fc0, fc0 + 1):
                    cc = co + (fc - fc0) * 256
                    for kc in range(8):
                        op("pe", lambda e, fc=fc, kc=kc, pp=pp, cc=cc: e.matmul(
                            pp[:, cc:cc + 256],
                            winT[:, kc * 4256 + 2048 + fc * 128:
                                 kc * 4256 + 2048 + (fc + 1) * 128],
                            u_fm[:, kc * 256:(kc + 1) * 256],
                            start=(kc == 0), stop=(kc == 7)),
                           reads=(WK[kc], f"ufm{kc // 4}"), writes=(bk,),
                           tag=f"C{c0}-x{fc}.{kc}")
                op("act", lambda e, fc0=fc0, pp=pp, co=co: e.copy(
                    out=xr3[:, fc0:fc0 + 2, 4:260], in_=pp[:, co:co + 512]),
                   reads=(bk,), writes=(f"xbc{fc0 // 2}",), tag=f"C{c0}-xev{fc0}")
        for kc in range(8):
            op("pe", lambda e, kc=kc: e.matmul(
                pe_[:, 0:256], winT[:, kc * 4256 + 2048 + 16 * 128:
                                    kc * 4256 + 2048 + 17 * 128],
                u_fm[:, kc * 256:(kc + 1) * 256],
                start=(kc == 0), stop=(kc == 7)),
               reads=(WK[kc], f"ufm{kc // 4}"), writes=("pe",), tag=f"C{c0}-x16.{kc}")
        op("act", lambda e: e.copy(out=xr3[:, 16:17, 4:260], in_=pe_[:, 0:256]),
           reads=("pe",), writes=("xbc8",), tag=f"C{c0}-xev16")

    def inproj_z(c):
        par = c % 2
        for gi, (bk, pp, co) in enumerate(PABANK):
            for kc in range(8):
                op("pe", lambda e, gi=gi, kc=kc, pp=pp, co=co, par=par: e.matmul(
                    pp[:, co:co + 512],
                    u_fm[:, kc * 256 + par * 128:kc * 256 + par * 128 + 128],
                    winT[:, kc * 4256 + gi * 512:kc * 4256 + (gi + 1) * 512],
                    start=(kc == 0), stop=(kc == 7)),
                   reads=(WK[kc], f"ufm{kc // 4}"), writes=(bk,),
                   tag=f"C{c}-z{gi}.{kc}")
            op("act", lambda e, gi=gi, pp=pp, co=co, par=par: e.activation(
                out=sz[par][:, gi * 512:(gi + 1) * 512], in_=pp[:, co:co + 512],
                func=AF.Silu),
               reads=(bk,), writes=(f"sz{par}",), tag=f"C{c}-szev{gi}")

    def conv_pair(c0):
        for gg in range(9):
            fcs = [2 * gg, 2 * gg + 1] if gg < 8 else [16]
            ck = f"convtmp{gg % 2}"
            for j in range(4):
                for fc in fcs:
                    q = (fc % 2) * 256
                    if j == 0:
                        op("act", lambda e, fc=fc, q=q, gg=gg: e.activation(
                            out=convtmp[gg % 2][:, q:q + 256], in_=xr3[:, fc, 1:257],
                            func=AF.Identity,
                            scale=convw_s[:, fc * 4:fc * 4 + 1],
                            bias=convb_s[:, fc:fc + 1]),
                           reads=(f"xbc{gg}", "convw", "convb"), writes=(ck,),
                           tag=f"C{c0}-cv{fc}.0")
                    else:
                        op("dve", lambda e, fc=fc, j=j, q=q, gg=gg: e.scalar_tensor_tensor(
                            out=convtmp[gg % 2][:, q:q + 256],
                            in0=xr3[:, fc, 1 + j:257 + j],
                            scalar=convw_s[:, fc * 4 + j:fc * 4 + j + 1],
                            in1=convtmp[gg % 2][:, q:q + 256], op0=OP.mult, op1=OP.add),
                           reads=(f"xbc{gg}", "convw", ck), writes=(ck,),
                           tag=f"C{c0}-cv{fc}.{j}")
            if gg < 8:
                op("act", lambda e, gg=gg: e.activation(
                    out=xact_pair[:, gg * 512:(gg + 1) * 512], in_=convtmp[gg % 2][:],
                    func=AF.Silu),
                   reads=(ck,), writes=("xact",), tag=f"C{c0}-cs{gg}")
            else:
                op("act", lambda e: e.activation(
                    out=B_s2[:], in_=convtmp[0][0:64, 0:256], func=AF.Silu),
                   reads=("convtmp0",), writes=("Bs",), tag=f"C{c0}-csB")
                op("act", lambda e: e.activation(
                    out=C_s2[:], in_=convtmp[0][64:128, 0:256], func=AF.Silu),
                   reads=("convtmp0",), writes=("Cs",), tag=f"C{c0}-csC")
        op("dve", lambda e: e.tensor_copy(out=xr3[:, :, 1:4], in_=xr3[:, :, 257:260]),
           reads=('xbc0', 'xbc1', 'xbc2', 'xbc3', 'xbc4', 'xbc5', 'xbc6', 'xbc7', 'xbc8'), writes=('xbc0', 'xbc1', 'xbc2', 'xbc3', 'xbc4', 'xbc5', 'xbc6', 'xbc7', 'xbc8'), tag=f"C{c0}-halo")

    def ufm_load(c):
        for hf in range(2):
            op("pool", lambda e, c=c, hf=hf: e.dma_start(
                out=u_fm[:, hf * 1024:(hf + 1) * 1024]
                    .rearrange("p (kc t) -> p kc t", t=256),
                in_=u_fmT_d[hf * 512:(hf + 1) * 512, c * CH:(c + 2) * CH]
                    .rearrange("(kc p) t -> p kc t", p=128)),
               writes=(f"ufm{hf}",), dma=f"ufm{hf}", tag=f"C{c}-ufm{hf}")

    def chunk_stage(c):
        par = c % 2
        op("sp", lambda e, c=c: e.dma_start(
            out=ddb[:], in_=aflat_d[c:c + 1, 0:2048].broadcast_to([128, 2048])),
           reads=("aflat",), writes=("ddb",), dma="ddb0", tag=f"C{c}-db0")
        op("sp", lambda e, c=c: e.dma_start(
            out=u_tm[:], in_=u_tm_d[c * CH:(c + 1) * CH, :]),
           writes=("u_tm",), dma="utm", tag=f"C{c}-utm")

        for hh in range(2):
            op("dve", lambda e, c=c, hh=hh: e.tensor_tensor(
                out=ddb[:].rearrange("p (h l) -> p h l", l=CH),
                in0=ddb[:].rearrange("p (h l) -> p h l", l=CH),
                in1=subl[:, c * 32 + hh * 16:c * 32 + (hh + 1) * 16][:, :, None]
                    .broadcast_to([128, 16, CH]),
                op=OP.subtract),
               reads=("ddb", "subl"), writes=("ddb",), tag=f"C{c}-sub{hh}")
            if hh == 0:
                op("act", lambda e: e.activation(out=lt[:], in_=ddb[:],
                                                 func=AF.Exp),
                   reads=("ddb",), writes=("lt",), tag=f"C{c}-exp0")
                op("sp", lambda e, c=c: e.dma_start(
                    out=ddb[:],
                    in_=aflat_d[c:c + 1, 2048:4096].broadcast_to([128, 2048])),
                   reads=("aflat",), writes=("ddb",), dma="ddb1", tag=f"C{c}-db1")
            else:
                op("act", lambda e: e.activation(out=ddb[:], in_=ddb[:],
                                                 func=AF.Exp),
                   reads=("ddb",), writes=("ddb",), tag=f"C{c}-exp1")

        op("pe", lambda e, par=par: e.transpose(
            pt[:, 0:64], B_s2[:, par * 128:par * 128 + 128][0:64, :]
            if False else B_s2[0:64, par * 128:par * 128 + 128],
            identb_s[0:64, 0:64]),
           reads=("Bs", "identb"), writes=("pt0",), tag=f"C{c}-tB")
        op("dve", lambda e, par=par: e.tensor_copy(
            out=B_tm2[:, par * 64:par * 64 + 64], in_=pt[:, 0:64]),
           reads=("pt0",), writes=("Btm",), tag=f"C{c}-evB")
        op("pe", lambda e, par=par: e.matmul(
            pe_[:, 128:256], B_s2[0:64, par * 128:par * 128 + 128],
            C_s2[0:64, par * 128:par * 128 + 128], start=True, stop=True),
           reads=("Bs", "Cs"), writes=("pe",), tag=f"C{c}-GT")
        op("dve", lambda e: e.tensor_tensor(out=gtm[:], in0=pe_[:, 128:256],
                                            in1=maskT_s[:], op=OP.mult),
           reads=("pe", "maskT"), writes=("gtm",), tag=f"C{c}-GTm")
        op("act", lambda e: e.copy(out=srun_bf[:, 0:1024], in_=srun[0:64, :]),
           reads=("srun",), writes=("srunbf",), tag=f"C{c}-stc0")
        op("act", lambda e: e.copy(out=srun_bf[:, 1024:2048], in_=srun[64:128, :]),
           reads=("srun",), writes=("srunbf",), tag=f"C{c}-stc1")

        for hh in range(2):
            ptk = f"pt{hh}"
            ptco = hh * 1024
            for i in range(8):
                op("pe", lambda e, i=i, hh=hh, par=par, ptco=ptco: e.transpose(
                    pt[:, ptco + i * 128:ptco + (i + 1) * 128],
                    xact_pair[:, (hh * 8 + i) * 256 + par * 128:
                              (hh * 8 + i) * 256 + par * 128 + 128],
                    identb_s[:]),
                   reads=("xact", "identb"), writes=(ptk,), tag=f"C{c}-tx{hh}.{i}")
            op("act", lambda e, ptco=ptco: e.copy(
                out=xt[:], in_=pt[:, ptco:ptco + 1024]),
               reads=(ptk,), writes=("xt",), tag=f"C{c}-evx{hh}")
            op("dve", lambda e, hh=hh: e.tensor_tensor(
                out=xtc[:], in0=xt[:], in1=drep[:, hh * 1024:(hh + 1) * 1024],
                op=OP.mult),
               reads=("xt", "drep"), writes=("xtc",), tag=f"C{c}-xtd{hh}")
            for j in range(2):
                op("pe", lambda e, j=j, hh=hh, par=par: e.matmul(
                    pa[:, j * 512:(j + 1) * 512],
                    C_s2[0:64, par * 128:par * 128 + 128],
                    srun_bf[:, hh * 1024 + j * 512:hh * 1024 + (j + 1) * 512],
                    start=True, stop=True),
                   reads=("Cs", "srunbf"), writes=(f"pa{j}",), tag=f"C{c}-yo{hh}.{j}")
            if hh == 0:
                op("dve", lambda e: e.scalar_tensor_tensor(
                    out=lt[:].rearrange("p (h l) -> p h l", l=CH),
                    in0=lt[:].rearrange("p (h l) -> p h l", l=CH), scalar=1e30,
                    in1=gtm[:, None, :].broadcast_to([128, 16, CH]),
                    op0=OP.min, op1=OP.mult),
                   reads=("lt", "gtm"), writes=("lt",), tag=f"C{c}-stt{hh}")
            else:
                op("dve", lambda e: e.scalar_tensor_tensor(
                    out=lt[:].rearrange("p (h l) -> p h l", l=CH),
                    in0=ddb[:].rearrange("p (h l) -> p h l", l=CH), scalar=1e30,
                    in1=gtm[:, None, :].broadcast_to([128, 16, CH]),
                    op0=OP.min, op1=OP.mult),
                   reads=("ddb", "gtm"), writes=("lt",), tag=f"C{c}-stt{hh}")
            for h in range(16):
                bk = "pb0" if h < 8 else "pb1"
                co = (h % 8) * 64 + (h // 8) * 512
                op("pe", lambda e, h=h, co=co: e.matmul(
                    pb[:, co:co + 64], identb_s[:],
                    xtc[:, h * 64:(h + 1) * 64], start=True, stop=False),
                   reads=("identb", "xtc"), writes=(bk,), tag=f"C{c}-ydx{hh}.{h}")
                op("pe", lambda e, h=h, co=co: e.matmul(
                    pb[:, co:co + 64], lt[:, h * 128:(h + 1) * 128],
                    xt[:, h * 64:(h + 1) * 64], start=False, stop=True),
                   reads=("lt", "xt"), writes=(bk,), tag=f"C{c}-ydg{hh}.{h}")
            op("dve", lambda e, c=c, hh=hh: e.tensor_tensor(
                out=xtc[:].rearrange("p (h q) -> p h q", q=64),
                in0=xt[:].rearrange("p (h q) -> p h q", q=64),
                in1=coefl[:, c * 32 + hh * 16:c * 32 + (hh + 1) * 16][:, :, None]
                    .broadcast_to([128, 16, 64]),
                op=OP.mult),
               reads=("xt", "coefl"), writes=("xtc",), tag=f"C{c}-xtc{hh}")
            for j, (bk2, pp2) in enumerate([("pe", pe_), ("pf", pf)]):
                op("pe", lambda e, j=j, par=par, pp2=pp2: e.matmul(
                    pp2[0:64, 0:512], B_tm2[:, par * 64:par * 64 + 64],
                    xtc[:, j * 512:(j + 1) * 512], start=True, stop=True),
                   reads=("Btm", "xtc"), writes=(bk2,), tag=f"C{c}-sc{hh}.{j}")
            op("dve", lambda e, c=c, hh=hh: e.tensor_tensor(
                out=y1[:, hh * 1024:(hh + 1) * 1024].rearrange("p (h q) -> p h q", q=64),
                in0=pa[:].rearrange("p (h q) -> p h q", q=64),
                in1=sdol[:, c * 32 + hh * 16:c * 32 + (hh + 1) * 16][:, :, None]
                    .broadcast_to([128, 16, 64]),
                op=OP.mult),
               reads=("pa0", "pa1", "sdol"), writes=("y1",), tag=f"C{c}-y1{hh}")
            op("dve", lambda e, hh=hh: e.tensor_tensor(
                out=y1[:, hh * 1024:(hh + 1) * 1024],
                in0=y1[:, hh * 1024:(hh + 1) * 1024], in1=pb[:], op=OP.add),
               reads=("pb0", "pb1", "y1"), writes=("y1",), tag=f"C{c}-y2{hh}")
            op("dve", lambda e, c=c, hh=hh: e.tensor_tensor(
                out=srun[hh * 64:(hh + 1) * 64, :].rearrange("p (h q) -> p h q", q=64),
                in0=srun[hh * 64:(hh + 1) * 64, :].rearrange("p (h q) -> p h q", q=64),
                in1=crepl[hh * 64:(hh + 1) * 64,
                          c * 32 + hh * 16:c * 32 + (hh + 1) * 16][:, :, None]
                    .broadcast_to([64, 16, 64]),
                op=OP.mult),
               reads=("srun", "crepl", "srunbf"), writes=("srun",), tag=f"C{c}-sd{hh}")
            op("dve", lambda e, hh=hh: e.tensor_tensor(
                out=srun[hh * 64:(hh + 1) * 64, 0:512],
                in0=srun[hh * 64:(hh + 1) * 64, 0:512], in1=pe_[0:64, 0:512],
                op=OP.add),
               reads=("srun", "pe"), writes=("srun",), tag=f"C{c}-sa{hh}.0")
            op("dve", lambda e, hh=hh: e.tensor_tensor(
                out=srun[hh * 64:(hh + 1) * 64, 512:1024],
                in0=srun[hh * 64:(hh + 1) * 64, 512:1024], in1=pf[0:64, 0:512],
                op=OP.add),
               reads=("srun", "pf"), writes=("srun",), tag=f"C{c}-sa{hh}.1")

        op("dve", lambda e, par=par: e.tensor_tensor(out=y1[:], in0=y1[:],
                                                     in1=sz[par][:], op=OP.mult),
           reads=("y1", f"sz{par}"), writes=("y1",), tag=f"C{c}-gate")
        op("act", lambda e, par=par: e.activation(out=ynorm[:], in_=y1[:], func=AF.Square,
                                         accum_out=ssum[:, par:par + 1]),
           reads=("y1",), writes=("ynorm", f"ssum{par}"), tag=f"C{c}-sq")
        op("act", lambda e, par=par: e.activation(out=lnt[:, par:par + 1],
                                         in_=ssum[:, par:par + 1], func=AF.Ln,
                                         scale=1.0 / DI, bias=cbias[:, 0:1]),
           reads=(f"ssum{par}", "cbias"), writes=(f"lnt{par}",), tag=f"C{c}-ln")
        op("act", lambda e, par=par: e.activation(out=rstd[:, par:par + 1],
                                         in_=lnt[:, par:par + 1], func=AF.Exp,
                                         scale=-0.5),
           reads=(f"lnt{par}",), writes=(f"rstd{par}",), tag=f"C{c}-rs")
        op("dve", lambda e, par=par: e.tensor_scalar(out=ynorm[:], in0=y1[:],
                                            scalar1=rstd[:, par:par + 1], scalar2=None,
                                            op0=OP.mult),
           reads=("y1", f"rstd{par}"), writes=("ynorm",), tag=f"C{c}-yn")
        for i in range(16):
            ptk = "pt0" if i < 8 else "pt1"
            op("pe", lambda e, i=i: e.transpose(
                pt[:, i * 128:(i + 1) * 128], ynorm[:, i * 128:(i + 1) * 128],
                identb_s[:]),
               reads=("ynorm", "identb"), writes=(ptk,), tag=f"C{c}-ty{i}")
        op("act", lambda e: e.copy(out=yt[:, 0:1024], in_=pt[:, 0:1024]),
           reads=("pt0",), writes=("yt",), tag=f"C{c}-evy0")
        op("act", lambda e: e.copy(out=yt[:, 1024:2048], in_=pt[:, 1024:2048]),
           reads=("pt1",), writes=("yt",), tag=f"C{c}-evy1")
        for nh in range(2):
            for kc in range(16):
                op("pe", lambda e, nh=nh, kc=kc: e.matmul(
                    pa[:, nh * 512:(nh + 1) * 512], yt[:, kc * 128:(kc + 1) * 128],
                    woutT[:, kc * 1024 + nh * 512:kc * 1024 + (nh + 1) * 512],
                    start=(kc == 0), stop=(kc == 15)),
                   reads=("yt", WOK[kc // 4]), writes=(f"pa{nh}",), tag=f"C{c}-op{nh}.{kc}")
        op("dve", lambda e, par=par: e.scalar_tensor_tensor(
            out=x1s[:], in0=pa[:], scalar=1.0, in1=u_tm[:], op0=OP.mult, op1=OP.add,
            accum_out=xsum[:, par:par + 1]),
           reads=("pa0", "pa1", "u_tm"), writes=("x1s", f"xsum{par}"), tag=f"C{c}-res")
        op("act", lambda e, par=par: e.activation(out=ynorm[:, 0:1024], in_=x1s[:],
                                         func=AF.Square, accum_out=x2sum[:, par:par + 1]),
           reads=("x1s",), writes=("ynorm", f"x2sum{par}"), tag=f"C{c}-sq1")
        op("dve", lambda e, par=par: e.tensor_scalar(out=mstat[:, par:par + 1],
                                            in0=xsum[:, par:par + 1],
                                            scalar1=1.0 / DM, scalar2=None, op0=OP.mult),
           reads=(f"xsum{par}",), writes=(f"mstat{par}",), tag=f"C{c}-m")
        op("dve", lambda e, par=par: e.tensor_tensor(out=mm_t[:, par:par + 1],
                                            in0=mstat[:, par:par + 1],
                                            in1=mstat[:, par:par + 1], op=OP.mult),
           reads=(f"mstat{par}",), writes=(f"mm_t{par}",), tag=f"C{c}-mm")
        op("dve", lambda e, par=par: e.scalar_tensor_tensor(
            out=vstat[:, par:par + 1], in0=x2sum[:, par:par + 1], scalar=1.0 / DM,
            in1=mm_t[:, par:par + 1], op0=OP.mult, op1=OP.subtract),
           reads=(f"x2sum{par}", f"mm_t{par}"), writes=(f"vstat{par}",), tag=f"C{c}-v")
        op("act", lambda e, par=par: e.activation(out=lnt[:, par:par + 1],
                                         in_=vstat[:, par:par + 1], func=AF.Ln,
                                         bias=cbias[:, 0:1]),
           reads=(f"vstat{par}", "cbias"), writes=(f"lnt{par}",), tag=f"C{c}-lnv")
        op("act", lambda e, par=par: e.activation(out=rstd[:, par:par + 1],
                                         in_=lnt[:, par:par + 1], func=AF.Exp,
                                         scale=-0.5),
           reads=(f"lnt{par}",), writes=(f"rstd{par}",), tag=f"C{c}-rs1")
        op("dve", lambda e, par=par: e.scalar_tensor_tensor(
            out=b2[:, par:par + 1], in0=mstat[:, par:par + 1], scalar=-1.0,
            in1=rstd[:, par:par + 1], op0=OP.mult, op1=OP.mult),
           reads=(f"mstat{par}", f"rstd{par}"), writes=(f"b2{par}",), tag=f"C{c}-b2")
        op("act", lambda e, par=par: e.activation(out=x1n[:], in_=x1s[:], func=AF.Identity,
                                         scale=rstd[:, par:par + 1], bias=b2[:, par:par + 1]),
           reads=("x1s", f"rstd{par}", f"b2{par}"), writes=("x1n",), tag=f"C{c}-x1n")
        op("act", lambda e, c=c: e.dma_start(out=x1_d[c * CH:(c + 1) * CH, :],
                                              in_=x1n[:]),
           reads=("x1n",), writes=("x1d",), dma="x1st", tag=f"C{c}-x1st")

    ufm_load(0)
    inproj_xbc_pair(0)
    inproj_z(0)
    inproj_z(1)
    for jj in range(NCH // 2):
        c0 = 2 * jj
        conv_pair(c0)
        if c0 + 2 < NCH:
            ufm_load(c0 + 2)
            inproj_xbc_pair(c0 + 2)
        chunk_stage(c0)
        if c0 + 2 < NCH:
            inproj_z(c0 + 2)
        chunk_stage(c0 + 1)
        if c0 + 2 < NCH:
            inproj_z(c0 + 3)


def script_D(B, nc, d, V):
    op = B.op
    g = lambda n: V[n]
    wst, trow, trow2 = g("wst"), g("trow"), g("trow2")
    part1, part2 = g("part1"), g("part2")
    scs, chs, cls, ncls = g("scs"), g("chs"), g("cls"), g("ncls")
    wfc_d, wproj_d = d["wfc"], d["wproj"]
    ternfc_d, ternproj_d = d["ternfc"], d["ternproj"]
    items = []
    kctr = [0]
    tctr = [0]

    def emit_load(src_ap, width, tag):
        b = kctr[0] % 4
        kctr[0] += 1
        op("sp", lambda e: e.dma_start(out=wst[b][:, 0:width], in_=src_ap),
           writes=(f"wst{b}",), dma=f"wst{b}", tag=tag)
        return b

    def emit_r1(b, width, p1sl, p1k, tag, eng="dve"):
        op(eng, lambda e: e.tensor_reduce(
            out=part1[:, p1sl],
            in_=wst[b][:, 0:width].rearrange("p (a b) -> p a b", b=32),
            axis=AXX, op=OP.add, apply_absolute_value=True),
           reads=(f"wst{b}",), writes=(p1k,), tag=tag)

    def emit_scale(par, width, tagp):
        sk = f"scs{par}"
        sl_ = slice(par, par + 1)
        op("dve", lambda e: e.tensor_scalar(out=scs[:, sl_], in0=scs[:, sl_],
                                            scalar1=1.0 / width, scalar2=1e-5,
                                            op0=OP.mult, op1=OP.max),
           reads=(sk,), writes=(sk,), tag=f"{tagp}-mc")
        op("dve", lambda e: e.tensor_scalar(out=chs[:, sl_], in0=scs[:, sl_],
                                            scalar1=0.5, scalar2=None, op0=OP.mult),
           reads=(sk,), writes=(f"chs{par}",), tag=f"{tagp}-ch")
        op("dve", lambda e: e.tensor_scalar(out=cls[:, sl_], in0=scs[:, sl_],
                                            scalar1=2.0 ** -25, scalar2=None,
                                            op0=OP.mult),
           reads=(sk,), writes=(f"cls{par}",), tag=f"{tagp}-cl")
        op("dve", lambda e: e.tensor_scalar(out=ncls[:, sl_], in0=cls[:, sl_],
                                            scalar1=-1.0, scalar2=None, op0=OP.mult),
           reads=(f"cls{par}",), writes=(f"ncls{par}",), tag=f"{tagp}-ncl")

    def emit_thresh(bref, par, width, dst_ap, wkey, tagp):
        t = tctr[0] % 2
        tctr[0] += 1
        b = bref[0]
        sk = f"scs{par}"
        sl_ = slice(par, par + 1)
        wk = f"wst{b}"
        tk = f"trow{t}"
        op("dve", lambda e: e.tensor_scalar(out=trow[t][:, 0:width],
                                            in0=wst[b][:, 0:width],
                                            scalar1=chs[:, sl_], scalar2=cls[:, sl_],
                                            op0=OP.subtract, op1=OP.is_gt),
           reads=(wk, f"chs{par}", f"cls{par}"), writes=(tk,), tag=f"{tagp}-tp")
        op("dve", lambda e: e.tensor_scalar(out=trow2[:, 0:width],
                                            in0=wst[b][:, 0:width],
                                            scalar1=chs[:, sl_], scalar2=ncls[:, sl_],
                                            op0=OP.add, op1=OP.is_lt),
           reads=(wk, f"chs{par}", f"ncls{par}"), writes=("trow2",),
           tag=f"{tagp}-tn")
        op("dve", lambda e: e.tensor_tensor(out=trow[t][:, 0:width],
                                            in0=trow[t][:, 0:width],
                                            in1=trow2[:, 0:width],
                                            op=OP.subtract),
           reads=(tk, "trow2"), writes=(tk,), tag=f"{tagp}-sub")
        op("act", lambda e: e.activation(out=trow[t][:, 0:width],
                                         in_=trow[t][:, 0:width],
                                         func=AF.Identity, scale=scs[:, sl_]),
           reads=(tk, sk), writes=(tk,), tag=f"{tagp}-sc")
        op("act", lambda e: e.dma_start(out=dst_ap, in_=trow[t][:, 0:width]),
           reads=(tk,), writes=(wkey,), dma=f"tern{t}", tag=f"{tagp}-st")

    def fc_item(i):
        par = i % 2
        po = par * 128
        p1k = f"part1{par}"
        sl_ = slice(par, par + 1)

        def run():
            b = emit_load(wfc_d[i * 128:(i + 1) * 128, :], 1024, f"ternfc{i}-ld")
            emit_r1(b, 1024, slice(po, po + 32), p1k, f"ternfc{i}-r1")
            op("dve", lambda e: e.tensor_reduce(
                out=scs[:, sl_], in_=part1[:, po:po + 32], axis=AXX, op=OP.add),
               reads=(p1k,), writes=(f"scs{par}",), tag=f"ternfc{i}-r3")
            emit_scale(par, 1024, f"ternfc{i}")
            emit_thresh([b], par, 1024, ternfc_d[i * 128:(i + 1) * 128, :],
                        "ternfc", f"ternfc{i}")
        return run

    def proj_items(i):
        par = i % 2
        po = par * 128
        p1k = f"part1{par}"
        sl_ = slice(par, par + 1)
        bufA = [None]
        bufB = [None]

        def runA():
            bufA[0] = emit_load(wproj_d[i * 128:(i + 1) * 128, 0:2048], 2048,
                                f"ternproj{i}a-ld")
            emit_r1(bufA[0], 2048, slice(po, po + 64), p1k, f"ternproj{i}a-r1")

        def runB():
            bufB[0] = emit_load(wproj_d[i * 128:(i + 1) * 128, 2048:4096], 2048,
                                f"ternproj{i}b-ld")
            emit_r1(bufB[0], 2048, slice(po + 64, po + 128), p1k,
                    f"ternproj{i}b-r1")
            op("dve", lambda e: e.tensor_reduce(
                out=part2[:, par * 4:par * 4 + 4],
                in_=part1[:, po:po + 128].rearrange("p (a b) -> p a b", b=32),
                axis=AXX, op=OP.add),
               reads=(p1k,), writes=(f"part2{par}",), tag=f"ternproj{i}-r2")
            op("dve", lambda e: e.tensor_reduce(
                out=scs[:, sl_], in_=part2[:, par * 4:par * 4 + 4],
                axis=AXX, op=OP.add),
               reads=(f"part2{par}",), writes=(f"scs{par}",), tag=f"ternproj{i}-r3")
            emit_scale(par, 4096, f"ternproj{i}")
            emit_thresh(bufA, par, 2048,
                        ternproj_d[i * 128:(i + 1) * 128, 0:2048],
                        "ternproj", f"ternproj{i}a")

        def runC():
            emit_thresh(bufB, par, 2048,
                        ternproj_d[i * 128:(i + 1) * 128, 2048:4096],
                        "ternproj", f"ternproj{i}b")
        return [runA, runB, runC]

    for i in range(32):
        items.append(fc_item(i))
    for i in range(8):
        items.extend(proj_items(i))
    return items


def script_E(B, nc, d, V, S, SG, NSG):
    op = B.op
    g = lambda n: V[n]
    (pa, pb, pt, pe_, pf) = (g("pa"), g("pb"), g("pt"), g("pe_"), g("pf"))
    wfcT, wprT = g("wfcT"), g("wprT")
    x1tm, x1bf, x1fm, r2, hr = g("x1tm"), g("x1bf"), g("x1fm"), g("r2"), g("hr")
    t1b, t2b, vtb, fbtb, uTg = g("t1b"), g("t2b"), g("vtb"), g("fbtb"), g("uTg")
    outst = g("outst")
    identb_s, gate_s = g("identb_s"), g("gate_s")
    xsum, x2sum = g("xsum"), g("x2sum")
    mstat, lnt, mm_t, vstat, rstd, b2 = (g("mstat"), g("lnt"), g("mm_t"),
                                          g("vstat"), g("rstd"), g("b2"))
    cbias = g("cbias")
    x1_d, out_d = d["x1d"], d["out"]
    ternfc_d, ternproj_d = d["ternfc"], d["ternproj"]
    vT_d, fbT_d, uTlr_d, gater_d = d["vT"], d["fbT"], d["uTlr"], d["gater"]
    NT = SG // 128
    PABANK = [("pa0", pa, 0, 0), ("pa1", pa, 512, 0), ("pe", pe_, 0, 1), ("pf", pf, 0, 1)]

    def x1block(sg):
        for tg in range(NT):
            op("sp", lambda e, sg=sg, tg=tg: e.dma_start(
                out=x1tm[:, tg * 1024:(tg + 1) * 1024],
                in_=x1_d[sg * SG + tg * 128:sg * SG + (tg + 1) * 128, :]),
               writes=(f"x1tm{tg}",), dma=f"x1ld{tg % 2}", tag=f"E{sg}-x1ld{tg}")
            op("dve", lambda e, tg=tg: e.tensor_copy(
                out=x1bf[:], in_=x1tm[:, tg * 1024:(tg + 1) * 1024]),
               reads=(f"x1tm{tg}",), writes=("x1bf",), tag=f"E{sg}-cast{tg}")
            for i in range(8):
                op("pe", lambda e, i=i: e.transpose(
                    pt[:, i * 128:(i + 1) * 128], x1bf[:, i * 128:(i + 1) * 128],
                    identb_s[:]),
                   reads=("x1bf", "identb"), writes=("pt0",), tag=f"E{sg}-t{tg}.{i}")
            op("dve", lambda e, tg=tg: e.tensor_copy(
                out=x1fm[:].rearrange("p (kc t) -> p kc t", t=SG)
                    [:, :, tg * 128:(tg + 1) * 128],
                in_=pt[:, 0:1024].rearrange("p (kc t) -> p kc t", t=128)),
               reads=("pt0",), writes=("x1fm",), tag=f"E{sg}-evt{tg}")

    x1block(0)
    for k in range(8):
        op("sp", lambda e, k=k: e.dma_start(
            out=wfcT[:, k * 4096:(k + 1) * 4096],
            in_=ternfc_d[0:4096, k * 128:(k + 1) * 128], transpose=True),
           reads=("ternfc",), writes=(f"wfcT{k}",), dma=f"wef{k%8}", tag=f"E-wfc{k}")
    for k in range(32):
        op("sp", lambda e, k=k: e.dma_start(
            out=wprT[:, k * 1024:(k + 1) * 1024],
            in_=ternproj_d[0:1024, k * 128:(k + 1) * 128], transpose=True),
           reads=("ternproj",), writes=(f"wprT{k//8}",), dma=f"wef{k%8}", tag=f"E-wpr{k}")
    op("pool", lambda e: e.dma_start(
        out=vtb[:].rearrange("p (kc j) -> p kc j", j=16),
        in_=vT_d.rearrange("(kc p) j -> p kc j", p=128)),
       writes=("vtb",), dma="lr", tag="E-vtb")
    op("pool", lambda e: e.dma_start(out=fbtb[:], in_=fbT_d),
       writes=("fbtb",), dma="lr", tag="E-fbtb")
    op("pool", lambda e: e.dma_start(out=uTg[:], in_=uTlr_d),
       writes=("uTg",), dma="lr", tag="E-uTg")
    op("sp", lambda e: e.dma_start(out=gate_s[:], in_=gater_d),
       writes=("gate",), dma="cst", tag="E-gate")
    op("dve", lambda e: e.tensor_scalar(out=uTg[:], in0=uTg[:],
                                        scalar1=gate_s[:, 0:1], scalar2=None,
                                        op0=OP.mult),
       reads=("uTg", "gate"), writes=("uTg",), tag="E-uTgs")

    for sg in range(NSG):
        if sg > 0:
            x1block(sg)
        for kc in range(8):
            op("pe", lambda e, kc=kc: e.matmul(
                pa[0:16, 0:SG], vtb[:, kc * 16:(kc + 1) * 16],
                x1fm[:, kc * SG:(kc + 1) * SG], start=(kc == 0), stop=(kc == 7)),
               reads=("vtb", "x1fm"), writes=("pa0",), tag=f"E{sg}-t1.{kc}")
        op("act", lambda e: e.copy(out=t1b[:], in_=pa[0:16, 0:SG]),
           reads=("pa0",), writes=("t1b",), tag=f"E{sg}-evt1")
        op("pe", lambda e: e.matmul(pa[0:16, 0:SG], fbtb[:], t1b[:],
                                    start=True, stop=True),
           reads=("fbtb", "t1b"), writes=("pa0",), tag=f"E{sg}-t2")
        op("act", lambda e: e.copy(out=t2b[:], in_=pa[0:16, 0:SG]),
           reads=("pa0",), writes=("t2b",), tag=f"E{sg}-evt2")
        for fg in range(32):
            bk, pp, co, _ = PABANK[fg % 4]
            for kc in range(8):
                op("pe", lambda e, fg=fg, kc=kc, pp=pp, co=co: e.matmul(
                    pp[:, co:co + SG] if SG == 512 else pp[:, co:co + SG],
                    wfcT[:, kc * 4096 + fg * 128:kc * 4096 + (fg + 1) * 128],
                    x1fm[:, kc * SG:(kc + 1) * SG],
                    start=(kc == 0), stop=False),
                   reads=(f"wfcT{kc}", "x1fm"), writes=(bk,), tag=f"E{sg}-fc{fg}.{kc}")
            op("pe", lambda e, fg=fg, pp=pp, co=co: e.matmul(
                pp[:, co:co + SG], uTg[:, fg * 128:(fg + 1) * 128], t2b[:],
                start=False, stop=True),
               reads=("uTg", "t2b"), writes=(bk,), tag=f"E{sg}-lr{fg}")
            op("act", lambda e, fg=fg, pp=pp, co=co: e.activation(
                out=hr[fg % 2][:, 0:SG], in_=pp[:, co:co + SG], func=AF.Relu),
               reads=(bk,), writes=(f"hr{fg%2}",), tag=f"E{sg}-rl{fg}")
            op("dve", lambda e, fg=fg: e.tensor_tensor(
                out=r2[:, fg * SG:(fg + 1) * SG], in0=hr[fg % 2][:, 0:SG],
                in1=hr[fg % 2][:, 0:SG], op=OP.mult),
               reads=(f"hr{fg%2}",), writes=("r2",), tag=f"E{sg}-r2{fg}")
        for tg in range(NT):
            for nh in range(2):
                for kc in range(32):
                    op("pe", lambda e, tg=tg, nh=nh, kc=kc: e.matmul(
                        pb[:, nh * 512:(nh + 1) * 512],
                        r2[:, kc * SG + tg * 128:kc * SG + (tg + 1) * 128],
                        wprT[:, kc * 1024 + nh * 512:kc * 1024 + (nh + 1) * 512],
                        start=(kc == 0), stop=(kc == 31)),
                       reads=("r2", f"wprT{kc//8}"), writes=(f"pb{nh}",),
                       tag=f"E{sg}-pj{tg}.{nh}.{kc}")
            pq = tg % 2
            op("dve", lambda e, tg=tg, pq=pq: e.scalar_tensor_tensor(
                out=outst[:], in0=pb[:], scalar=1.0,
                in1=x1tm[:, tg * 1024:(tg + 1) * 1024], op0=OP.mult, op1=OP.add,
                accum_out=xsum[:, pq:pq + 1]),
               reads=("pb0", "pb1", f"x1tm{tg}"), writes=("outst", f"xsum{pq}"),
               tag=f"E{sg}-res{tg}")
            op("act", lambda e, pq=pq: e.activation(out=x1bf[:], in_=outst[:],
                                             func=AF.Square, accum_out=x2sum[:, pq:pq + 1]),
               reads=("outst",), writes=("x1bf", f"x2sum{pq}"), tag=f"E{sg}-sq{tg}")
            op("dve", lambda e, pq=pq: e.tensor_scalar(out=mstat[:, pq:pq + 1],
                                                in0=xsum[:, pq:pq + 1],
                                                scalar1=1.0 / DM, scalar2=None,
                                                op0=OP.mult),
               reads=(f"xsum{pq}",), writes=(f"mstat{pq}",), tag=f"E{sg}-m{tg}")
            op("dve", lambda e, pq=pq: e.tensor_tensor(out=mm_t[:, pq:pq + 1],
                                                in0=mstat[:, pq:pq + 1],
                                                in1=mstat[:, pq:pq + 1], op=OP.mult),
               reads=(f"mstat{pq}",), writes=(f"mm_t{pq}",), tag=f"E{sg}-mm{tg}")
            op("dve", lambda e, pq=pq: e.scalar_tensor_tensor(
                out=vstat[:, pq:pq + 1], in0=x2sum[:, pq:pq + 1], scalar=1.0 / DM,
                in1=mm_t[:, pq:pq + 1], op0=OP.mult, op1=OP.subtract),
               reads=(f"x2sum{pq}", f"mm_t{pq}"), writes=(f"vstat{pq}",), tag=f"E{sg}-v{tg}")
            op("act", lambda e, pq=pq: e.activation(out=lnt[:, pq:pq + 1],
                                             in_=vstat[:, pq:pq + 1], func=AF.Ln,
                                             bias=cbias[:, 0:1]),
               reads=(f"vstat{pq}", "cbias"), writes=(f"lnt{pq}",), tag=f"E{sg}-lnv{tg}")
            op("act", lambda e, pq=pq: e.activation(out=rstd[:, pq:pq + 1],
                                             in_=lnt[:, pq:pq + 1], func=AF.Exp,
                                             scale=-0.5),
               reads=(f"lnt{pq}",), writes=(f"rstd{pq}",), tag=f"E{sg}-rs{tg}")
            op("dve", lambda e, pq=pq: e.scalar_tensor_tensor(
                out=b2[:, pq:pq + 1], in0=mstat[:, pq:pq + 1], scalar=-1.0,
                in1=rstd[:, pq:pq + 1], op0=OP.mult, op1=OP.mult),
               reads=(f"mstat{pq}", f"rstd{pq}"), writes=(f"b2{pq}",), tag=f"E{sg}-b2{tg}")
            op("act", lambda e, tg=tg, pq=pq: e.activation(
                out=outst[:], in_=outst[:], func=AF.Identity,
                scale=rstd[:, pq:pq + 1], bias=b2[:, pq:pq + 1]),
               reads=("outst", f"rstd{pq}", f"b2{pq}"), writes=("outst",),
               tag=f"E{sg}-app{tg}")
            op("act", lambda e, sg=sg, tg=tg: e.dma_start(
                out=out_d[sg * SG + tg * 128:sg * SG + (tg + 1) * 128, :],
                in_=outst[:]),
               reads=("outst",), dma="outst", tag=f"E{sg}-st{tg}")


def prep_in_map(inp, b, S=2048):
    f32 = np.float32
    u = np.ascontiguousarray(inp["L0_attn_u"][b], dtype=f32)[:S]
    W_in = np.asarray(inp["L0_attn_W_in"], f32)
    conv_w = np.asarray(inp["L0_attn_conv_w"], f32)
    conv_b = np.asarray(inp["L0_attn_conv_b"], f32)
    dt_bias = np.asarray(inp["L0_attn_dt_bias"], f32)
    A_log = np.asarray(inp["L0_attn_A_log"], f32)
    Dp = np.asarray(inp["L0_attn_D"], f32)
    norm_w = np.asarray(inp["L0_attn_norm_w"], f32)
    W_out = np.asarray(inp["L0_attn_W_out"], f32)
    fast_b = np.asarray(inp["L0_ffn_fast_b"], f32)
    w_fc = np.asarray(inp["L0_ffn_w_fc"], f32)
    w_proj = np.asarray(inp["L0_ffn_w_proj"], f32)
    u_lr = np.asarray(inp["L0_ffn_u"], f32)
    v_lr = np.asarray(inp["L0_ffn_v"], f32)
    gate = np.asarray(inp["L0_ffn_gate"], f32)
    m = {
        "u_tm": u,
        "u_fmT": np.ascontiguousarray(u.T),
        "winT": np.ascontiguousarray(W_in.T),
        "woutT": np.ascontiguousarray(W_out.T),
        "convw": np.ascontiguousarray(
            conv_w[:, 0, :].reshape(NFC, 128, 4).transpose(1, 0, 2).reshape(128, NFC * 4)),
        "convb": np.ascontiguousarray(conv_b.reshape(NFC, 128).T),
        "dtb": np.ascontiguousarray(np.tile(dt_bias, 4)[:, None]),
        "alog": np.ascontiguousarray(np.tile(A_log, 4)[:, None]),
        "nwpk": np.ascontiguousarray(norm_w.reshape(16, 128).T),
        "dflat": np.ascontiguousarray(np.repeat(Dp, HD)[None, :]),
        "identf": np.eye(128, dtype=f32),
        "triu": np.triu(np.ones((128, 128), f32)),
        "maskT": np.triu(np.ones((128, 128), f32)),
        "onesf": np.ones((1, 128), f32),
        "wfc": w_fc,
        "wproj": w_proj,
        "vT": np.ascontiguousarray(v_lr.T),
        "fbT": np.ascontiguousarray(fast_b.T),
        "uTlr": np.ascontiguousarray(u_lr.T),
        "gater": np.full((16, 1), gate, f32),
    }
    return m


def kernel(**inputs):
    S = inputs["L0_attn_u"].shape[1]
    nb = inputs["L0_attn_u"].shape[0]
    nc = build(S=S)
    in_maps = [prep_in_map(inputs, b, S) for b in range(nb)]
    res = run_bass_kernel_spmd(nc, in_maps, core_ids=list(range(nb)))
    return np.stack([np.asarray(r["out"], dtype=np.float32) for r in res.results])


# version 23
# speedup vs baseline: 1.0538x; 1.0538x over previous
import numpy as np
import ml_dtypes
import concourse.bass as bass
import concourse.mybir as mybir
from concourse.bass_utils import run_bass_kernel_spmd
from concourse.alu_op_type import AluOpType as OP

F32 = mybir.dt.float32
BF16 = mybir.dt.bfloat16
AF = mybir.ActivationFunctionType
AXX = mybir.AxisListType.X

DM = 1024
DI = 2048
DS = 64
NH = 32
HD = 64
CDIM = 2176
CH = 128
NFC = 17
EPS = 1e-5

ENGS = ("pe", "act", "dve", "pool", "sp")
ENG_ATTR = {"pe": "tensor", "act": "scalar", "dve": "vector", "pool": "gpsimd", "sp": "sync"}


class Builder:
    def __init__(self):
        self.ops = []
        self.eng_n = {e: 0 for e in ENGS}
        self.dma_n = {}
        self.last_w = {}
        self.readers = {}

    def _dep_waits(self, eng, reads, writes):
        waits = []
        def need(tag):
            if tag is None:
                return
            if tag[0] == "e":
                if tag[1] != eng or eng != "pe":
                    waits.append(tag)
            else:
                waits.append(tag)
        for k in reads:
            need(self.last_w.get(k))
        for k in writes:
            need(self.last_w.get(k))
            for t in self.readers.get(k, ()):
                need(t)
        return waits

    def op(self, eng, fn, reads=(), writes=(), dma=None, tag=""):
        reads = list(reads); writes = list(writes)
        waits = self._dep_waits(eng, reads, writes)
        if dma is None:
            self.eng_n[eng] += 1
            mytag = ("e", eng, self.eng_n[eng])
        else:
            k = self.dma_n.get(dma, 0)
            if k > 0:
                waits.append(("d", dma, k))
            self.dma_n[dma] = k + 1
            mytag = ("d", dma, k + 1)
        for k2 in reads:
            self.readers.setdefault(k2, []).append(mytag)
        for k2 in writes:
            self.last_w[k2] = mytag
            self.readers[k2] = []
        self.ops.append(dict(eng=eng, fn=fn, waits=waits, tag=mytag, dbg=tag))
        return mytag

    def barrier(self, tag="bar"):
        marks = {}
        for e in ENGS:
            fn, rd, wr = self.nopfns[e]
            marks[e] = self.op(e, fn, reads=rd, writes=wr, tag=f"{tag}-mark")
        dma_marks = [("d", q, self.dma_n[q]) for q in self.dma_n if self.dma_n[q] > 0]
        marks2 = {}
        for e in ENGS:
            fn, rd, wr = self.nopfns[e]
            marks2[e] = self.op(e, fn, reads=rd, writes=wr, tag=f"{tag}-wait")
            self.ops[-1]["waits"].extend(
                [marks[o] for o in ENGS if o != e] + dma_marks)
        for k in list(self.last_w):
            t = self.last_w[k]
            if t[0] == "e":
                self.last_w[k] = marks2[t[1]]
        self.readers = {k: [] for k in self.readers}

    def emit(self, nc, engines, esems, dsems):
        awaited = {e: set() for e in ENGS}
        for o in self.ops:
            for t in o["waits"]:
                if t[0] == "e":
                    awaited[t[1]].add(t[2])
        rank = {}
        inc_set = {}
        for e in ENGS:
            s = sorted(awaited[e])
            inc_set[e] = set(s)
            rank[e] = {idx: i + 1 for i, idx in enumerate(s)}
        self.final_counts = {e: len(awaited[e]) for e in ENGS}

        streams = {e: [o for o in self.ops if o["eng"] == e] for e in ENGS}
        wm = {}

        def emit_stream(e, eng_h):
            for o in streams[e]:
                for t in o["waits"]:
                    if t[0] == "e":
                        if t[2] not in rank[t[1]]:
                            continue
                        v = rank[t[1]][t[2]]
                        kk = (e, "e" + t[1])
                        if wm.get(kk, 0) >= v:
                            continue
                        wm[kk] = v
                        eng_h.wait_ge(esems[t[1]], v)
                    else:
                        v = 16 * t[2]
                        kk = (e, "d" + t[1])
                        if wm.get(kk, 0) >= v:
                            continue
                        wm[kk] = v
                        eng_h.wait_ge(dsems[t[1]], v)
                instr = o["fn"](eng_h)
                t = o["tag"]
                if t[0] == "d":
                    instr.then_inc(dsems[t[1]], 16)
                elif t[2] in inc_set[t[1]]:
                    instr.then_inc(esems[t[1]], 1)
        return emit_stream


class Arena:
    def __init__(self, nc, budget_bytes=206000):
        self.nc = nc
        self.budget = budget_bytes
        self.tiles = {}
        self.handle = None
        self.regions = {}
        self.pending = []

    def tile(self, name, shape, dtype, offset):
        per_part = int(np.prod(shape[1:])) * mybir.dt.size(dtype)
        assert offset % 32 == 0, (name, offset)
        assert offset + per_part <= self.budget, (name, offset, per_part, self.budget)
        h = self.nc.virtual_sbuf_tensor(name, list(shape), dtype)
        self.pending.append((h, offset))
        self.tiles[name] = (h, offset, per_part)
        return h

    def allocate(self):
        self.handle = self.nc.sbuf_tensor("arena", [128, self.budget], mybir.dt.uint8)
        self.handle_cm = self.handle
        h = self.handle.__enter__()
        base = self.nc.lookup_mloc(h).addr
        for hnd, off in self.pending:
            self.nc._tensor(hnd.name, hnd.shape, hnd.dtype, type="SB")
            mloc = self.nc.lookup_mloc(hnd)
            mloc.allocated = True
            mloc.addr = base + off
        return h

    def free(self):
        self.handle_cm.__exit__(None, None, None)


def ceil32(x):
    return (x + 31) // 32 * 32


def build(S=2048, budget=212832):
    NCH = S // CH
    SG = min(512, S)
    NSG = S // SG
    nc = bass.Bass("TRN2", target_bir_lowering=False)

    d = {}
    def di(name, shape, dtype=F32, kind="ExternalInput"):
        d[name] = nc.dram_tensor(name, list(shape), dtype, kind=kind).ap()
        return d[name]
    u_tm_d = di("u_tm", [S, DM])
    u_fmT_d = di("u_fmT", [DM, S])
    winT_d = di("winT", [DM, 4256])
    woutT_d = di("woutT", [DI, DM])
    convw_d = di("convw", [128, NFC * 4])
    convb_d = di("convb", [128, NFC])
    dtb_d = di("dtb", [128, 1])
    alog_d = di("alog", [128, 1])
    nwpk_d = di("nwpk", [128, 16])
    dflat_d = di("dflat", [1, DI])
    identf_d = di("identf", [128, 128])
    triu_d = di("triu", [128, 128])
    maskT_d = di("maskT", [128, 128])
    onesf_d = di("onesf", [1, 128])
    wfc_d = di("wfc", [4096, DM])
    wproj_d = di("wproj", [DM, 4096])
    vT_d = di("vT", [DM, 16])
    fbT_d = di("fbT", [16, 16])
    uTlr_d = di("uTlr", [16, 4096])
    gater_d = di("gater", [16, 1])
    out_d = di("out", [S, DM], F32, "ExternalOutput")
    aflat_d = di("acumflat", [NCH, NH * CH], F32, "Internal")
    x1_d = di("x1d", [S, DM], F32, "Internal")
    ternfc_d = di("ternfc", [4096, DM], BF16, "Internal")
    ternproj_d = di("ternproj", [DM, 4096], BF16, "Internal")
    cdf_d = di("cdf", [8, 128], F32, "Internal")

    PERS = 4096
    ar = Arena(nc, budget)
    off = [0]
    def T(name, shape, dtype, at=None):
        if at is not None:
            off[0] = at
        t = ar.tile(name, shape, dtype, off[0])
        off[0] = ceil32(off[0] + int(np.prod(shape[1:])) * mybir.dt.size(dtype))
        return t
    off[0] = budget - PERS
    identf_s = T("identf_s", [128, 128], F32)
    identb_s = T("identb_s", [128, 128], BF16)
    triu_s = T("triu_s", [128, 128], F32)
    maskT_s = T("maskT_s", [128, 128], F32)
    convw_s = T("convw_s", [128, NFC * 4], F32)
    convb_s = T("convb_s", [128, NFC], F32)
    dtb_s = T("dtb_s", [128, 1], F32)
    alog_s = T("alog_s", [128, 1], F32)
    nwpk_s = T("nwpk_s", [128, 16], F32)
    onesf_s = T("onesf_s", [1, 128], F32)
    dumact = T("dumact", [1, 8], F32)
    dumdve = T("dumdve", [1, 8], F32)
    dumpool = T("dumpool", [1, 8], F32)
    cbias = T("cbias", [128, 4], F32)
    ssum = T("ssum", [128, 2], F32)
    xsum = T("xsum", [128, 2], F32)
    x2sum = T("x2sum", [128, 2], F32)
    mstat = T("mstat", [128, 2], F32)
    lnt = T("lnt", [128, 2], F32)
    mm_t = T("mm_t", [128, 2], F32)
    vstat = T("vstat", [128, 2], F32)
    rstd = T("rstd", [128, 2], F32)
    b2 = T("b2", [128, 2], F32)
    nalog = T("nalog", [128, 1], F32)
    scs = T("scs", [128, 2], F32)
    rcs = T("rcs", [128, 1], F32)
    hscs = T("hscs", [128, 1], F32)
    gate_s = T("gate_s", [16, 1], F32)
    assert off[0] <= budget, off[0]
    PERS_USED = off[0] - (budget - PERS)

    off[0] = 0
    winT = T("winT_s", [128, 8 * 4256], BF16)
    woutT = T("woutT_s", [128, 16 * 1024], BF16)
    u_tm = T("u_tm_s", [128, DM], F32)
    u_fm = T("u_fm_s", [128, 8 * 256], BF16)
    sz = [T("sz0", [128, DI], BF16), T("sz1", [128, DI], BF16)]
    xbcraw = T("xbcraw", [128, NFC * 260], BF16)
    convtmp = [T(f"convtmp{i}", [128, 512], F32) for i in range(2)]
    xact_pair = T("xact_pair", [128, 16 * 256], BF16)
    B_s2 = T("B_s2", [64, 256], BF16)
    C_s2 = T("C_s2", [64, 256], BF16)
    B_tm2 = T("B_tm2", [128, 128], BF16)
    subl = T("subl", [128, NCH * 32], F32)
    sdol = T("sdol", [128, NCH * 32], F32)
    coefl = T("coefl", [128, NCH * 32], F32)
    crepl = T("crepl", [128, NCH * 32], F32)
    ddb = T("ddb_t", [128, 16 * CH], F32)
    lt = T("lt", [128, 16 * CH], BF16)
    gtm = T("gtm", [128, 128], BF16)
    xt = [T("xt0", [128, 16 * 64], BF16), T("xt1", [128, 16 * 64], BF16)]
    xtc = T("xtc", [128, 16 * 64], BF16)
    srun_bf = T("srun_bf", [64, DI], BF16)
    srun = T("srun", [128, 1024], F32)
    y1 = T("y1", [128, DI], F32)
    ynorm = T("ynorm", [128, DI], BF16)
    yt = T("yt", [128, DI], BF16)
    drep = T("drep", [128, DI], BF16)
    x1s = T("x1s", [128, DM], F32)
    x1n = T("x1n", [128, DM], F32)
    CEND = off[0]
    cdrow8 = T("cdrow8", [1, 128], F32, at=budget - PERS + ceil32(PERS_USED))
    acum_pk = T("acum_pk", [128, 1024], F32, at=ar.tiles["u_tm_s"][1])
    subt_pk = T("subt_pk", [128, 1024], F32, at=ar.tiles["sz0"][1])
    coeft_pk = T("coeft_pk", [128, 1024], F32, at=ar.tiles["drep"][1])
    adlh = T("adlh", [128, 256], F32, at=ar.tiles["convtmp0"][1])
    wdt_s = T("wdt_s", [128, 8 * 32], BF16, at=ar.tiles["convtmp1"][1])
    y1_off = ar.tiles["y1"][1]
    dt_pk = T("dt_pk", [128, 1024], F32, at=y1_off)
    lndt_pk = T("lndt_pk", [128, 1024], F32)
    ad_pk = T("ad_pk", [128, 1024], F32)
    assert off[0] <= ar.tiles["yt"][1], (off[0], ar.tiles["yt"][1])
    assert CEND <= budget - PERS, (CEND, budget - PERS)

    off[0] = 0
    wfcT = T("wfcT", [128, 8 * 4096], BF16, at=0)
    wprT = T("wprT", [128, 32 * 1024], BF16)
    EW = off[0]
    x1tm = T("x1tm", [128, 4 * 1024], F32)
    x1bf = T("x1bf", [128, 1024], BF16)
    x1fm = T("x1fm", [128, 8 * SG], BF16)
    r2 = T("r2", [128, 32 * SG], BF16)
    hr = [T(f"hr{i}", [128, SG], BF16) for i in range(2)]
    t1b = T("t1b", [16, SG], BF16)
    t2b = T("t2b", [16, SG], BF16)
    vtb = T("vtb", [128, 8 * 16], BF16)
    fbtb = T("fbtb", [16, 16], BF16)
    uTg = T("uTg", [16, 4096], BF16)
    outst = T("outst", [128, DM], F32)
    EEND = off[0]
    assert EEND <= budget - PERS, EEND
    wst = [T("wst0", [128, 2048], F32, at=ar.tiles["ddb_t"][1]),
           T("wst1", [128, 2048], F32, at=ar.tiles["lt"][1]),
           T("wst2", [128, 2048], F32, at=ar.tiles["xbcraw"][1]),
           T("wst3", [128, 2048], F32, at=ar.tiles["xact_pair"][1])]
    trow = [T("trow0", [128, 2048], BF16, at=ar.tiles["srun_bf"][1]),
            T("trow1", [128, 2048], BF16, at=ar.tiles["x1s"][1])]
    trow2 = T("trow2", [128, 2048], BF16, at=ar.tiles["x1n"][1])
    part1 = T("part1", [128, 256], F32, at=ar.tiles["yt"][1])
    part2 = T("part2", [128, 8], F32)
    chs = T("chs", [128, 2], F32)
    cls = T("cls", [128, 2], F32)
    ncls = T("ncls", [128, 2], F32)
    assert off[0] <= ar.tiles["yt"][1] + 4096

    ar.allocate()

    B = Builder()
    import contextlib
    es = contextlib.ExitStack()
    with es:
        pa = es.enter_context(nc.psum_tensor("pa", [128, 1024], F32))
        pb = es.enter_context(nc.psum_tensor("pb", [128, 1024], F32))
        pt = es.enter_context(nc.psum_tensor("pt", [128, 2048], BF16))
        pe_ = es.enter_context(nc.psum_tensor("pe_", [128, 512], F32))
        pf = es.enter_context(nc.psum_tensor("pf", [128, 512], F32))

        B.nopfns = {
            "pe": (lambda e: e.matmul(pa[0:1, 0:8], identb_s[0:1, 0:1],
                                      identb_s[0:1, 0:8], start=True, stop=True),
                   ("identb",), ("pa0",)),
            "act": (lambda e: e.memzero(dumact[:]), ("dumact",), ("dumact",)),
            "dve": (lambda e: e.memset(dumdve[:], 0.0), ("dumdve",), ("dumdve",)),
            "pool": (lambda e: e.memset(dumpool[:], 0.0), ("dumpool",), ("dumpool",)),
            "sp": (lambda e: e.nop(), (), ()),
        }

        op = B.op

        op("pool", lambda e: e.dma_start(
            out=wdt_s[:].rearrange("p (kc c) -> p kc c", c=32),
            in_=winT_d[:, 4224:4256].rearrange("(kc p) c -> p kc c", p=128)),
           writes=("wdt",), dma="wdt", tag="ld-wdt")
        for nm, dst in [("identf", identf_s), ("triu", triu_s), ("maskT", maskT_s),
                        ("convw", convw_s), ("convb", convb_s), ("dtb", dtb_s),
                        ("alog", alog_s), ("nwpk", nwpk_s), ("onesf", onesf_s)]:
            op("sp", lambda e, a=d[nm], b=dst: e.dma_start(out=b[:], in_=a),
               writes=(nm,), dma="cst", tag=f"ld-{nm}")
        op("pool", lambda e: e.dma_start(out=identb_s[:], in_=d["identf"]),
           writes=("identb",), dma="cst2", tag="ld-identb")
        def emit_bigw():
            for kc in range(8):
                op("pool", lambda e, kc=kc: e.dma_start(
                    out=winT[:, kc * 4256:(kc + 1) * 4256],
                    in_=winT_d.rearrange("(kc p) c -> p kc c", p=128)[:, kc]),
                   writes=(f"winT{kc}",), dma="winT", tag=f"ld-winT{kc}")
            for q in range(4):
                op("pool", lambda e, q=q: e.dma_start(
                    out=woutT[:, q * 4096:(q + 1) * 4096],
                    in_=woutT_d.rearrange("(kc p) c -> p kc c", p=128)
                        [:, 4 * q:4 * (q + 1)]),
                   writes=(f"woutT{q}",), dma="woutT", tag=f"ld-woutT{q}")
        op("act", lambda e: e.activation(out=nalog[:], in_=alog_s[:], func=AF.Exp),
           reads=("alog",), writes=("nalog",), tag="expA")
        op("pool", lambda e: e.tensor_scalar(out=nalog[:], in0=nalog[:], scalar1=-1.0,
                                            scalar2=None, op0=OP.mult),
           writes=("nalog",), tag="negA")
        op("pool", lambda e: e.memset(cbias[:, 0:1], EPS), writes=("cbias",), tag="cb0")
        op("pool", lambda e: e.memset(cbias[:, 1:2], 1.0), writes=("cbias",), tag="cb1")
        op("pool", lambda e: e.memset(cbias[:, 2:3], -0.5), writes=("cbias",), tag="cb2")
        op("pool", lambda e: e.memset(cbias[:, 3:4], 0.5), writes=("cbias",), tag="cb3")

        d_items = script_D(B, nc, d, locals())
        di = [0]

        def emitD(n):
            while n > 0 and di[0] < len(d_items):
                d_items[di[0]]()
                di[0] += 1
                n -= 1

        NPAIR = NCH // 2
        op("pool", lambda e: e.memset(dt_pk[:], 0.0), writes=("dt_pk",), tag="z-dtpk")
        for c in range(NCH):
            bse = 64 * (c % 2)
            col = 128 * (c // 2)
            cp = c % 2
            pp = pe_ if cp == 0 else pf
            bk = "pe" if cp == 0 else "pf"
            op("pool", lambda e, c=c: e.dma_start(
                out=u_fm[:, 0:1024].rearrange("p (kc t) -> p kc t", t=128),
                in_=u_fmT_d[:, c * CH:(c + 1) * CH].rearrange("(kc p) t -> p kc t", p=128)),
               writes=("u_fm",), dma=f"ufm{cp}", tag=f"B-ufm{c}")
            for kc in range(8):
                op("pe", lambda e, kc=kc, pp=pp: e.matmul(
                    pp[0:32, 0:128], wdt_s[:, kc * 32:(kc + 1) * 32],
                    u_fm[:, kc * 128:(kc + 1) * 128],
                    start=(kc == 0), stop=(kc == 7)),
                   reads=("wdt", "u_fm"), writes=(bk,), tag=f"B-dt{c}.{kc}")
            op("act", lambda e, bse=bse, col=col, pp=pp: e.copy(
                out=dt_pk[bse:bse + 32, col:col + 128], in_=pp[0:32, 0:128]),
               reads=(bk,), writes=("dt_pk",), tag=f"B-ev{c}")
            if c == 1:
                emit_bigw()
            emitD(2)
        op("act", lambda e: e.activation(out=lndt_pk[:], in_=dt_pk[:], func=AF.Relu,
                                         bias=dtb_s[:, 0:1], scale=1.0),
           reads=("dt_pk", "dtb"), writes=("lndt_pk",), tag="sp-relu")
        op("act", lambda e: e.activation(out=ad_pk[:], in_=dt_pk[:], func=AF.Abs,
                                         bias=dtb_s[:, 0:1], scale=1.0),
           reads=("dt_pk", "dtb"), writes=("ad_pk",), tag="sp-abs")
        op("act", lambda e: e.activation(out=ad_pk[:], in_=ad_pk[:], func=AF.Exp,
                                         scale=-1.0),
           reads=("ad_pk",), writes=("ad_pk",), tag="sp-exp")
        op("act", lambda e: e.activation(out=ad_pk[:], in_=ad_pk[:], func=AF.Ln,
                                         bias=cbias[:, 1:2], scale=1.0),
           reads=("ad_pk", "cbias"), writes=("ad_pk",), tag="sp-ln")
        op("pool", lambda e: e.tensor_tensor(out=dt_pk[:], in0=lndt_pk[:], in1=ad_pk[:],
                                            op=OP.add),
           reads=("lndt_pk", "ad_pk"), writes=("dt_pk",), tag="sp-add")
        op("act", lambda e: e.activation(out=lndt_pk[:], in_=dt_pk[:], func=AF.Ln),
           reads=("dt_pk",), writes=("lndt_pk",), tag="lndt")
        op("pool", lambda e: e.tensor_tensor(out=ad_pk[:], in0=dt_pk[:],
                                            in1=nalog[:, 0:1].broadcast_to([128, 1024]),
                                            op=OP.mult),
           reads=("dt_pk", "nalog"), writes=("ad_pk",), tag="Ad")
        for j in range(NPAIR):
            col = 128 * j
            jp = j % 2
            ao = 128 * jp
            pp = pe_ if jp == 0 else pf
            bk = "pe" if jp == 0 else "pf"
            op("pe", lambda e, col=col, pp=pp: e.transpose(
                pp[:, 0:128], ad_pk[:, col:col + 128], identf_s[:]),
               reads=("ad_pk", "identf"), writes=(bk,), tag=f"B-tAd{j}")
            op("act", lambda e, ao=ao, pp=pp: e.copy(
                out=adlh[:, ao:ao + 128], in_=pp[:, 0:128]),
               reads=(bk,), writes=(f"adlh{jp}",), tag=f"B-evAd{j}")
            op("pe", lambda e, ao=ao, pp=pp: e.matmul(
                pp[:, 128:256], adlh[:, ao:ao + 128], triu_s[:],
                start=True, stop=True),
               reads=(f"adlh{jp}", "triu"), writes=(bk,), tag=f"B-mh{j}")
            op("act", lambda e, col=col, pp=pp: e.copy(
                out=acum_pk[:, col:col + 128], in_=pp[:, 128:256]),
               reads=(bk,), writes=("acum_pk",), tag=f"B-evh{j}")
            op("pe", lambda e, ao=ao, pp=pp: e.matmul(
                pp[:, 256:384], triu_s[:], adlh[:, ao:ao + 128],
                start=True, stop=True),
               reads=(f"adlh{jp}", "triu"), writes=(bk,), tag=f"B-ml{j}")
            for q2 in range(2):
                op("act", lambda e, j=j, q2=q2, pp=pp: e.activation(
                    out=sdol[:, (2 * j + q2) * 32:(2 * j + q2) * 32 + 32],
                    in_=pp[:, 256 + 64 * q2:256 + 64 * q2 + 32], func=AF.Exp),
                   reads=(bk,), writes=("sdol",), tag=f"B-sdo{j}.{q2}")
            emitD(1)
        op("pool", lambda e: e.tensor_tensor(out=subt_pk[:, 0:NPAIR * 128],
                                            in0=acum_pk[:, 0:NPAIR * 128],
                                            in1=lndt_pk[:, 0:NPAIR * 128],
                                            op=OP.subtract),
           reads=("acum_pk", "lndt_pk"), writes=("subt_pk",), tag="B-sub")
        for j in range(NPAIR):
            col = 128 * j
            jp = j % 2
            pp = pe_ if jp == 0 else pf
            bk = "pe" if jp == 0 else "pf"
            op("act", lambda e, col=col: e.activation(
                out=coeft_pk[:, col:col + 128], in_=subt_pk[:, col:col + 128],
                func=AF.Exp, scale=-1.0, bias=acum_pk[:, col + 127:col + 128]),
               reads=("subt_pk", "acum_pk"), writes=("coeft_pk",), tag=f"B-coef{j}")
            for q2 in range(2):
                c = 2 * j + q2
                op("act", lambda e, c=c, col=col, q2=q2: e.dma_start(
                    out=aflat_d[c:c + 1, :].rearrange("o (h l) -> (o h) l", h=32),
                    in_=acum_pk[64 * q2:64 * q2 + 32, col:col + 128]),
                   reads=("acum_pk",), writes=("aflat",), dma=f"aflat{c % 2}",
                   tag=f"B-fl{c}")
            op("pe", lambda e, col=col, pp=pp: e.transpose(
                pp[:, 0:128], subt_pk[:, col:col + 128], identf_s[:]),
               reads=("subt_pk", "identf"), writes=(bk,), tag=f"B-tsub{j}")
            for q2 in range(2):
                c = 2 * j + q2
                op("act", lambda e, c=c, q2=q2, pp=pp: e.copy(
                    out=subl[:, c * 32:c * 32 + 32],
                    in_=pp[:, 64 * q2:64 * q2 + 32]),
                   reads=(bk,), writes=("subl",), tag=f"B-evsub{c}")
            op("pe", lambda e, col=col, pp=pp: e.transpose(
                pp[:, 128:256], coeft_pk[:, col:col + 128], identf_s[:]),
               reads=("coeft_pk", "identf"), writes=(bk,), tag=f"B-tcoef{j}")
            for q2 in range(2):
                c = 2 * j + q2
                op("act", lambda e, c=c, q2=q2, pp=pp: e.copy(
                    out=coefl[:, c * 32:c * 32 + 32],
                    in_=pp[:, 128 + 64 * q2:128 + 64 * q2 + 32]),
                   reads=(bk,), writes=("coefl",), tag=f"B-evcoef{c}")
            emitD(1)
        op("act", lambda e: e.activation(
            out=coeft_pk[:, 0:NPAIR],
            in_=acum_pk[:, 0:NPAIR * 128].rearrange("p (j l) -> p j l", l=128)[:, :, 127:128],
            func=AF.Exp),
           reads=("acum_pk",), writes=("cdall",), tag="B-cdall")
        op("pe", lambda e: e.transpose(pe_[0:NPAIR, 0:128], coeft_pk[:, 0:NPAIR],
                                       identf_s[:]),
           reads=("cdall", "identf"), writes=("pe",), tag="B-tcd")
        op("act", lambda e: e.copy(out=adlh[0:NPAIR, 0:128], in_=pe_[0:NPAIR, 0:128]),
           reads=("pe",), writes=("cdsb",), tag="B-evcd")
        for j in range(NPAIR):
            jp = j % 2
            pp = pe_ if jp == 0 else pf
            bk = "pe" if jp == 0 else "pf"
            op("act", lambda e, j=j: e.dma_start(out=cdrow8[:],
                                                   in_=adlh[j:j + 1, 0:128]),
               reads=("cdsb",), writes=("cdrow8",), dma="cdf", tag=f"B-cdmv{j}")
            op("pe", lambda e, pp=pp: e.matmul(pp[:, 384:512], onesf_s[:], cdrow8[:],
                                               start=True, stop=True),
               reads=("cdrow8", "onesf"), writes=(bk,), tag=f"B-mcd{j}")
            for q2 in range(2):
                c = 2 * j + q2
                op("act", lambda e, c=c, q2=q2, pp=pp: e.copy(
                    out=crepl[:, c * 32:c * 32 + 32],
                    in_=pp[:, 384 + 64 * q2:384 + 64 * q2 + 32]),
                   reads=(bk,), writes=("crepl",), tag=f"B-evcrep{c}")
        emitD(len(d_items))
        op("dve", lambda e: e.tensor_tensor(
            out=woutT[:].rearrange("p (kc c) -> p kc c", c=1024),
            in0=woutT[:].rearrange("p (kc c) -> p kc c", c=1024),
            in1=nwpk_s[:, :, None].broadcast_to([128, 16, 1024]),
            op=OP.mult),
           reads=("woutT0", "woutT1", "woutT2", "woutT3", "nwpk"),
           writes=("woutT0", "woutT1", "woutT2", "woutT3"), tag="nwfold")

        B.barrier("pre-C")
        op("sp", lambda e: e.dma_start(out=ddb[:, 0:DI],
                                       in_=dflat_d.broadcast_to([128, DI])),
           writes=("ddb",), dma="ddb0", tag="dbcast")
        op("dve", lambda e: e.tensor_copy(out=drep[:], in_=ddb[:, 0:DI]),
           reads=("ddb",), writes=("drep",), tag="drep")
        op("dve", lambda e: e.memset(srun[:], 0.0), writes=("srun",), tag="z-srun")
        op("dve", lambda e: e.memset(
            xbcraw[:].rearrange("p (fc c) -> p fc c", c=260)[:, :, 0:4], 0.0),
           writes=('xbc0', 'xbc1', 'xbc2', 'xbc3', 'xbc4', 'xbc5', 'xbc6', 'xbc7', 'xbc8'), tag="z-halo")
        script_C(B, nc, d, locals(), NCH)
        B.barrier("C-E")
        script_E(B, nc, d, locals(), S, SG, NSG)
        B.barrier("end")

        esems = {e: es.enter_context(nc.semaphore(f"s_{e}")) for e in ENGS}
        dsems = {ch: es.enter_context(nc.semaphore(f"d_{ch}")) for ch in B.dma_n}
        block = es.enter_context(nc.Block())

        emit_stream = B.emit(nc, None, esems, dsems)

        @block.tensor
        def _(e):
            emit_stream("pe", e)

        @block.scalar
        def _(e):
            emit_stream("act", e)

        @block.vector
        def _(e):
            emit_stream("dve", e)

        @block.gpsimd
        def _(e):
            emit_stream("pool", e)

        @block.sync
        def _(e):
            emit_stream("sp", e)

    ar.free()
    return nc


def script_C(B, nc, d, V, NCH):
    op = B.op
    g = lambda n: V[n]
    (pa, pb, pt, pe_, pf) = (g("pa"), g("pb"), g("pt"), g("pe_"), g("pf"))
    winT, woutT, u_tm, u_fm = g("winT"), g("woutT"), g("u_tm"), g("u_fm")
    sz, xbcraw, convtmp = g("sz"), g("xbcraw"), g("convtmp")
    xact_pair, B_s2, C_s2, B_tm2 = g("xact_pair"), g("B_s2"), g("C_s2"), g("B_tm2")
    subl, sdol, coefl, crepl = g("subl"), g("sdol"), g("coefl"), g("crepl")
    ddb, lt, gtm, xtc = g("ddb"), g("lt"), g("gtm"), g("xtc")
    xt = g("xt")
    srun_bf, srun, y1, ynorm, yt = g("srun_bf"), g("srun"), g("y1"), g("ynorm"), g("yt")
    drep, x1s, x1n = g("drep"), g("x1s"), g("x1n")
    identb_s, identf_s, maskT_s = g("identb_s"), g("identf_s"), g("maskT_s")
    convw_s, convb_s = g("convw_s"), g("convb_s")
    ssum, xsum, x2sum = g("ssum"), g("xsum"), g("x2sum")
    mstat, lnt, mm_t, vstat, rstd, b2 = (g("mstat"), g("lnt"), g("mm_t"),
                                          g("vstat"), g("rstd"), g("b2"))
    cbias = g("cbias")
    u_fmT_d, u_tm_d, aflat_d, x1_d = d["u_fmT"], d["u_tm"], d["acumflat"], d["x1d"]
    xr3 = xbcraw[:].rearrange("p (fc c) -> p fc c", c=260)
    PABANK = [("pa0", pa, 0), ("pa1", pa, 512), ("pb0", pb, 0), ("pb1", pb, 512)]
    WK = [f"winT{kc}" for kc in range(8)]
    WOK = ["woutT0", "woutT1", "woutT2", "woutT3"]

    def inproj_xbc_pair(c0):
        for w in range(2):
            for q in range(4):
                fc0 = w * 8 + q * 2
                bk, pp, co = PABANK[q]
                for fc in (fc0, fc0 + 1):
                    cc = co + (fc - fc0) * 256
                    for kc in range(8):
                        op("pe", lambda e, fc=fc, kc=kc, pp=pp, cc=cc: e.matmul(
                            pp[:, cc:cc + 256],
                            winT[:, kc * 4256 + 2048 + fc * 128:
                                 kc * 4256 + 2048 + (fc + 1) * 128],
                            u_fm[:, kc * 256:(kc + 1) * 256],
                            start=(kc == 0), stop=(kc == 7)),
                           reads=(WK[kc], f"ufm{kc // 4}"), writes=(bk,),
                           tag=f"C{c0}-x{fc}.{kc}")
                op("act", lambda e, fc0=fc0, pp=pp, co=co: e.copy(
                    out=xr3[:, fc0:fc0 + 2, 4:260], in_=pp[:, co:co + 512]),
                   reads=(bk,), writes=(f"xbc{fc0 // 2}",), tag=f"C{c0}-xev{fc0}")
        for kc in range(8):
            op("pe", lambda e, kc=kc: e.matmul(
                pe_[:, 0:256], winT[:, kc * 4256 + 2048 + 16 * 128:
                                    kc * 4256 + 2048 + 17 * 128],
                u_fm[:, kc * 256:(kc + 1) * 256],
                start=(kc == 0), stop=(kc == 7)),
               reads=(WK[kc], f"ufm{kc // 4}"), writes=("pe",), tag=f"C{c0}-x16.{kc}")
        op("act", lambda e: e.copy(out=xr3[:, 16:17, 4:260], in_=pe_[:, 0:256]),
           reads=("pe",), writes=("xbc8",), tag=f"C{c0}-xev16")

    def inproj_z(c):
        par = c % 2
        for gi, (bk, pp, co) in enumerate(PABANK):
            for kc in range(8):
                op("pe", lambda e, gi=gi, kc=kc, pp=pp, co=co, par=par: e.matmul(
                    pp[:, co:co + 512],
                    u_fm[:, kc * 256 + par * 128:kc * 256 + par * 128 + 128],
                    winT[:, kc * 4256 + gi * 512:kc * 4256 + (gi + 1) * 512],
                    start=(kc == 0), stop=(kc == 7)),
                   reads=(WK[kc], f"ufm{kc // 4}"), writes=(bk,),
                   tag=f"C{c}-z{gi}.{kc}")
            op("act", lambda e, gi=gi, pp=pp, co=co, par=par: e.activation(
                out=sz[par][:, gi * 512:(gi + 1) * 512], in_=pp[:, co:co + 512],
                func=AF.Silu),
               reads=(bk,), writes=(f"sz{par}",), tag=f"C{c}-szev{gi}")

    def conv_pair(c0):
        for gg in range(9):
            fcs = [2 * gg, 2 * gg + 1] if gg < 8 else [16]
            ck = f"convtmp{gg % 2}"
            for j in range(4):
                for fc in fcs:
                    q = (fc % 2) * 256
                    if j == 0:
                        op("dve", lambda e, fc=fc, q=q, gg=gg: e.scalar_tensor_tensor(
                            out=convtmp[gg % 2][:, q:q + 256], in0=xr3[:, fc, 1:257],
                            scalar=convw_s[:, fc * 4:fc * 4 + 1],
                            in1=convb_s[:, fc:fc + 1].broadcast_to([128, 256]),
                            op0=OP.mult, op1=OP.add),
                           reads=(f"xbc{gg}", "convw", "convb"), writes=(ck,),
                           tag=f"C{c0}-cv{fc}.0")
                    else:
                        op("dve", lambda e, fc=fc, j=j, q=q, gg=gg: e.scalar_tensor_tensor(
                            out=convtmp[gg % 2][:, q:q + 256],
                            in0=xr3[:, fc, 1 + j:257 + j],
                            scalar=convw_s[:, fc * 4 + j:fc * 4 + j + 1],
                            in1=convtmp[gg % 2][:, q:q + 256], op0=OP.mult, op1=OP.add),
                           reads=(f"xbc{gg}", "convw", ck), writes=(ck,),
                           tag=f"C{c0}-cv{fc}.{j}")
            if gg < 8:
                op("act", lambda e, gg=gg: e.activation(
                    out=xact_pair[:, gg * 512:(gg + 1) * 512], in_=convtmp[gg % 2][:],
                    func=AF.Silu),
                   reads=(ck,), writes=("xact",), tag=f"C{c0}-cs{gg}")
            else:
                op("act", lambda e: e.activation(
                    out=B_s2[:], in_=convtmp[0][0:64, 0:256], func=AF.Silu),
                   reads=("convtmp0",), writes=("Bs",), tag=f"C{c0}-csB")
                op("act", lambda e: e.activation(
                    out=C_s2[:], in_=convtmp[0][64:128, 0:256], func=AF.Silu),
                   reads=("convtmp0",), writes=("Cs",), tag=f"C{c0}-csC")
        op("dve", lambda e: e.tensor_copy(out=xr3[:, :, 1:4], in_=xr3[:, :, 257:260]),
           reads=('xbc0', 'xbc1', 'xbc2', 'xbc3', 'xbc4', 'xbc5', 'xbc6', 'xbc7', 'xbc8'), writes=('xbc0', 'xbc1', 'xbc2', 'xbc3', 'xbc4', 'xbc5', 'xbc6', 'xbc7', 'xbc8'), tag=f"C{c0}-halo")

    def ufm_load(c):
        for hf in range(2):
            op("pool", lambda e, c=c, hf=hf: e.dma_start(
                out=u_fm[:, hf * 1024:(hf + 1) * 1024]
                    .rearrange("p (kc t) -> p kc t", t=256),
                in_=u_fmT_d[hf * 512:(hf + 1) * 512, c * CH:(c + 2) * CH]
                    .rearrange("(kc p) t -> p kc t", p=128)),
               writes=(f"ufm{hf}",), dma=f"ufm{hf}", tag=f"C{c}-ufm{hf}")

    def chunk_stage(c):
        par = c % 2
        op("sp", lambda e, c=c: e.dma_start(
            out=ddb[:], in_=aflat_d[c:c + 1, 0:2048].broadcast_to([128, 2048])),
           reads=("aflat",), writes=("ddb",), dma="ddb0", tag=f"C{c}-db0")
        op("sp", lambda e, c=c: e.dma_start(
            out=u_tm[:], in_=u_tm_d[c * CH:(c + 1) * CH, :]),
           writes=("u_tm",), dma="utm", tag=f"C{c}-utm")

        for hh in range(2):
            op("dve", lambda e, c=c, hh=hh: e.tensor_tensor(
                out=ddb[:].rearrange("p (h l) -> p h l", l=CH),
                in0=ddb[:].rearrange("p (h l) -> p h l", l=CH),
                in1=subl[:, c * 32 + hh * 16:c * 32 + (hh + 1) * 16][:, :, None]
                    .broadcast_to([128, 16, CH]),
                op=OP.subtract),
               reads=("ddb", "subl"), writes=("ddb",), tag=f"C{c}-sub{hh}")
            if hh == 0:
                op("act", lambda e: e.activation(out=lt[:], in_=ddb[:],
                                                 func=AF.Exp),
                   reads=("ddb",), writes=("lt",), tag=f"C{c}-exp0")
                op("sp", lambda e, c=c: e.dma_start(
                    out=ddb[:],
                    in_=aflat_d[c:c + 1, 2048:4096].broadcast_to([128, 2048])),
                   reads=("aflat",), writes=("ddb",), dma="ddb1", tag=f"C{c}-db1")
            else:
                op("act", lambda e: e.activation(out=ddb[:], in_=ddb[:],
                                                 func=AF.Exp),
                   reads=("ddb",), writes=("ddb",), tag=f"C{c}-exp1")

        op("pe", lambda e, par=par: e.transpose(
            pt[:, 0:64], B_s2[:, par * 128:par * 128 + 128][0:64, :]
            if False else B_s2[0:64, par * 128:par * 128 + 128],
            identb_s[0:64, 0:64]),
           reads=("Bs", "identb"), writes=("pt0",), tag=f"C{c}-tB")
        op("dve", lambda e, par=par: e.tensor_copy(
            out=B_tm2[:, par * 64:par * 64 + 64], in_=pt[:, 0:64]),
           reads=("pt0",), writes=("Btm",), tag=f"C{c}-evB")
        op("pe", lambda e, par=par: e.matmul(
            pe_[:, 128:256], B_s2[0:64, par * 128:par * 128 + 128],
            C_s2[0:64, par * 128:par * 128 + 128], start=True, stop=True),
           reads=("Bs", "Cs"), writes=("pe",), tag=f"C{c}-GT")
        op("dve", lambda e: e.tensor_tensor(out=gtm[:], in0=pe_[:, 128:256],
                                            in1=maskT_s[:], op=OP.mult),
           reads=("pe", "maskT"), writes=("gtm",), tag=f"C{c}-GTm")
        op("act", lambda e: e.copy(out=srun_bf[:, 0:1024], in_=srun[0:64, :]),
           reads=("srun",), writes=("srunbf",), tag=f"C{c}-stc0")
        op("act", lambda e: e.copy(out=srun_bf[:, 1024:2048], in_=srun[64:128, :]),
           reads=("srun",), writes=("srunbf",), tag=f"C{c}-stc1")

        for hh in range(2):
            ptk = f"pt{hh}"
            ptco = hh * 1024
            for i in range(8):
                op("pe", lambda e, i=i, hh=hh, par=par, ptco=ptco: e.transpose(
                    pt[:, ptco + i * 128:ptco + (i + 1) * 128],
                    xact_pair[:, (hh * 8 + i) * 256 + par * 128:
                              (hh * 8 + i) * 256 + par * 128 + 128],
                    identb_s[:]),
                   reads=("xact", "identb"), writes=(ptk,), tag=f"C{c}-tx{hh}.{i}")
            op("act", lambda e, ptco=ptco, hh=hh: e.copy(
                out=xt[hh][:], in_=pt[:, ptco:ptco + 1024]),
               reads=(ptk,), writes=(f"xt{hh}",), tag=f"C{c}-evx{hh}")
            op("dve", lambda e, hh=hh: e.tensor_tensor(
                out=xtc[:], in0=xt[hh][:], in1=drep[:, hh * 1024:(hh + 1) * 1024],
                op=OP.mult),
               reads=(f"xt{hh}", "drep"), writes=("xtc",), tag=f"C{c}-xtd{hh}")
            for j in range(2):
                op("pe", lambda e, j=j, hh=hh, par=par: e.matmul(
                    pa[:, j * 512:(j + 1) * 512],
                    C_s2[0:64, par * 128:par * 128 + 128],
                    srun_bf[:, hh * 1024 + j * 512:hh * 1024 + (j + 1) * 512],
                    start=True, stop=True),
                   reads=("Cs", "srunbf"), writes=(f"pa{j}",), tag=f"C{c}-yo{hh}.{j}")
            if hh == 0:
                op("dve", lambda e: e.scalar_tensor_tensor(
                    out=lt[:].rearrange("p (h l) -> p h l", l=CH),
                    in0=lt[:].rearrange("p (h l) -> p h l", l=CH), scalar=1e30,
                    in1=gtm[:, None, :].broadcast_to([128, 16, CH]),
                    op0=OP.min, op1=OP.mult),
                   reads=("lt", "gtm"), writes=("lt",), tag=f"C{c}-stt{hh}")
            else:
                op("dve", lambda e: e.scalar_tensor_tensor(
                    out=lt[:].rearrange("p (h l) -> p h l", l=CH),
                    in0=ddb[:].rearrange("p (h l) -> p h l", l=CH), scalar=1e30,
                    in1=gtm[:, None, :].broadcast_to([128, 16, CH]),
                    op0=OP.min, op1=OP.mult),
                   reads=("ddb", "gtm"), writes=("lt",), tag=f"C{c}-stt{hh}")
            for h in range(16):
                bk = "pb0" if h < 8 else "pb1"
                co = (h % 8) * 64 + (h // 8) * 512
                op("pe", lambda e, h=h, co=co: e.matmul(
                    pb[:, co:co + 64], identb_s[:],
                    xtc[:, h * 64:(h + 1) * 64], start=True, stop=False),
                   reads=("identb", "xtc"), writes=(bk,), tag=f"C{c}-ydx{hh}.{h}")
                op("pe", lambda e, h=h, co=co, hh=hh: e.matmul(
                    pb[:, co:co + 64], lt[:, h * 128:(h + 1) * 128],
                    xt[hh][:, h * 64:(h + 1) * 64], start=False, stop=True),
                   reads=("lt", f"xt{hh}"), writes=(bk,), tag=f"C{c}-ydg{hh}.{h}")
            op("dve", lambda e, c=c, hh=hh: e.tensor_tensor(
                out=xtc[:].rearrange("p (h q) -> p h q", q=64),
                in0=xt[hh][:].rearrange("p (h q) -> p h q", q=64),
                in1=coefl[:, c * 32 + hh * 16:c * 32 + (hh + 1) * 16][:, :, None]
                    .broadcast_to([128, 16, 64]),
                op=OP.mult),
               reads=(f"xt{hh}", "coefl"), writes=("xtc",), tag=f"C{c}-xtc{hh}")
            for j, (bk2, pp2) in enumerate([("pe", pe_), ("pf", pf)]):
                op("pe", lambda e, j=j, par=par, pp2=pp2: e.matmul(
                    pp2[0:64, 0:512], B_tm2[:, par * 64:par * 64 + 64],
                    xtc[:, j * 512:(j + 1) * 512], start=True, stop=True),
                   reads=("Btm", "xtc"), writes=(bk2,), tag=f"C{c}-sc{hh}.{j}")
            op("dve", lambda e, c=c, hh=hh: e.tensor_tensor(
                out=y1[:, hh * 1024:(hh + 1) * 1024].rearrange("p (h q) -> p h q", q=64),
                in0=pa[:].rearrange("p (h q) -> p h q", q=64),
                in1=sdol[:, c * 32 + hh * 16:c * 32 + (hh + 1) * 16][:, :, None]
                    .broadcast_to([128, 16, 64]),
                op=OP.mult),
               reads=("pa0", "pa1", "sdol"), writes=("y1",), tag=f"C{c}-y1{hh}")
            op("dve", lambda e, hh=hh: e.tensor_tensor(
                out=y1[:, hh * 1024:(hh + 1) * 1024],
                in0=y1[:, hh * 1024:(hh + 1) * 1024], in1=pb[:], op=OP.add),
               reads=("pb0", "pb1", "y1"), writes=("y1",), tag=f"C{c}-y2{hh}")
            op("dve", lambda e, c=c, hh=hh: e.tensor_tensor(
                out=srun[hh * 64:(hh + 1) * 64, :].rearrange("p (h q) -> p h q", q=64),
                in0=srun[hh * 64:(hh + 1) * 64, :].rearrange("p (h q) -> p h q", q=64),
                in1=crepl[hh * 64:(hh + 1) * 64,
                          c * 32 + hh * 16:c * 32 + (hh + 1) * 16][:, :, None]
                    .broadcast_to([64, 16, 64]),
                op=OP.mult),
               reads=("srun", "crepl", "srunbf"), writes=("srun",), tag=f"C{c}-sd{hh}")
            op("dve", lambda e, hh=hh: e.tensor_tensor(
                out=srun[hh * 64:(hh + 1) * 64, 0:512],
                in0=srun[hh * 64:(hh + 1) * 64, 0:512], in1=pe_[0:64, 0:512],
                op=OP.add),
               reads=("srun", "pe"), writes=("srun",), tag=f"C{c}-sa{hh}.0")
            op("dve", lambda e, hh=hh: e.tensor_tensor(
                out=srun[hh * 64:(hh + 1) * 64, 512:1024],
                in0=srun[hh * 64:(hh + 1) * 64, 512:1024], in1=pf[0:64, 0:512],
                op=OP.add),
               reads=("srun", "pf"), writes=("srun",), tag=f"C{c}-sa{hh}.1")

        op("dve", lambda e, par=par: e.tensor_tensor(out=y1[:], in0=y1[:],
                                                     in1=sz[par][:], op=OP.mult),
           reads=("y1", f"sz{par}"), writes=("y1",), tag=f"C{c}-gate")
        op("act", lambda e, par=par: e.activation(out=ynorm[:], in_=y1[:], func=AF.Square,
                                         accum_out=ssum[:, par:par + 1]),
           reads=("y1",), writes=("ynorm", f"ssum{par}"), tag=f"C{c}-sq")
        op("act", lambda e, par=par: e.activation(out=lnt[:, par:par + 1],
                                         in_=ssum[:, par:par + 1], func=AF.Ln,
                                         scale=1.0 / DI, bias=cbias[:, 0:1]),
           reads=(f"ssum{par}", "cbias"), writes=(f"lnt{par}",), tag=f"C{c}-ln")
        op("act", lambda e, par=par: e.activation(out=rstd[:, par:par + 1],
                                         in_=lnt[:, par:par + 1], func=AF.Exp,
                                         scale=-0.5),
           reads=(f"lnt{par}",), writes=(f"rstd{par}",), tag=f"C{c}-rs")
        op("dve", lambda e, par=par: e.tensor_scalar(out=ynorm[:], in0=y1[:],
                                            scalar1=rstd[:, par:par + 1], scalar2=None,
                                            op0=OP.mult),
           reads=("y1", f"rstd{par}"), writes=("ynorm",), tag=f"C{c}-yn")
        for i in range(16):
            ptk = "pt0" if i < 8 else "pt1"
            op("pe", lambda e, i=i: e.transpose(
                pt[:, i * 128:(i + 1) * 128], ynorm[:, i * 128:(i + 1) * 128],
                identb_s[:]),
               reads=("ynorm", "identb"), writes=(ptk,), tag=f"C{c}-ty{i}")
        op("act", lambda e: e.copy(out=yt[:, 0:1024], in_=pt[:, 0:1024]),
           reads=("pt0",), writes=("yt",), tag=f"C{c}-evy0")
        op("act", lambda e: e.copy(out=yt[:, 1024:2048], in_=pt[:, 1024:2048]),
           reads=("pt1",), writes=("yt",), tag=f"C{c}-evy1")
        for nh in range(2):
            for kc in range(16):
                op("pe", lambda e, nh=nh, kc=kc: e.matmul(
                    pa[:, nh * 512:(nh + 1) * 512], yt[:, kc * 128:(kc + 1) * 128],
                    woutT[:, kc * 1024 + nh * 512:kc * 1024 + (nh + 1) * 512],
                    start=(kc == 0), stop=(kc == 15)),
                   reads=("yt", WOK[kc // 4]), writes=(f"pa{nh}",), tag=f"C{c}-op{nh}.{kc}")
        op("dve", lambda e, par=par: e.scalar_tensor_tensor(
            out=x1s[:], in0=pa[:], scalar=1.0, in1=u_tm[:], op0=OP.mult, op1=OP.add,
            accum_out=xsum[:, par:par + 1]),
           reads=("pa0", "pa1", "u_tm"), writes=("x1s", f"xsum{par}"), tag=f"C{c}-res")
        op("act", lambda e, par=par: e.activation(out=ynorm[:, 0:1024], in_=x1s[:],
                                         func=AF.Square, accum_out=x2sum[:, par:par + 1]),
           reads=("x1s",), writes=("ynorm", f"x2sum{par}"), tag=f"C{c}-sq1")
        op("dve", lambda e, par=par: e.tensor_scalar(out=mstat[:, par:par + 1],
                                            in0=xsum[:, par:par + 1],
                                            scalar1=1.0 / DM, scalar2=None, op0=OP.mult),
           reads=(f"xsum{par}",), writes=(f"mstat{par}",), tag=f"C{c}-m")
        op("dve", lambda e, par=par: e.tensor_tensor(out=mm_t[:, par:par + 1],
                                            in0=mstat[:, par:par + 1],
                                            in1=mstat[:, par:par + 1], op=OP.mult),
           reads=(f"mstat{par}",), writes=(f"mm_t{par}",), tag=f"C{c}-mm")
        op("dve", lambda e, par=par: e.scalar_tensor_tensor(
            out=vstat[:, par:par + 1], in0=x2sum[:, par:par + 1], scalar=1.0 / DM,
            in1=mm_t[:, par:par + 1], op0=OP.mult, op1=OP.subtract),
           reads=(f"x2sum{par}", f"mm_t{par}"), writes=(f"vstat{par}",), tag=f"C{c}-v")
        op("act", lambda e, par=par: e.activation(out=lnt[:, par:par + 1],
                                         in_=vstat[:, par:par + 1], func=AF.Ln,
                                         bias=cbias[:, 0:1]),
           reads=(f"vstat{par}", "cbias"), writes=(f"lnt{par}",), tag=f"C{c}-lnv")
        op("act", lambda e, par=par: e.activation(out=rstd[:, par:par + 1],
                                         in_=lnt[:, par:par + 1], func=AF.Exp,
                                         scale=-0.5),
           reads=(f"lnt{par}",), writes=(f"rstd{par}",), tag=f"C{c}-rs1")
        op("dve", lambda e, par=par: e.scalar_tensor_tensor(
            out=b2[:, par:par + 1], in0=mstat[:, par:par + 1], scalar=-1.0,
            in1=rstd[:, par:par + 1], op0=OP.mult, op1=OP.mult),
           reads=(f"mstat{par}", f"rstd{par}"), writes=(f"b2{par}",), tag=f"C{c}-b2")
        op("act", lambda e, par=par: e.activation(out=x1n[:], in_=x1s[:], func=AF.Identity,
                                         scale=rstd[:, par:par + 1], bias=b2[:, par:par + 1]),
           reads=("x1s", f"rstd{par}", f"b2{par}"), writes=("x1n",), tag=f"C{c}-x1n")
        op("act", lambda e, c=c: e.dma_start(out=x1_d[c * CH:(c + 1) * CH, :],
                                              in_=x1n[:]),
           reads=("x1n",), writes=("x1d",), dma="x1st", tag=f"C{c}-x1st")

    ufm_load(0)
    inproj_xbc_pair(0)
    inproj_z(0)
    inproj_z(1)
    for jj in range(NCH // 2):
        c0 = 2 * jj
        conv_pair(c0)
        if c0 + 2 < NCH:
            ufm_load(c0 + 2)
            inproj_xbc_pair(c0 + 2)
        chunk_stage(c0)
        if c0 + 2 < NCH:
            inproj_z(c0 + 2)
        chunk_stage(c0 + 1)
        if c0 + 2 < NCH:
            inproj_z(c0 + 3)


def script_D(B, nc, d, V):
    op = B.op
    g = lambda n: V[n]
    wst, trow, trow2 = g("wst"), g("trow"), g("trow2")
    part1, part2 = g("part1"), g("part2")
    scs, chs, cls, ncls = g("scs"), g("chs"), g("cls"), g("ncls")
    wfc_d, wproj_d = d["wfc"], d["wproj"]
    ternfc_d, ternproj_d = d["ternfc"], d["ternproj"]
    items = []
    kctr = [0]
    tctr = [0]

    def emit_load(src_ap, width, tag):
        b = kctr[0] % 4
        kctr[0] += 1
        op("sp", lambda e: e.dma_start(out=wst[b][:, 0:width], in_=src_ap),
           writes=(f"wst{b}",), dma=f"wst{b}", tag=tag)
        return b

    def emit_r1(b, width, p1sl, p1k, tag, eng="dve"):
        op(eng, lambda e: e.tensor_reduce(
            out=part1[:, p1sl],
            in_=wst[b][:, 0:width].rearrange("p (a b) -> p a b", b=32),
            axis=AXX, op=OP.add, apply_absolute_value=True),
           reads=(f"wst{b}",), writes=(p1k,), tag=tag)

    def emit_scale(par, width, tagp):
        sk = f"scs{par}"
        sl_ = slice(par, par + 1)
        op("dve", lambda e: e.tensor_scalar(out=scs[:, sl_], in0=scs[:, sl_],
                                            scalar1=1.0 / width, scalar2=1e-5,
                                            op0=OP.mult, op1=OP.max),
           reads=(sk,), writes=(sk,), tag=f"{tagp}-mc")
        op("dve", lambda e: e.tensor_scalar(out=chs[:, sl_], in0=scs[:, sl_],
                                            scalar1=0.5, scalar2=None, op0=OP.mult),
           reads=(sk,), writes=(f"chs{par}",), tag=f"{tagp}-ch")
        op("dve", lambda e: e.tensor_scalar(out=cls[:, sl_], in0=scs[:, sl_],
                                            scalar1=2.0 ** -25, scalar2=None,
                                            op0=OP.mult),
           reads=(sk,), writes=(f"cls{par}",), tag=f"{tagp}-cl")
        op("dve", lambda e: e.tensor_scalar(out=ncls[:, sl_], in0=cls[:, sl_],
                                            scalar1=-1.0, scalar2=None, op0=OP.mult),
           reads=(f"cls{par}",), writes=(f"ncls{par}",), tag=f"{tagp}-ncl")

    def emit_thresh(bref, par, width, dst_ap, wkey, tagp):
        t = tctr[0] % 2
        tctr[0] += 1
        b = bref[0]
        sk = f"scs{par}"
        sl_ = slice(par, par + 1)
        wk = f"wst{b}"
        tk = f"trow{t}"
        op("dve", lambda e: e.tensor_scalar(out=trow[t][:, 0:width],
                                            in0=wst[b][:, 0:width],
                                            scalar1=chs[:, sl_], scalar2=cls[:, sl_],
                                            op0=OP.subtract, op1=OP.is_gt),
           reads=(wk, f"chs{par}", f"cls{par}"), writes=(tk,), tag=f"{tagp}-tp")
        op("dve", lambda e: e.tensor_scalar(out=trow2[:, 0:width],
                                            in0=wst[b][:, 0:width],
                                            scalar1=chs[:, sl_], scalar2=ncls[:, sl_],
                                            op0=OP.add, op1=OP.is_lt),
           reads=(wk, f"chs{par}", f"ncls{par}"), writes=("trow2",),
           tag=f"{tagp}-tn")
        op("dve", lambda e: e.tensor_tensor(out=trow[t][:, 0:width],
                                            in0=trow[t][:, 0:width],
                                            in1=trow2[:, 0:width],
                                            op=OP.subtract),
           reads=(tk, "trow2"), writes=(tk,), tag=f"{tagp}-sub")
        op("act", lambda e: e.activation(out=trow[t][:, 0:width],
                                         in_=trow[t][:, 0:width],
                                         func=AF.Identity, scale=scs[:, sl_]),
           reads=(tk, sk), writes=(tk,), tag=f"{tagp}-sc")
        op("act", lambda e: e.dma_start(out=dst_ap, in_=trow[t][:, 0:width]),
           reads=(tk,), writes=(wkey,), dma=f"tern{t}", tag=f"{tagp}-st")

    def fc_item(i):
        par = i % 2
        po = par * 128
        p1k = f"part1{par}"
        sl_ = slice(par, par + 1)

        def run():
            b = emit_load(wfc_d[i * 128:(i + 1) * 128, :], 1024, f"ternfc{i}-ld")
            emit_r1(b, 1024, slice(po, po + 32), p1k, f"ternfc{i}-r1")
            op("dve", lambda e: e.tensor_reduce(
                out=scs[:, sl_], in_=part1[:, po:po + 32], axis=AXX, op=OP.add),
               reads=(p1k,), writes=(f"scs{par}",), tag=f"ternfc{i}-r3")
            emit_scale(par, 1024, f"ternfc{i}")
            emit_thresh([b], par, 1024, ternfc_d[i * 128:(i + 1) * 128, :],
                        "ternfc", f"ternfc{i}")
        return run

    def proj_items(i):
        par = i % 2
        po = par * 128
        p1k = f"part1{par}"
        sl_ = slice(par, par + 1)
        bufA = [None]
        bufB = [None]

        def runA():
            bufA[0] = emit_load(wproj_d[i * 128:(i + 1) * 128, 0:2048], 2048,
                                f"ternproj{i}a-ld")
            emit_r1(bufA[0], 2048, slice(po, po + 64), p1k, f"ternproj{i}a-r1")

        def runB():
            bufB[0] = emit_load(wproj_d[i * 128:(i + 1) * 128, 2048:4096], 2048,
                                f"ternproj{i}b-ld")
            emit_r1(bufB[0], 2048, slice(po + 64, po + 128), p1k,
                    f"ternproj{i}b-r1")
            op("dve", lambda e: e.tensor_reduce(
                out=part2[:, par * 4:par * 4 + 4],
                in_=part1[:, po:po + 128].rearrange("p (a b) -> p a b", b=32),
                axis=AXX, op=OP.add),
               reads=(p1k,), writes=(f"part2{par}",), tag=f"ternproj{i}-r2")
            op("dve", lambda e: e.tensor_reduce(
                out=scs[:, sl_], in_=part2[:, par * 4:par * 4 + 4],
                axis=AXX, op=OP.add),
               reads=(f"part2{par}",), writes=(f"scs{par}",), tag=f"ternproj{i}-r3")
            emit_scale(par, 4096, f"ternproj{i}")
            emit_thresh(bufA, par, 2048,
                        ternproj_d[i * 128:(i + 1) * 128, 0:2048],
                        "ternproj", f"ternproj{i}a")

        def runC():
            emit_thresh(bufB, par, 2048,
                        ternproj_d[i * 128:(i + 1) * 128, 2048:4096],
                        "ternproj", f"ternproj{i}b")
        return [runA, runB, runC]

    for i in range(32):
        items.append(fc_item(i))
    for i in range(8):
        items.extend(proj_items(i))
    return items


def script_E(B, nc, d, V, S, SG, NSG):
    op = B.op
    g = lambda n: V[n]
    (pa, pb, pt, pe_, pf) = (g("pa"), g("pb"), g("pt"), g("pe_"), g("pf"))
    wfcT, wprT = g("wfcT"), g("wprT")
    x1tm, x1bf, x1fm, r2, hr = g("x1tm"), g("x1bf"), g("x1fm"), g("r2"), g("hr")
    t1b, t2b, vtb, fbtb, uTg = g("t1b"), g("t2b"), g("vtb"), g("fbtb"), g("uTg")
    outst = g("outst")
    identb_s, gate_s = g("identb_s"), g("gate_s")
    xsum, x2sum = g("xsum"), g("x2sum")
    mstat, lnt, mm_t, vstat, rstd, b2 = (g("mstat"), g("lnt"), g("mm_t"),
                                          g("vstat"), g("rstd"), g("b2"))
    cbias = g("cbias")
    x1_d, out_d = d["x1d"], d["out"]
    ternfc_d, ternproj_d = d["ternfc"], d["ternproj"]
    vT_d, fbT_d, uTlr_d, gater_d = d["vT"], d["fbT"], d["uTlr"], d["gater"]
    NT = SG // 128
    PABANK = [("pa0", pa, 0, 0), ("pa1", pa, 512, 0), ("pe", pe_, 0, 1), ("pf", pf, 0, 1)]

    def x1block(sg):
        for tg in range(NT):
            op("sp", lambda e, sg=sg, tg=tg: e.dma_start(
                out=x1tm[:, tg * 1024:(tg + 1) * 1024],
                in_=x1_d[sg * SG + tg * 128:sg * SG + (tg + 1) * 128, :]),
               writes=(f"x1tm{tg}",), dma=f"x1ld{tg % 2}", tag=f"E{sg}-x1ld{tg}")
            op("dve", lambda e, tg=tg: e.tensor_copy(
                out=x1bf[:], in_=x1tm[:, tg * 1024:(tg + 1) * 1024]),
               reads=(f"x1tm{tg}",), writes=("x1bf",), tag=f"E{sg}-cast{tg}")
            for i in range(8):
                op("pe", lambda e, i=i: e.transpose(
                    pt[:, i * 128:(i + 1) * 128], x1bf[:, i * 128:(i + 1) * 128],
                    identb_s[:]),
                   reads=("x1bf", "identb"), writes=("pt0",), tag=f"E{sg}-t{tg}.{i}")
            op("dve", lambda e, tg=tg: e.tensor_copy(
                out=x1fm[:].rearrange("p (kc t) -> p kc t", t=SG)
                    [:, :, tg * 128:(tg + 1) * 128],
                in_=pt[:, 0:1024].rearrange("p (kc t) -> p kc t", t=128)),
               reads=("pt0",), writes=("x1fm",), tag=f"E{sg}-evt{tg}")

    x1block(0)
    for k in range(8):
        op("sp", lambda e, k=k: e.dma_start(
            out=wfcT[:, k * 4096:(k + 1) * 4096],
            in_=ternfc_d[0:4096, k * 128:(k + 1) * 128], transpose=True),
           reads=("ternfc",), writes=(f"wfcT{k}",), dma=f"wef{k%8}", tag=f"E-wfc{k}")
    for k in range(32):
        op("sp", lambda e, k=k: e.dma_start(
            out=wprT[:, k * 1024:(k + 1) * 1024],
            in_=ternproj_d[0:1024, k * 128:(k + 1) * 128], transpose=True),
           reads=("ternproj",), writes=(f"wprT{k//8}",), dma=f"wef{k%8}", tag=f"E-wpr{k}")
    op("pool", lambda e: e.dma_start(
        out=vtb[:].rearrange("p (kc j) -> p kc j", j=16),
        in_=vT_d.rearrange("(kc p) j -> p kc j", p=128)),
       writes=("vtb",), dma="lr", tag="E-vtb")
    op("pool", lambda e: e.dma_start(out=fbtb[:], in_=fbT_d),
       writes=("fbtb",), dma="lr", tag="E-fbtb")
    op("pool", lambda e: e.dma_start(out=uTg[:], in_=uTlr_d),
       writes=("uTg",), dma="lr", tag="E-uTg")
    op("sp", lambda e: e.dma_start(out=gate_s[:], in_=gater_d),
       writes=("gate",), dma="cst", tag="E-gate")
    op("dve", lambda e: e.tensor_scalar(out=uTg[:], in0=uTg[:],
                                        scalar1=gate_s[:, 0:1], scalar2=None,
                                        op0=OP.mult),
       reads=("uTg", "gate"), writes=("uTg",), tag="E-uTgs")

    for sg in range(NSG):
        if sg > 0:
            x1block(sg)
        for kc in range(8):
            op("pe", lambda e, kc=kc: e.matmul(
                pa[0:16, 0:SG], vtb[:, kc * 16:(kc + 1) * 16],
                x1fm[:, kc * SG:(kc + 1) * SG], start=(kc == 0), stop=(kc == 7)),
               reads=("vtb", "x1fm"), writes=("pa0",), tag=f"E{sg}-t1.{kc}")
        op("act", lambda e: e.copy(out=t1b[:], in_=pa[0:16, 0:SG]),
           reads=("pa0",), writes=("t1b",), tag=f"E{sg}-evt1")
        op("pe", lambda e: e.matmul(pa[0:16, 0:SG], fbtb[:], t1b[:],
                                    start=True, stop=True),
           reads=("fbtb", "t1b"), writes=("pa0",), tag=f"E{sg}-t2")
        op("act", lambda e: e.copy(out=t2b[:], in_=pa[0:16, 0:SG]),
           reads=("pa0",), writes=("t2b",), tag=f"E{sg}-evt2")
        for fg in range(32):
            bk, pp, co, _ = PABANK[fg % 4]
            for kc in range(8):
                op("pe", lambda e, fg=fg, kc=kc, pp=pp, co=co: e.matmul(
                    pp[:, co:co + SG] if SG == 512 else pp[:, co:co + SG],
                    wfcT[:, kc * 4096 + fg * 128:kc * 4096 + (fg + 1) * 128],
                    x1fm[:, kc * SG:(kc + 1) * SG],
                    start=(kc == 0), stop=False),
                   reads=(f"wfcT{kc}", "x1fm"), writes=(bk,), tag=f"E{sg}-fc{fg}.{kc}")
            op("pe", lambda e, fg=fg, pp=pp, co=co: e.matmul(
                pp[:, co:co + SG], uTg[:, fg * 128:(fg + 1) * 128], t2b[:],
                start=False, stop=True),
               reads=("uTg", "t2b"), writes=(bk,), tag=f"E{sg}-lr{fg}")
            op("act", lambda e, fg=fg, pp=pp, co=co: e.activation(
                out=hr[fg % 2][:, 0:SG], in_=pp[:, co:co + SG], func=AF.Relu),
               reads=(bk,), writes=(f"hr{fg%2}",), tag=f"E{sg}-rl{fg}")
            op("dve", lambda e, fg=fg: e.tensor_tensor(
                out=r2[:, fg * SG:(fg + 1) * SG], in0=hr[fg % 2][:, 0:SG],
                in1=hr[fg % 2][:, 0:SG], op=OP.mult),
               reads=(f"hr{fg%2}",), writes=("r2",), tag=f"E{sg}-r2{fg}")
        for tg in range(NT):
            for nh in range(2):
                for kc in range(32):
                    op("pe", lambda e, tg=tg, nh=nh, kc=kc: e.matmul(
                        pb[:, nh * 512:(nh + 1) * 512],
                        r2[:, kc * SG + tg * 128:kc * SG + (tg + 1) * 128],
                        wprT[:, kc * 1024 + nh * 512:kc * 1024 + (nh + 1) * 512],
                        start=(kc == 0), stop=(kc == 31)),
                       reads=("r2", f"wprT{kc//8}"), writes=(f"pb{nh}",),
                       tag=f"E{sg}-pj{tg}.{nh}.{kc}")
            pq = tg % 2
            op("dve", lambda e, tg=tg, pq=pq: e.scalar_tensor_tensor(
                out=outst[:], in0=pb[:], scalar=1.0,
                in1=x1tm[:, tg * 1024:(tg + 1) * 1024], op0=OP.mult, op1=OP.add,
                accum_out=xsum[:, pq:pq + 1]),
               reads=("pb0", "pb1", f"x1tm{tg}"), writes=("outst", f"xsum{pq}"),
               tag=f"E{sg}-res{tg}")
            op("act", lambda e, pq=pq: e.activation(out=x1bf[:], in_=outst[:],
                                             func=AF.Square, accum_out=x2sum[:, pq:pq + 1]),
               reads=("outst",), writes=("x1bf", f"x2sum{pq}"), tag=f"E{sg}-sq{tg}")
            op("dve", lambda e, pq=pq: e.tensor_scalar(out=mstat[:, pq:pq + 1],
                                                in0=xsum[:, pq:pq + 1],
                                                scalar1=1.0 / DM, scalar2=None,
                                                op0=OP.mult),
               reads=(f"xsum{pq}",), writes=(f"mstat{pq}",), tag=f"E{sg}-m{tg}")
            op("dve", lambda e, pq=pq: e.tensor_tensor(out=mm_t[:, pq:pq + 1],
                                                in0=mstat[:, pq:pq + 1],
                                                in1=mstat[:, pq:pq + 1], op=OP.mult),
               reads=(f"mstat{pq}",), writes=(f"mm_t{pq}",), tag=f"E{sg}-mm{tg}")
            op("dve", lambda e, pq=pq: e.scalar_tensor_tensor(
                out=vstat[:, pq:pq + 1], in0=x2sum[:, pq:pq + 1], scalar=1.0 / DM,
                in1=mm_t[:, pq:pq + 1], op0=OP.mult, op1=OP.subtract),
               reads=(f"x2sum{pq}", f"mm_t{pq}"), writes=(f"vstat{pq}",), tag=f"E{sg}-v{tg}")
            op("act", lambda e, pq=pq: e.activation(out=lnt[:, pq:pq + 1],
                                             in_=vstat[:, pq:pq + 1], func=AF.Ln,
                                             bias=cbias[:, 0:1]),
               reads=(f"vstat{pq}", "cbias"), writes=(f"lnt{pq}",), tag=f"E{sg}-lnv{tg}")
            op("act", lambda e, pq=pq: e.activation(out=rstd[:, pq:pq + 1],
                                             in_=lnt[:, pq:pq + 1], func=AF.Exp,
                                             scale=-0.5),
               reads=(f"lnt{pq}",), writes=(f"rstd{pq}",), tag=f"E{sg}-rs{tg}")
            op("dve", lambda e, pq=pq: e.scalar_tensor_tensor(
                out=b2[:, pq:pq + 1], in0=mstat[:, pq:pq + 1], scalar=-1.0,
                in1=rstd[:, pq:pq + 1], op0=OP.mult, op1=OP.mult),
               reads=(f"mstat{pq}", f"rstd{pq}"), writes=(f"b2{pq}",), tag=f"E{sg}-b2{tg}")
            op("act", lambda e, tg=tg, pq=pq: e.activation(
                out=outst[:], in_=outst[:], func=AF.Identity,
                scale=rstd[:, pq:pq + 1], bias=b2[:, pq:pq + 1]),
               reads=("outst", f"rstd{pq}", f"b2{pq}"), writes=("outst",),
               tag=f"E{sg}-app{tg}")
            op("act", lambda e, sg=sg, tg=tg: e.dma_start(
                out=out_d[sg * SG + tg * 128:sg * SG + (tg + 1) * 128, :],
                in_=outst[:]),
               reads=("outst",), dma="outst", tag=f"E{sg}-st{tg}")


def prep_in_map(inp, b, S=2048):
    f32 = np.float32
    u = np.ascontiguousarray(inp["L0_attn_u"][b], dtype=f32)[:S]
    W_in = np.asarray(inp["L0_attn_W_in"], f32)
    conv_w = np.asarray(inp["L0_attn_conv_w"], f32)
    conv_b = np.asarray(inp["L0_attn_conv_b"], f32)
    dt_bias = np.asarray(inp["L0_attn_dt_bias"], f32)
    A_log = np.asarray(inp["L0_attn_A_log"], f32)
    Dp = np.asarray(inp["L0_attn_D"], f32)
    norm_w = np.asarray(inp["L0_attn_norm_w"], f32)
    W_out = np.asarray(inp["L0_attn_W_out"], f32)
    fast_b = np.asarray(inp["L0_ffn_fast_b"], f32)
    w_fc = np.asarray(inp["L0_ffn_w_fc"], f32)
    w_proj = np.asarray(inp["L0_ffn_w_proj"], f32)
    u_lr = np.asarray(inp["L0_ffn_u"], f32)
    v_lr = np.asarray(inp["L0_ffn_v"], f32)
    gate = np.asarray(inp["L0_ffn_gate"], f32)
    m = {
        "u_tm": u,
        "u_fmT": np.ascontiguousarray(u.T),
        "winT": np.ascontiguousarray(W_in.T),
        "woutT": np.ascontiguousarray(W_out.T),
        "convw": np.ascontiguousarray(
            conv_w[:, 0, :].reshape(NFC, 128, 4).transpose(1, 0, 2).reshape(128, NFC * 4)),
        "convb": np.ascontiguousarray(conv_b.reshape(NFC, 128).T),
        "dtb": np.ascontiguousarray(np.tile(dt_bias, 4)[:, None]),
        "alog": np.ascontiguousarray(np.tile(A_log, 4)[:, None]),
        "nwpk": np.ascontiguousarray(norm_w.reshape(16, 128).T),
        "dflat": np.ascontiguousarray(np.repeat(Dp, HD)[None, :]),
        "identf": np.eye(128, dtype=f32),
        "triu": np.triu(np.ones((128, 128), f32)),
        "maskT": np.triu(np.ones((128, 128), f32)),
        "onesf": np.ones((1, 128), f32),
        "wfc": w_fc,
        "wproj": w_proj,
        "vT": np.ascontiguousarray(v_lr.T),
        "fbT": np.ascontiguousarray(fast_b.T),
        "uTlr": np.ascontiguousarray(u_lr.T),
        "gater": np.full((16, 1), gate, f32),
    }
    return m


def kernel(**inputs):
    S = inputs["L0_attn_u"].shape[1]
    nb = inputs["L0_attn_u"].shape[0]
    nc = build(S=S)
    in_maps = [prep_in_map(inputs, b, S) for b in range(nb)]
    res = run_bass_kernel_spmd(nc, in_maps, core_ids=list(range(nb)))
    return np.stack([np.asarray(r["out"], dtype=np.float32) for r in res.results])


# version 25 (final)
# speedup vs baseline: 1.0695x; 1.0149x over previous
import numpy as np
import ml_dtypes
import concourse.bass as bass
import concourse.mybir as mybir
from concourse.bass_utils import run_bass_kernel_spmd
from concourse.alu_op_type import AluOpType as OP

F32 = mybir.dt.float32
BF16 = mybir.dt.bfloat16
AF = mybir.ActivationFunctionType
AXX = mybir.AxisListType.X

DM = 1024
DI = 2048
DS = 64
NH = 32
HD = 64
CDIM = 2176
CH = 128
NFC = 17
EPS = 1e-5

ENGS = ("pe", "act", "dve", "pool", "sp")
ENG_ATTR = {"pe": "tensor", "act": "scalar", "dve": "vector", "pool": "gpsimd", "sp": "sync"}


class Builder:
    def __init__(self):
        self.ops = []
        self.eng_n = {e: 0 for e in ENGS}
        self.dma_n = {}
        self.last_w = {}
        self.readers = {}

    def _dep_waits(self, eng, reads, writes):
        waits = []
        def need(tag):
            if tag is None:
                return
            if tag[0] == "e":
                if tag[1] != eng or eng != "pe":
                    waits.append(tag)
            else:
                waits.append(tag)
        for k in reads:
            need(self.last_w.get(k))
        for k in writes:
            need(self.last_w.get(k))
            for t in self.readers.get(k, ()):
                need(t)
        return waits

    def op(self, eng, fn, reads=(), writes=(), dma=None, tag=""):
        reads = list(reads); writes = list(writes)
        waits = self._dep_waits(eng, reads, writes)
        if dma is None:
            self.eng_n[eng] += 1
            mytag = ("e", eng, self.eng_n[eng])
        else:
            k = self.dma_n.get(dma, 0)
            if k > 0:
                waits.append(("d", dma, k))
            self.dma_n[dma] = k + 1
            mytag = ("d", dma, k + 1)
        for k2 in reads:
            self.readers.setdefault(k2, []).append(mytag)
        for k2 in writes:
            self.last_w[k2] = mytag
            self.readers[k2] = []
        self.ops.append(dict(eng=eng, fn=fn, waits=waits, tag=mytag, dbg=tag))
        return mytag

    def barrier(self, tag="bar"):
        marks = {}
        for e in ENGS:
            fn, rd, wr = self.nopfns[e]
            marks[e] = self.op(e, fn, reads=rd, writes=wr, tag=f"{tag}-mark")
        dma_marks = [("d", q, self.dma_n[q]) for q in self.dma_n if self.dma_n[q] > 0]
        marks2 = {}
        for e in ENGS:
            fn, rd, wr = self.nopfns[e]
            marks2[e] = self.op(e, fn, reads=rd, writes=wr, tag=f"{tag}-wait")
            self.ops[-1]["waits"].extend(
                [marks[o] for o in ENGS if o != e] + dma_marks)
        for k in list(self.last_w):
            t = self.last_w[k]
            if t[0] == "e":
                self.last_w[k] = marks2[t[1]]
        self.readers = {k: [] for k in self.readers}

    def emit(self, nc, engines, esems, dsems):
        awaited = {e: set() for e in ENGS}
        for o in self.ops:
            for t in o["waits"]:
                if t[0] == "e":
                    awaited[t[1]].add(t[2])
        rank = {}
        inc_set = {}
        for e in ENGS:
            s = sorted(awaited[e])
            inc_set[e] = set(s)
            rank[e] = {idx: i + 1 for i, idx in enumerate(s)}
        self.final_counts = {e: len(awaited[e]) for e in ENGS}

        streams = {e: [o for o in self.ops if o["eng"] == e] for e in ENGS}
        wm = {}

        def emit_stream(e, eng_h):
            for o in streams[e]:
                for t in o["waits"]:
                    if t[0] == "e":
                        if t[2] not in rank[t[1]]:
                            continue
                        v = rank[t[1]][t[2]]
                        kk = (e, "e" + t[1])
                        if wm.get(kk, 0) >= v:
                            continue
                        wm[kk] = v
                        eng_h.wait_ge(esems[t[1]], v)
                    else:
                        v = 16 * t[2]
                        kk = (e, "d" + t[1])
                        if wm.get(kk, 0) >= v:
                            continue
                        wm[kk] = v
                        eng_h.wait_ge(dsems[t[1]], v)
                instr = o["fn"](eng_h)
                t = o["tag"]
                if t[0] == "d":
                    instr.then_inc(dsems[t[1]], 16)
                elif t[2] in inc_set[t[1]]:
                    instr.then_inc(esems[t[1]], 1)
        return emit_stream


class Arena:
    def __init__(self, nc, budget_bytes=206000):
        self.nc = nc
        self.budget = budget_bytes
        self.tiles = {}
        self.handle = None
        self.regions = {}
        self.pending = []

    def tile(self, name, shape, dtype, offset):
        per_part = int(np.prod(shape[1:])) * mybir.dt.size(dtype)
        assert offset % 32 == 0, (name, offset)
        assert offset + per_part <= self.budget, (name, offset, per_part, self.budget)
        h = self.nc.virtual_sbuf_tensor(name, list(shape), dtype)
        self.pending.append((h, offset))
        self.tiles[name] = (h, offset, per_part)
        return h

    def allocate(self):
        self.handle = self.nc.sbuf_tensor("arena", [128, self.budget], mybir.dt.uint8)
        self.handle_cm = self.handle
        h = self.handle.__enter__()
        base = self.nc.lookup_mloc(h).addr
        for hnd, off in self.pending:
            self.nc._tensor(hnd.name, hnd.shape, hnd.dtype, type="SB")
            mloc = self.nc.lookup_mloc(hnd)
            mloc.allocated = True
            mloc.addr = base + off
        return h

    def free(self):
        self.handle_cm.__exit__(None, None, None)


def ceil32(x):
    return (x + 31) // 32 * 32


def build(S=2048, budget=212832):
    NCH = S // CH
    SG = min(512, S)
    NSG = S // SG
    nc = bass.Bass("TRN2", target_bir_lowering=False)

    d = {}
    def di(name, shape, dtype=F32, kind="ExternalInput"):
        d[name] = nc.dram_tensor(name, list(shape), dtype, kind=kind).ap()
        return d[name]
    u_tm_d = di("u_tm", [S, DM])
    u_fmT_d = di("u_fmT", [DM, S])
    winT_d = di("winT", [DM, 4256])
    woutT_d = di("woutT", [DI, DM])
    convw_d = di("convw", [128, NFC * 4])
    convb_d = di("convb", [128, NFC])
    dtb_d = di("dtb", [128, 1])
    alog_d = di("alog", [128, 1])
    nwpk_d = di("nwpk", [128, 16])
    dflat_d = di("dflat", [1, DI])
    identf_d = di("identf", [128, 128])
    triu_d = di("triu", [128, 128])
    maskT_d = di("maskT", [128, 128])
    onesf_d = di("onesf", [1, 128])
    wfc_d = di("wfc", [4096, DM])
    wproj_d = di("wproj", [DM, 4096])
    vT_d = di("vT", [DM, 16])
    fbT_d = di("fbT", [16, 16])
    uTlr_d = di("uTlr", [16, 4096])
    gater_d = di("gater", [16, 1])
    out_d = di("out", [S, DM], F32, "ExternalOutput")
    aflat_d = di("acumflat", [NCH, NH * CH], F32, "Internal")
    x1_d = di("x1d", [S, DM], F32, "Internal")
    ternfc_d = di("ternfc", [4096, DM], BF16, "Internal")
    ternproj_d = di("ternproj", [DM, 4096], BF16, "Internal")
    cdf_d = di("cdf", [8, 128], F32, "Internal")

    PERS = 4096
    ar = Arena(nc, budget)
    off = [0]
    def T(name, shape, dtype, at=None):
        if at is not None:
            off[0] = at
        t = ar.tile(name, shape, dtype, off[0])
        off[0] = ceil32(off[0] + int(np.prod(shape[1:])) * mybir.dt.size(dtype))
        return t
    off[0] = budget - PERS
    identf_s = T("identf_s", [128, 128], F32)
    identb_s = T("identb_s", [128, 128], BF16)
    triu_s = T("triu_s", [128, 128], F32)
    maskT_s = T("maskT_s", [128, 128], F32)
    convw_s = T("convw_s", [128, NFC * 4], F32)
    convb_s = T("convb_s", [128, NFC], F32)
    dtb_s = T("dtb_s", [128, 1], F32)
    alog_s = T("alog_s", [128, 1], F32)
    nwpk_s = T("nwpk_s", [128, 16], F32)
    onesf_s = T("onesf_s", [1, 128], F32)
    dumact = T("dumact", [1, 8], F32)
    dumdve = T("dumdve", [1, 8], F32)
    dumpool = T("dumpool", [1, 8], F32)
    cbias = T("cbias", [128, 4], F32)
    ssum = T("ssum", [128, 2], F32)
    xsum = T("xsum", [128, 2], F32)
    x2sum = T("x2sum", [128, 2], F32)
    mstat = T("mstat", [128, 2], F32)
    lnt = T("lnt", [128, 2], F32)
    mm_t = T("mm_t", [128, 2], F32)
    vstat = T("vstat", [128, 2], F32)
    rstd = T("rstd", [128, 2], F32)
    b2 = T("b2", [128, 2], F32)
    nalog = T("nalog", [128, 1], F32)
    scs = T("scs", [128, 2], F32)
    rcs = T("rcs", [128, 1], F32)
    hscs = T("hscs", [128, 1], F32)
    gate_s = T("gate_s", [16, 1], F32)
    assert off[0] <= budget, off[0]
    PERS_USED = off[0] - (budget - PERS)

    off[0] = 0
    winT = T("winT_s", [128, 8 * 4256], BF16)
    woutT = T("woutT_s", [128, 16 * 1024], BF16)
    u_tm = T("u_tm_s", [128, DM], F32)
    u_fm = T("u_fm_s", [128, 8 * 256], BF16)
    sz = [T("sz0", [128, DI], BF16), T("sz1", [128, DI], BF16)]
    xbcraw = T("xbcraw", [128, NFC * 260], BF16)
    convtmp = [T(f"convtmp{i}", [128, 512], F32) for i in range(2)]
    xact_pair = T("xact_pair", [128, 16 * 256], BF16)
    B_s2 = T("B_s2", [64, 256], BF16)
    C_s2 = T("C_s2", [64, 256], BF16)
    B_tm2 = T("B_tm2", [128, 128], BF16)
    subl = T("subl", [128, NCH * 32], F32)
    sdol = T("sdol", [128, NCH * 32], F32)
    coefl = T("coefl", [128, NCH * 32], F32)
    crepl = T("crepl", [128, NCH * 32], F32)
    ddb = T("ddb_t", [128, 16 * CH], F32)
    lt = T("lt", [128, 16 * CH], BF16)
    gtm = T("gtm", [128, 128], BF16)
    xt = [T("xt0", [128, 16 * 64], BF16), T("xt1", [128, 16 * 64], BF16)]
    xtc = [T("xtc0", [128, 16 * 64], BF16), T("xtc1", [128, 16 * 64], BF16)]
    srun_bf = T("srun_bf", [64, DI], BF16)
    srun = T("srun", [128, 1024], F32)
    y1 = T("y1", [128, DI], F32)
    yt = T("yt", [128, DI], BF16)
    drep = T("drep", [128, DI], BF16)
    x1s = T("x1s", [128, DM], F32)
    x1n = T("x1n", [128, DM], F32)
    CEND = off[0]
    cdrow8 = T("cdrow8", [1, 128], F32, at=budget - PERS + ceil32(PERS_USED))
    ynorm = T("ynorm", [128, DI], BF16, at=ar.tiles["xt0"][1])
    acum_pk = T("acum_pk", [128, 1024], F32, at=ar.tiles["u_tm_s"][1])
    subt_pk = T("subt_pk", [128, 1024], F32, at=ar.tiles["sz0"][1])
    coeft_pk = T("coeft_pk", [128, 1024], F32, at=ar.tiles["drep"][1])
    adlh = T("adlh", [128, 256], F32, at=ar.tiles["convtmp0"][1])
    wdt_s = T("wdt_s", [128, 8 * 32], BF16, at=ar.tiles["convtmp1"][1])
    y1_off = ar.tiles["y1"][1]
    dt_pk = T("dt_pk", [128, 1024], F32, at=y1_off)
    lndt_pk = T("lndt_pk", [128, 1024], F32)
    assert off[0] <= ar.tiles["yt"][1], (off[0], ar.tiles["yt"][1])
    ad_pk = T("ad_pk", [128, 1024], F32, at=ar.tiles["srun"][1])
    assert CEND <= budget - PERS, (CEND, budget - PERS)

    off[0] = 0
    wfcT = T("wfcT", [128, 8 * 4096], BF16, at=0)
    wprT = T("wprT", [128, 32 * 1024], BF16)
    EW = off[0]
    x1tm = T("x1tm", [128, 4 * 1024], F32)
    x1bf = T("x1bf", [128, 1024], BF16)
    x1fm = T("x1fm", [128, 8 * SG], BF16)
    r2 = T("r2", [128, 32 * SG], BF16)
    hr = [T(f"hr{i}", [128, SG], BF16) for i in range(2)]
    t1b = T("t1b", [16, SG], BF16)
    t2b = T("t2b", [16, SG], BF16)
    vtb = T("vtb", [128, 8 * 16], BF16)
    fbtb = T("fbtb", [16, 16], BF16)
    uTg = T("uTg", [16, 4096], BF16)
    outst = T("outst", [128, DM], F32)
    EEND = off[0]
    assert EEND <= budget - PERS, EEND
    wst = [T("wst0", [128, 2048], F32, at=ar.tiles["ddb_t"][1]),
           T("wst1", [128, 2048], F32, at=ar.tiles["lt"][1]),
           T("wst2", [128, 2048], F32, at=ar.tiles["xbcraw"][1]),
           T("wst3", [128, 2048], F32, at=ar.tiles["xact_pair"][1])]
    trow = [T("trow0", [128, 2048], BF16, at=ar.tiles["srun_bf"][1]),
            T("trow1", [128, 2048], BF16, at=ar.tiles["x1s"][1])]
    trow2 = T("trow2", [128, 2048], BF16, at=ar.tiles["x1n"][1])
    part1 = T("part1", [128, 256], F32, at=ar.tiles["yt"][1])
    part2 = T("part2", [128, 8], F32)
    chs = T("chs", [128, 2], F32)
    cls = T("cls", [128, 2], F32)
    ncls = T("ncls", [128, 2], F32)
    assert off[0] <= ar.tiles["yt"][1] + 4096

    ar.allocate()

    B = Builder()
    import contextlib
    es = contextlib.ExitStack()
    with es:
        pa = es.enter_context(nc.psum_tensor("pa", [128, 1024], F32))
        pb = es.enter_context(nc.psum_tensor("pb", [128, 1024], F32))
        pt = es.enter_context(nc.psum_tensor("pt", [128, 2048], BF16))
        pe_ = es.enter_context(nc.psum_tensor("pe_", [128, 512], F32))
        pf = es.enter_context(nc.psum_tensor("pf", [128, 512], F32))

        B.nopfns = {
            "pe": (lambda e: e.matmul(pa[0:1, 0:8], identb_s[0:1, 0:1],
                                      identb_s[0:1, 0:8], start=True, stop=True),
                   ("identb",), ("pa0",)),
            "act": (lambda e: e.memzero(dumact[:]), ("dumact",), ("dumact",)),
            "dve": (lambda e: e.memset(dumdve[:], 0.0), ("dumdve",), ("dumdve",)),
            "pool": (lambda e: e.memset(dumpool[:], 0.0), ("dumpool",), ("dumpool",)),
            "sp": (lambda e: e.nop(), (), ()),
        }

        op = B.op

        op("pool", lambda e: e.dma_start(
            out=wdt_s[:].rearrange("p (kc c) -> p kc c", c=32),
            in_=winT_d[:, 4224:4256].rearrange("(kc p) c -> p kc c", p=128)),
           writes=("wdt",), dma="wdt", tag="ld-wdt")
        for nm, dst in [("identf", identf_s), ("triu", triu_s), ("maskT", maskT_s),
                        ("convw", convw_s), ("convb", convb_s), ("dtb", dtb_s),
                        ("alog", alog_s), ("nwpk", nwpk_s), ("onesf", onesf_s)]:
            op("sp", lambda e, a=d[nm], b=dst: e.dma_start(out=b[:], in_=a),
               writes=(nm,), dma="cst", tag=f"ld-{nm}")
        op("pool", lambda e: e.dma_start(out=identb_s[:], in_=d["identf"]),
           writes=("identb",), dma="cst2", tag="ld-identb")
        def emit_bigw():
            for kc in range(8):
                op("pool", lambda e, kc=kc: e.dma_start(
                    out=winT[:, kc * 4256:(kc + 1) * 4256],
                    in_=winT_d.rearrange("(kc p) c -> p kc c", p=128)[:, kc]),
                   writes=(f"winT{kc}",), dma="winT", tag=f"ld-winT{kc}")
            for q in range(4):
                op("pool", lambda e, q=q: e.dma_start(
                    out=woutT[:, q * 4096:(q + 1) * 4096],
                    in_=woutT_d.rearrange("(kc p) c -> p kc c", p=128)
                        [:, 4 * q:4 * (q + 1)]),
                   writes=(f"woutT{q}",), dma="woutT", tag=f"ld-woutT{q}")
        op("act", lambda e: e.activation(out=nalog[:], in_=alog_s[:], func=AF.Exp),
           reads=("alog",), writes=("nalog",), tag="expA")
        op("pool", lambda e: e.tensor_scalar(out=nalog[:], in0=nalog[:], scalar1=-1.0,
                                            scalar2=None, op0=OP.mult),
           writes=("nalog",), tag="negA")
        op("pool", lambda e: e.memset(cbias[:, 0:1], EPS), writes=("cbias",), tag="cb0")
        op("pool", lambda e: e.memset(cbias[:, 1:2], 1.0), writes=("cbias",), tag="cb1")
        op("pool", lambda e: e.memset(cbias[:, 2:3], -0.5), writes=("cbias",), tag="cb2")
        op("pool", lambda e: e.memset(cbias[:, 3:4], 0.5), writes=("cbias",), tag="cb3")

        d_items = script_D(B, nc, d, locals())
        di = [0]

        def emitD(n):
            while n > 0 and di[0] < len(d_items):
                d_items[di[0]]()
                di[0] += 1
                n -= 1

        NPAIR = NCH // 2
        op("pool", lambda e: e.memset(dt_pk[:], 0.0), writes=("dt_pk",), tag="z-dtpk")
        for c in range(NCH):
            bse = 64 * (c % 2)
            col = 128 * (c // 2)
            cp = c % 2
            pp = pe_ if cp == 0 else pf
            bk = "pe" if cp == 0 else "pf"
            op("pool", lambda e, c=c: e.dma_start(
                out=u_fm[:, 0:1024].rearrange("p (kc t) -> p kc t", t=128),
                in_=u_fmT_d[:, c * CH:(c + 1) * CH].rearrange("(kc p) t -> p kc t", p=128)),
               writes=("u_fm",), dma=f"ufm{cp}", tag=f"B-ufm{c}")
            for kc in range(8):
                op("pe", lambda e, kc=kc, pp=pp: e.matmul(
                    pp[0:32, 0:128], wdt_s[:, kc * 32:(kc + 1) * 32],
                    u_fm[:, kc * 128:(kc + 1) * 128],
                    start=(kc == 0), stop=(kc == 7)),
                   reads=("wdt", "u_fm"), writes=(bk,), tag=f"B-dt{c}.{kc}")
            op("act", lambda e, bse=bse, col=col, pp=pp: e.copy(
                out=dt_pk[bse:bse + 32, col:col + 128], in_=pp[0:32, 0:128]),
               reads=(bk,), writes=("dt_pk",), tag=f"B-ev{c}")
            if c == 1:
                emit_bigw()
            emitD(2)
        op("act", lambda e: e.activation(out=lndt_pk[:], in_=dt_pk[:], func=AF.Relu,
                                         bias=dtb_s[:, 0:1], scale=1.0),
           reads=("dt_pk", "dtb"), writes=("lndt_pk",), tag="sp-relu")
        op("act", lambda e: e.activation(out=ad_pk[:], in_=dt_pk[:], func=AF.Abs,
                                         bias=dtb_s[:, 0:1], scale=1.0),
           reads=("dt_pk", "dtb"), writes=("ad_pk",), tag="sp-abs")
        op("act", lambda e: e.activation(out=ad_pk[:], in_=ad_pk[:], func=AF.Exp,
                                         scale=-1.0),
           reads=("ad_pk",), writes=("ad_pk",), tag="sp-exp")
        op("act", lambda e: e.activation(out=ad_pk[:], in_=ad_pk[:], func=AF.Ln,
                                         bias=cbias[:, 1:2], scale=1.0),
           reads=("ad_pk", "cbias"), writes=("ad_pk",), tag="sp-ln")
        op("pool", lambda e: e.tensor_tensor(out=dt_pk[:], in0=lndt_pk[:], in1=ad_pk[:],
                                            op=OP.add),
           reads=("lndt_pk", "ad_pk"), writes=("dt_pk",), tag="sp-add")
        op("act", lambda e: e.activation(out=lndt_pk[:], in_=dt_pk[:], func=AF.Ln),
           reads=("dt_pk",), writes=("lndt_pk",), tag="lndt")
        op("pool", lambda e: e.tensor_tensor(out=ad_pk[:], in0=dt_pk[:],
                                            in1=nalog[:, 0:1].broadcast_to([128, 1024]),
                                            op=OP.mult),
           reads=("dt_pk", "nalog"), writes=("ad_pk",), tag="Ad")
        for j in range(NPAIR):
            col = 128 * j
            jp = j % 2
            ao = 128 * jp
            pp = pe_ if jp == 0 else pf
            bk = "pe" if jp == 0 else "pf"
            op("pe", lambda e, col=col, pp=pp: e.transpose(
                pp[:, 0:128], ad_pk[:, col:col + 128], identf_s[:]),
               reads=("ad_pk", "identf"), writes=(bk,), tag=f"B-tAd{j}")
            op("act", lambda e, ao=ao, pp=pp: e.copy(
                out=adlh[:, ao:ao + 128], in_=pp[:, 0:128]),
               reads=(bk,), writes=(f"adlh{jp}",), tag=f"B-evAd{j}")
            op("pe", lambda e, ao=ao, pp=pp: e.matmul(
                pp[:, 128:256], adlh[:, ao:ao + 128], triu_s[:],
                start=True, stop=True),
               reads=(f"adlh{jp}", "triu"), writes=(bk,), tag=f"B-mh{j}")
            op("act", lambda e, col=col, pp=pp: e.copy(
                out=acum_pk[:, col:col + 128], in_=pp[:, 128:256]),
               reads=(bk,), writes=("acum_pk",), tag=f"B-evh{j}")
            op("pe", lambda e, ao=ao, pp=pp: e.matmul(
                pp[:, 256:384], triu_s[:], adlh[:, ao:ao + 128],
                start=True, stop=True),
               reads=(f"adlh{jp}", "triu"), writes=(bk,), tag=f"B-ml{j}")
            for q2 in range(2):
                op("act", lambda e, j=j, q2=q2, pp=pp: e.activation(
                    out=sdol[:, (2 * j + q2) * 32:(2 * j + q2) * 32 + 32],
                    in_=pp[:, 256 + 64 * q2:256 + 64 * q2 + 32], func=AF.Exp),
                   reads=(bk,), writes=("sdol",), tag=f"B-sdo{j}.{q2}")
            emitD(1)
        op("pool", lambda e: e.tensor_tensor(out=subt_pk[:, 0:NPAIR * 128],
                                            in0=acum_pk[:, 0:NPAIR * 128],
                                            in1=lndt_pk[:, 0:NPAIR * 128],
                                            op=OP.subtract),
           reads=("acum_pk", "lndt_pk"), writes=("subt_pk",), tag="B-sub")
        for j in range(NPAIR):
            col = 128 * j
            jp = j % 2
            pp = pe_ if jp == 0 else pf
            bk = "pe" if jp == 0 else "pf"
            op("act", lambda e, col=col: e.activation(
                out=coeft_pk[:, col:col + 128], in_=subt_pk[:, col:col + 128],
                func=AF.Exp, scale=-1.0, bias=acum_pk[:, col + 127:col + 128]),
               reads=("subt_pk", "acum_pk"), writes=("coeft_pk",), tag=f"B-coef{j}")
            for q2 in range(2):
                c = 2 * j + q2
                op("act", lambda e, c=c, col=col, q2=q2: e.dma_start(
                    out=aflat_d[c:c + 1, :].rearrange("o (h l) -> (o h) l", h=32),
                    in_=acum_pk[64 * q2:64 * q2 + 32, col:col + 128]),
                   reads=("acum_pk",), writes=("aflat",), dma=f"aflat{c % 2}",
                   tag=f"B-fl{c}")
            op("pe", lambda e, col=col, pp=pp: e.transpose(
                pp[:, 0:128], subt_pk[:, col:col + 128], identf_s[:]),
               reads=("subt_pk", "identf"), writes=(bk,), tag=f"B-tsub{j}")
            for q2 in range(2):
                c = 2 * j + q2
                op("act", lambda e, c=c, q2=q2, pp=pp: e.copy(
                    out=subl[:, c * 32:c * 32 + 32],
                    in_=pp[:, 64 * q2:64 * q2 + 32]),
                   reads=(bk,), writes=("subl",), tag=f"B-evsub{c}")
            op("pe", lambda e, col=col, pp=pp: e.transpose(
                pp[:, 128:256], coeft_pk[:, col:col + 128], identf_s[:]),
               reads=("coeft_pk", "identf"), writes=(bk,), tag=f"B-tcoef{j}")
            for q2 in range(2):
                c = 2 * j + q2
                op("act", lambda e, c=c, q2=q2, pp=pp: e.copy(
                    out=coefl[:, c * 32:c * 32 + 32],
                    in_=pp[:, 128 + 64 * q2:128 + 64 * q2 + 32]),
                   reads=(bk,), writes=("coefl",), tag=f"B-evcoef{c}")
            emitD(1)
        op("act", lambda e: e.activation(
            out=coeft_pk[:, 0:NPAIR],
            in_=acum_pk[:, 0:NPAIR * 128].rearrange("p (j l) -> p j l", l=128)[:, :, 127:128],
            func=AF.Exp),
           reads=("acum_pk",), writes=("cdall",), tag="B-cdall")
        op("pe", lambda e: e.transpose(pe_[0:NPAIR, 0:128], coeft_pk[:, 0:NPAIR],
                                       identf_s[:]),
           reads=("cdall", "identf"), writes=("pe",), tag="B-tcd")
        op("act", lambda e: e.copy(out=adlh[0:NPAIR, 0:128], in_=pe_[0:NPAIR, 0:128]),
           reads=("pe",), writes=("cdsb",), tag="B-evcd")
        for j in range(NPAIR):
            jp = j % 2
            pp = pe_ if jp == 0 else pf
            bk = "pe" if jp == 0 else "pf"
            op("act", lambda e, j=j: e.dma_start(out=cdrow8[:],
                                                   in_=adlh[j:j + 1, 0:128]),
               reads=("cdsb",), writes=("cdrow8",), dma="cdf", tag=f"B-cdmv{j}")
            op("pe", lambda e, pp=pp: e.matmul(pp[:, 384:512], onesf_s[:], cdrow8[:],
                                               start=True, stop=True),
               reads=("cdrow8", "onesf"), writes=(bk,), tag=f"B-mcd{j}")
            for q2 in range(2):
                c = 2 * j + q2
                op("act", lambda e, c=c, q2=q2, pp=pp: e.copy(
                    out=crepl[:, c * 32:c * 32 + 32],
                    in_=pp[:, 384 + 64 * q2:384 + 64 * q2 + 32]),
                   reads=(bk,), writes=("crepl",), tag=f"B-evcrep{c}")
        emitD(len(d_items))
        op("dve", lambda e: e.tensor_tensor(
            out=woutT[:].rearrange("p (kc c) -> p kc c", c=1024),
            in0=woutT[:].rearrange("p (kc c) -> p kc c", c=1024),
            in1=nwpk_s[:, :, None].broadcast_to([128, 16, 1024]),
            op=OP.mult),
           reads=("woutT0", "woutT1", "woutT2", "woutT3", "nwpk"),
           writes=("woutT0", "woutT1", "woutT2", "woutT3"), tag="nwfold")

        B.barrier("pre-C")
        op("sp", lambda e: e.dma_start(out=ddb[:, 0:DI],
                                       in_=dflat_d.broadcast_to([128, DI])),
           writes=("ddb",), dma="ddb0", tag="dbcast")
        op("dve", lambda e: e.tensor_copy(out=drep[:], in_=ddb[:, 0:DI]),
           reads=("ddb",), writes=("drep",), tag="drep")
        op("dve", lambda e: e.memset(srun[:], 0.0), writes=("srun",), tag="z-srun")
        op("dve", lambda e: e.memset(
            xbcraw[:].rearrange("p (fc c) -> p fc c", c=260)[:, :, 0:4], 0.0),
           writes=('xbc0', 'xbc1', 'xbc2', 'xbc3', 'xbc4', 'xbc5', 'xbc6', 'xbc7', 'xbc8'), tag="z-halo")
        script_C(B, nc, d, locals(), NCH)
        B.barrier("C-E")
        script_E(B, nc, d, locals(), S, SG, NSG)
        B.barrier("end")

        esems = {e: es.enter_context(nc.semaphore(f"s_{e}")) for e in ENGS}
        dsems = {ch: es.enter_context(nc.semaphore(f"d_{ch}")) for ch in B.dma_n}
        block = es.enter_context(nc.Block())

        emit_stream = B.emit(nc, None, esems, dsems)

        @block.tensor
        def _(e):
            emit_stream("pe", e)

        @block.scalar
        def _(e):
            emit_stream("act", e)

        @block.vector
        def _(e):
            emit_stream("dve", e)

        @block.gpsimd
        def _(e):
            emit_stream("pool", e)

        @block.sync
        def _(e):
            emit_stream("sp", e)

    ar.free()
    return nc


def script_C(B, nc, d, V, NCH):
    op = B.op
    g = lambda n: V[n]
    (pa, pb, pt, pe_, pf) = (g("pa"), g("pb"), g("pt"), g("pe_"), g("pf"))
    winT, woutT, u_tm, u_fm = g("winT"), g("woutT"), g("u_tm"), g("u_fm")
    sz, xbcraw, convtmp = g("sz"), g("xbcraw"), g("convtmp")
    xact_pair, B_s2, C_s2, B_tm2 = g("xact_pair"), g("B_s2"), g("C_s2"), g("B_tm2")
    subl, sdol, coefl, crepl = g("subl"), g("sdol"), g("coefl"), g("crepl")
    ddb, lt, gtm = g("ddb"), g("lt"), g("gtm")
    xt, xtc = g("xt"), g("xtc")
    srun_bf, srun, y1, ynorm, yt = g("srun_bf"), g("srun"), g("y1"), g("ynorm"), g("yt")
    drep, x1s, x1n = g("drep"), g("x1s"), g("x1n")
    identb_s, identf_s, maskT_s = g("identb_s"), g("identf_s"), g("maskT_s")
    convw_s, convb_s = g("convw_s"), g("convb_s")
    ssum, xsum, x2sum = g("ssum"), g("xsum"), g("x2sum")
    mstat, lnt, mm_t, vstat, rstd, b2 = (g("mstat"), g("lnt"), g("mm_t"),
                                          g("vstat"), g("rstd"), g("b2"))
    cbias = g("cbias")
    u_fmT_d, u_tm_d, aflat_d, x1_d = d["u_fmT"], d["u_tm"], d["acumflat"], d["x1d"]
    xr3 = xbcraw[:].rearrange("p (fc c) -> p fc c", c=260)
    PABANK = [("pa0", pa, 0), ("pa1", pa, 512), ("pb0", pb, 0), ("pb1", pb, 512)]
    WK = [f"winT{kc}" for kc in range(8)]
    WOK = ["woutT0", "woutT1", "woutT2", "woutT3"]

    def inproj_xbc_pair(c0):
        for w in range(2):
            for q in range(4):
                fc0 = w * 8 + q * 2
                bk, pp, co = PABANK[q]
                for fc in (fc0, fc0 + 1):
                    cc = co + (fc - fc0) * 256
                    for kc in range(8):
                        op("pe", lambda e, fc=fc, kc=kc, pp=pp, cc=cc: e.matmul(
                            pp[:, cc:cc + 256],
                            winT[:, kc * 4256 + 2048 + fc * 128:
                                 kc * 4256 + 2048 + (fc + 1) * 128],
                            u_fm[:, kc * 256:(kc + 1) * 256],
                            start=(kc == 0), stop=(kc == 7)),
                           reads=(WK[kc], f"ufm{kc // 4}"), writes=(bk,),
                           tag=f"C{c0}-x{fc}.{kc}")
                op("act", lambda e, fc0=fc0, pp=pp, co=co: e.copy(
                    out=xr3[:, fc0:fc0 + 2, 4:260], in_=pp[:, co:co + 512]),
                   reads=(bk,), writes=(f"xbc{fc0 // 2}",), tag=f"C{c0}-xev{fc0}")
        for kc in range(8):
            op("pe", lambda e, kc=kc: e.matmul(
                pe_[:, 0:256], winT[:, kc * 4256 + 2048 + 16 * 128:
                                    kc * 4256 + 2048 + 17 * 128],
                u_fm[:, kc * 256:(kc + 1) * 256],
                start=(kc == 0), stop=(kc == 7)),
               reads=(WK[kc], f"ufm{kc // 4}"), writes=("pe",), tag=f"C{c0}-x16.{kc}")
        op("act", lambda e: e.copy(out=xr3[:, 16:17, 4:260], in_=pe_[:, 0:256]),
           reads=("pe",), writes=("xbc8",), tag=f"C{c0}-xev16")

    def inproj_z(c):
        par = c % 2
        for gi, (bk, pp, co) in enumerate(PABANK):
            for kc in range(8):
                op("pe", lambda e, gi=gi, kc=kc, pp=pp, co=co, par=par: e.matmul(
                    pp[:, co:co + 512],
                    u_fm[:, kc * 256 + par * 128:kc * 256 + par * 128 + 128],
                    winT[:, kc * 4256 + gi * 512:kc * 4256 + (gi + 1) * 512],
                    start=(kc == 0), stop=(kc == 7)),
                   reads=(WK[kc], f"ufm{kc // 4}"), writes=(bk,),
                   tag=f"C{c}-z{gi}.{kc}")
            op("act", lambda e, gi=gi, pp=pp, co=co, par=par: e.activation(
                out=sz[par][:, gi * 512:(gi + 1) * 512], in_=pp[:, co:co + 512],
                func=AF.Silu),
               reads=(bk,), writes=(f"sz{par}",), tag=f"C{c}-szev{gi}")

    def conv_pair(c0):
        for gg in range(9):
            fcs = [2 * gg, 2 * gg + 1] if gg < 8 else [16]
            ck = f"convtmp{gg % 2}"
            for j in range(4):
                for fc in fcs:
                    q = (fc % 2) * 256
                    if j == 0:
                        op("dve", lambda e, fc=fc, q=q, gg=gg: e.scalar_tensor_tensor(
                            out=convtmp[gg % 2][:, q:q + 256], in0=xr3[:, fc, 1:257],
                            scalar=convw_s[:, fc * 4:fc * 4 + 1],
                            in1=convb_s[:, fc:fc + 1].broadcast_to([128, 256]),
                            op0=OP.mult, op1=OP.add),
                           reads=(f"xbc{gg}", "convw", "convb"), writes=(ck,),
                           tag=f"C{c0}-cv{fc}.0")
                    else:
                        op("dve", lambda e, fc=fc, j=j, q=q, gg=gg: e.scalar_tensor_tensor(
                            out=convtmp[gg % 2][:, q:q + 256],
                            in0=xr3[:, fc, 1 + j:257 + j],
                            scalar=convw_s[:, fc * 4 + j:fc * 4 + j + 1],
                            in1=convtmp[gg % 2][:, q:q + 256], op0=OP.mult, op1=OP.add),
                           reads=(f"xbc{gg}", "convw", ck), writes=(ck,),
                           tag=f"C{c0}-cv{fc}.{j}")
            if gg < 8:
                op("act", lambda e, gg=gg: e.activation(
                    out=xact_pair[:, gg * 512:(gg + 1) * 512], in_=convtmp[gg % 2][:],
                    func=AF.Silu),
                   reads=(ck,), writes=(f"xact{gg}",), tag=f"C{c0}-cs{gg}")
            else:
                op("act", lambda e: e.activation(
                    out=B_s2[:], in_=convtmp[0][0:64, 0:256], func=AF.Silu),
                   reads=("convtmp0",), writes=("Bs",), tag=f"C{c0}-csB")
                op("act", lambda e: e.activation(
                    out=C_s2[:], in_=convtmp[0][64:128, 0:256], func=AF.Silu),
                   reads=("convtmp0",), writes=("Cs",), tag=f"C{c0}-csC")
        op("dve", lambda e: e.tensor_copy(out=xr3[:, :, 1:4], in_=xr3[:, :, 257:260]),
           reads=('xbc0', 'xbc1', 'xbc2', 'xbc3', 'xbc4', 'xbc5', 'xbc6', 'xbc7', 'xbc8'), writes=('xbc0', 'xbc1', 'xbc2', 'xbc3', 'xbc4', 'xbc5', 'xbc6', 'xbc7', 'xbc8'), tag=f"C{c0}-halo")

    def ufm_load(c):
        for hf in range(2):
            op("pool", lambda e, c=c, hf=hf: e.dma_start(
                out=u_fm[:, hf * 1024:(hf + 1) * 1024]
                    .rearrange("p (kc t) -> p kc t", t=256),
                in_=u_fmT_d[hf * 512:(hf + 1) * 512, c * CH:(c + 2) * CH]
                    .rearrange("(kc p) t -> p kc t", p=128)),
               writes=(f"ufm{hf}",), dma=f"ufm{hf}", tag=f"C{c}-ufm{hf}")

    def chunk_stage(c):
        par = c % 2
        op("sp", lambda e, c=c: e.dma_start(
            out=ddb[:], in_=aflat_d[c:c + 1, 0:2048].broadcast_to([128, 2048])),
           reads=("aflat",), writes=("ddb",), dma="ddb0", tag=f"C{c}-db0")
        op("sp", lambda e, c=c: e.dma_start(
            out=u_tm[:], in_=u_tm_d[c * CH:(c + 1) * CH, :]),
           writes=("u_tm",), dma="utm", tag=f"C{c}-utm")

        for hh in range(2):
            op("dve", lambda e, c=c, hh=hh: e.tensor_tensor(
                out=ddb[:].rearrange("p (h l) -> p h l", l=CH),
                in0=ddb[:].rearrange("p (h l) -> p h l", l=CH),
                in1=subl[:, c * 32 + hh * 16:c * 32 + (hh + 1) * 16][:, :, None]
                    .broadcast_to([128, 16, CH]),
                op=OP.subtract),
               reads=("ddb", "subl"), writes=("ddb",), tag=f"C{c}-sub{hh}")
            if hh == 0:
                op("act", lambda e: e.activation(out=lt[:], in_=ddb[:],
                                                 func=AF.Exp),
                   reads=("ddb",), writes=("lt",), tag=f"C{c}-exp0")
                op("sp", lambda e, c=c: e.dma_start(
                    out=ddb[:],
                    in_=aflat_d[c:c + 1, 2048:4096].broadcast_to([128, 2048])),
                   reads=("aflat",), writes=("ddb",), dma="ddb1", tag=f"C{c}-db1")
            else:
                op("act", lambda e: e.activation(out=ddb[:], in_=ddb[:],
                                                 func=AF.Exp),
                   reads=("ddb",), writes=("ddb",), tag=f"C{c}-exp1")

        op("pe", lambda e, par=par: e.transpose(
            pt[:, 0:64], B_s2[:, par * 128:par * 128 + 128][0:64, :]
            if False else B_s2[0:64, par * 128:par * 128 + 128],
            identb_s[0:64, 0:64]),
           reads=("Bs", "identb"), writes=("pt0",), tag=f"C{c}-tB")
        op("dve", lambda e, par=par: e.tensor_copy(
            out=B_tm2[:, par * 64:par * 64 + 64], in_=pt[:, 0:64]),
           reads=("pt0",), writes=("Btm",), tag=f"C{c}-evB")
        op("pe", lambda e, par=par: e.matmul(
            pe_[:, 128:256], B_s2[0:64, par * 128:par * 128 + 128],
            C_s2[0:64, par * 128:par * 128 + 128], start=True, stop=True),
           reads=("Bs", "Cs"), writes=("pe",), tag=f"C{c}-GT")
        op("dve", lambda e: e.tensor_tensor(out=gtm[:], in0=pe_[:, 128:256],
                                            in1=maskT_s[:], op=OP.mult),
           reads=("pe", "maskT"), writes=("gtm",), tag=f"C{c}-GTm")
        op("act", lambda e: e.copy(out=srun_bf[:, 0:1024], in_=srun[0:64, :]),
           reads=("srun",), writes=("srunbf",), tag=f"C{c}-stc0")
        op("act", lambda e: e.copy(out=srun_bf[:, 1024:2048], in_=srun[64:128, :]),
           reads=("srun",), writes=("srunbf",), tag=f"C{c}-stc1")

        for hh in range(2):
            ptk = f"pt{hh}"
            ptco = hh * 1024
            for i in range(8):
                op("pe", lambda e, i=i, hh=hh, par=par, ptco=ptco: e.transpose(
                    pt[:, ptco + i * 128:ptco + (i + 1) * 128],
                    xact_pair[:, (hh * 8 + i) * 256 + par * 128:
                              (hh * 8 + i) * 256 + par * 128 + 128],
                    identb_s[:]),
                   reads=(f"xact{(hh * 8 + i) // 2}", "identb"), writes=(ptk,),
                   tag=f"C{c}-tx{hh}.{i}")
            op("act", lambda e, ptco=ptco, hh=hh: e.copy(
                out=xt[hh][:], in_=pt[:, ptco:ptco + 1024]),
               reads=(ptk,), writes=(f"xt{hh}",), tag=f"C{c}-evx{hh}")
            op("dve", lambda e, hh=hh: e.tensor_tensor(
                out=xtc[hh][:], in0=xt[hh][:],
                in1=drep[:, hh * 1024:(hh + 1) * 1024], op=OP.mult),
               reads=(f"xt{hh}", "drep"), writes=(f"xtc{hh}",), tag=f"C{c}-xtd{hh}")
            for j in range(2):
                op("pe", lambda e, j=j, hh=hh, par=par: e.matmul(
                    pa[:, j * 512:(j + 1) * 512],
                    C_s2[0:64, par * 128:par * 128 + 128],
                    srun_bf[:, hh * 1024 + j * 512:hh * 1024 + (j + 1) * 512],
                    start=True, stop=True),
                   reads=("Cs", "srunbf"), writes=(f"pa{j}",), tag=f"C{c}-yo{hh}.{j}")
            if hh == 0:
                op("dve", lambda e: e.scalar_tensor_tensor(
                    out=lt[:].rearrange("p (h l) -> p h l", l=CH),
                    in0=lt[:].rearrange("p (h l) -> p h l", l=CH), scalar=1e30,
                    in1=gtm[:, None, :].broadcast_to([128, 16, CH]),
                    op0=OP.min, op1=OP.mult),
                   reads=("lt", "gtm"), writes=("lt",), tag=f"C{c}-stt{hh}")
            else:
                op("dve", lambda e: e.scalar_tensor_tensor(
                    out=lt[:].rearrange("p (h l) -> p h l", l=CH),
                    in0=ddb[:].rearrange("p (h l) -> p h l", l=CH), scalar=1e30,
                    in1=gtm[:, None, :].broadcast_to([128, 16, CH]),
                    op0=OP.min, op1=OP.mult),
                   reads=("ddb", "gtm"), writes=("lt",), tag=f"C{c}-stt{hh}")
            for jb in range(2):
                op("pe", lambda e, hh=hh, jb=jb: e.matmul(
                    pb[:, jb * 512:(jb + 1) * 512], identb_s[:],
                    xtc[hh][:, jb * 512:(jb + 1) * 512],
                    start=True, stop=False),
                   reads=("identb", f"xtc{hh}"), writes=(f"pb{jb}",),
                   tag=f"C{c}-ydx{hh}.{jb}")
            for h in range(16):
                bk = "pb0" if h < 8 else "pb1"
                co = (h % 8) * 64 + (h // 8) * 512
                op("pe", lambda e, h=h, co=co, hh=hh: e.matmul(
                    pb[:, co:co + 64], lt[:, h * 128:(h + 1) * 128],
                    xt[hh][:, h * 64:(h + 1) * 64], start=False, stop=True),
                   reads=("lt", f"xt{hh}"), writes=(bk,), tag=f"C{c}-ydg{hh}.{h}")
            op("dve", lambda e, c=c, hh=hh: e.tensor_tensor(
                out=xtc[hh][:].rearrange("p (h q) -> p h q", q=64),
                in0=xt[hh][:].rearrange("p (h q) -> p h q", q=64),
                in1=coefl[:, c * 32 + hh * 16:c * 32 + (hh + 1) * 16][:, :, None]
                    .broadcast_to([128, 16, 64]),
                op=OP.mult),
               reads=(f"xt{hh}", "coefl"), writes=(f"xtc{hh}",), tag=f"C{c}-xtc{hh}")
            for j, (bk2, pp2) in enumerate([("pe", pe_), ("pf", pf)]):
                op("pe", lambda e, j=j, par=par, pp2=pp2, hh=hh: e.matmul(
                    pp2[0:64, 0:512], B_tm2[:, par * 64:par * 64 + 64],
                    xtc[hh][:, j * 512:(j + 1) * 512], start=True, stop=True),
                   reads=("Btm", f"xtc{hh}"), writes=(bk2,), tag=f"C{c}-sc{hh}.{j}")
            op("dve", lambda e, c=c, hh=hh: e.tensor_tensor(
                out=y1[:, hh * 1024:(hh + 1) * 1024].rearrange("p (h q) -> p h q", q=64),
                in0=pa[:].rearrange("p (h q) -> p h q", q=64),
                in1=sdol[:, c * 32 + hh * 16:c * 32 + (hh + 1) * 16][:, :, None]
                    .broadcast_to([128, 16, 64]),
                op=OP.mult),
               reads=("pa0", "pa1", "sdol"), writes=("y1",), tag=f"C{c}-y1{hh}")
            op("dve", lambda e, hh=hh: e.tensor_tensor(
                out=y1[:, hh * 1024:(hh + 1) * 1024],
                in0=y1[:, hh * 1024:(hh + 1) * 1024], in1=pb[:], op=OP.add),
               reads=("pb0", "pb1", "y1"), writes=("y1",), tag=f"C{c}-y2{hh}")
            op("dve", lambda e, c=c, hh=hh: e.tensor_tensor(
                out=srun[hh * 64:(hh + 1) * 64, :].rearrange("p (h q) -> p h q", q=64),
                in0=srun[hh * 64:(hh + 1) * 64, :].rearrange("p (h q) -> p h q", q=64),
                in1=crepl[hh * 64:(hh + 1) * 64,
                          c * 32 + hh * 16:c * 32 + (hh + 1) * 16][:, :, None]
                    .broadcast_to([64, 16, 64]),
                op=OP.mult),
               reads=("srun", "crepl", "srunbf"), writes=("srun",), tag=f"C{c}-sd{hh}")
            op("dve", lambda e, hh=hh: e.tensor_tensor(
                out=srun[hh * 64:(hh + 1) * 64, 0:512],
                in0=srun[hh * 64:(hh + 1) * 64, 0:512], in1=pe_[0:64, 0:512],
                op=OP.add),
               reads=("srun", "pe"), writes=("srun",), tag=f"C{c}-sa{hh}.0")
            op("dve", lambda e, hh=hh: e.tensor_tensor(
                out=srun[hh * 64:(hh + 1) * 64, 512:1024],
                in0=srun[hh * 64:(hh + 1) * 64, 512:1024], in1=pf[0:64, 0:512],
                op=OP.add),
               reads=("srun", "pf"), writes=("srun",), tag=f"C{c}-sa{hh}.1")

        op("dve", lambda e, par=par: e.tensor_tensor(out=y1[:], in0=y1[:],
                                                     in1=sz[par][:], op=OP.mult),
           reads=("y1", f"sz{par}"), writes=("y1",), tag=f"C{c}-gate")
        op("act", lambda e, par=par: e.activation(out=ynorm[:], in_=y1[:], func=AF.Square,
                                         accum_out=ssum[:, par:par + 1]),
           reads=("y1",), writes=("xt0", "xt1", f"ssum{par}"), tag=f"C{c}-sq")
        op("act", lambda e, par=par: e.activation(out=lnt[:, par:par + 1],
                                         in_=ssum[:, par:par + 1], func=AF.Ln,
                                         scale=1.0 / DI, bias=cbias[:, 0:1]),
           reads=(f"ssum{par}", "cbias"), writes=(f"lnt{par}",), tag=f"C{c}-ln")
        op("act", lambda e, par=par: e.activation(out=rstd[:, par:par + 1],
                                         in_=lnt[:, par:par + 1], func=AF.Exp,
                                         scale=-0.5),
           reads=(f"lnt{par}",), writes=(f"rstd{par}",), tag=f"C{c}-rs")
        op("dve", lambda e, par=par: e.tensor_scalar(out=ynorm[:], in0=y1[:],
                                            scalar1=rstd[:, par:par + 1], scalar2=None,
                                            op0=OP.mult),
           reads=("y1", f"rstd{par}"), writes=("xt0", "xt1"), tag=f"C{c}-yn")
        for i in range(16):
            ptk = "pt0" if i < 8 else "pt1"
            op("pe", lambda e, i=i: e.transpose(
                pt[:, i * 128:(i + 1) * 128], ynorm[:, i * 128:(i + 1) * 128],
                identb_s[:]),
               reads=("xt0" if i < 8 else "xt1", "identb"), writes=(ptk,),
               tag=f"C{c}-ty{i}")
        op("act", lambda e: e.copy(out=yt[:, 0:1024], in_=pt[:, 0:1024]),
           reads=("pt0",), writes=("yt",), tag=f"C{c}-evy0")
        op("act", lambda e: e.copy(out=yt[:, 1024:2048], in_=pt[:, 1024:2048]),
           reads=("pt1",), writes=("yt",), tag=f"C{c}-evy1")
        for nh in range(2):
            for kc in range(16):
                op("pe", lambda e, nh=nh, kc=kc: e.matmul(
                    pa[:, nh * 512:(nh + 1) * 512], yt[:, kc * 128:(kc + 1) * 128],
                    woutT[:, kc * 1024 + nh * 512:kc * 1024 + (nh + 1) * 512],
                    start=(kc == 0), stop=(kc == 15)),
                   reads=("yt", WOK[kc // 4]), writes=(f"pa{nh}",), tag=f"C{c}-op{nh}.{kc}")
        op("dve", lambda e, par=par: e.scalar_tensor_tensor(
            out=x1s[:], in0=pa[:], scalar=1.0, in1=u_tm[:], op0=OP.mult, op1=OP.add,
            accum_out=xsum[:, par:par + 1]),
           reads=("pa0", "pa1", "u_tm"), writes=("x1s", f"xsum{par}"), tag=f"C{c}-res")
        op("act", lambda e, par=par: e.activation(out=ynorm[:, 0:1024], in_=x1s[:],
                                         func=AF.Square, accum_out=x2sum[:, par:par + 1]),
           reads=("x1s",), writes=("xt0", f"x2sum{par}"), tag=f"C{c}-sq1")
        op("dve", lambda e, par=par: e.tensor_scalar(out=mstat[:, par:par + 1],
                                            in0=xsum[:, par:par + 1],
                                            scalar1=1.0 / DM, scalar2=None, op0=OP.mult),
           reads=(f"xsum{par}",), writes=(f"mstat{par}",), tag=f"C{c}-m")
        op("dve", lambda e, par=par: e.tensor_tensor(out=mm_t[:, par:par + 1],
                                            in0=mstat[:, par:par + 1],
                                            in1=mstat[:, par:par + 1], op=OP.mult),
           reads=(f"mstat{par}",), writes=(f"mm_t{par}",), tag=f"C{c}-mm")
        op("dve", lambda e, par=par: e.scalar_tensor_tensor(
            out=vstat[:, par:par + 1], in0=x2sum[:, par:par + 1], scalar=1.0 / DM,
            in1=mm_t[:, par:par + 1], op0=OP.mult, op1=OP.subtract),
           reads=(f"x2sum{par}", f"mm_t{par}"), writes=(f"vstat{par}",), tag=f"C{c}-v")
        op("act", lambda e, par=par: e.activation(out=lnt[:, par:par + 1],
                                         in_=vstat[:, par:par + 1], func=AF.Ln,
                                         bias=cbias[:, 0:1]),
           reads=(f"vstat{par}", "cbias"), writes=(f"lnt{par}",), tag=f"C{c}-lnv")
        op("act", lambda e, par=par: e.activation(out=rstd[:, par:par + 1],
                                         in_=lnt[:, par:par + 1], func=AF.Exp,
                                         scale=-0.5),
           reads=(f"lnt{par}",), writes=(f"rstd{par}",), tag=f"C{c}-rs1")
        op("dve", lambda e, par=par: e.scalar_tensor_tensor(
            out=b2[:, par:par + 1], in0=mstat[:, par:par + 1], scalar=-1.0,
            in1=rstd[:, par:par + 1], op0=OP.mult, op1=OP.mult),
           reads=(f"mstat{par}", f"rstd{par}"), writes=(f"b2{par}",), tag=f"C{c}-b2")
        op("act", lambda e, par=par: e.activation(out=x1n[:], in_=x1s[:], func=AF.Identity,
                                         scale=rstd[:, par:par + 1], bias=b2[:, par:par + 1]),
           reads=("x1s", f"rstd{par}", f"b2{par}"), writes=("x1n",), tag=f"C{c}-x1n")
        op("act", lambda e, c=c: e.dma_start(out=x1_d[c * CH:(c + 1) * CH, :],
                                              in_=x1n[:]),
           reads=("x1n",), writes=("x1d",), dma="x1st", tag=f"C{c}-x1st")

    ufm_load(0)
    inproj_xbc_pair(0)
    inproj_z(0)
    inproj_z(1)
    for jj in range(NCH // 2):
        c0 = 2 * jj
        conv_pair(c0)
        if c0 + 2 < NCH:
            ufm_load(c0 + 2)
            inproj_xbc_pair(c0 + 2)
        chunk_stage(c0)
        if c0 + 2 < NCH:
            inproj_z(c0 + 2)
        chunk_stage(c0 + 1)
        if c0 + 2 < NCH:
            inproj_z(c0 + 3)


def script_D(B, nc, d, V):
    op = B.op
    g = lambda n: V[n]
    wst, trow, trow2 = g("wst"), g("trow"), g("trow2")
    part1, part2 = g("part1"), g("part2")
    scs, chs, cls, ncls = g("scs"), g("chs"), g("cls"), g("ncls")
    wfc_d, wproj_d = d["wfc"], d["wproj"]
    ternfc_d, ternproj_d = d["ternfc"], d["ternproj"]
    items = []
    kctr = [0]
    tctr = [0]

    def emit_load(src_ap, width, tag):
        b = kctr[0] % 4
        kctr[0] += 1
        op("sp", lambda e: e.dma_start(out=wst[b][:, 0:width], in_=src_ap),
           writes=(f"wst{b}",), dma=f"wst{b}", tag=tag)
        return b

    def emit_r1(b, width, p1sl, p1k, tag, eng="dve"):
        op(eng, lambda e: e.tensor_reduce(
            out=part1[:, p1sl],
            in_=wst[b][:, 0:width].rearrange("p (a b) -> p a b", b=32),
            axis=AXX, op=OP.add, apply_absolute_value=True),
           reads=(f"wst{b}",), writes=(p1k,), tag=tag)

    def emit_scale(par, width, tagp):
        sk = f"scs{par}"
        sl_ = slice(par, par + 1)
        op("dve", lambda e: e.tensor_scalar(out=scs[:, sl_], in0=scs[:, sl_],
                                            scalar1=1.0 / width, scalar2=1e-5,
                                            op0=OP.mult, op1=OP.max),
           reads=(sk,), writes=(sk,), tag=f"{tagp}-mc")
        op("dve", lambda e: e.tensor_scalar(out=chs[:, sl_], in0=scs[:, sl_],
                                            scalar1=0.5, scalar2=None, op0=OP.mult),
           reads=(sk,), writes=(f"chs{par}",), tag=f"{tagp}-ch")
        op("dve", lambda e: e.tensor_scalar(out=cls[:, sl_], in0=scs[:, sl_],
                                            scalar1=2.0 ** -25, scalar2=None,
                                            op0=OP.mult),
           reads=(sk,), writes=(f"cls{par}",), tag=f"{tagp}-cl")
        op("dve", lambda e: e.tensor_scalar(out=ncls[:, sl_], in0=cls[:, sl_],
                                            scalar1=-1.0, scalar2=None, op0=OP.mult),
           reads=(f"cls{par}",), writes=(f"ncls{par}",), tag=f"{tagp}-ncl")

    def emit_thresh(bref, par, width, dst_ap, wkey, tagp):
        t = tctr[0] % 2
        tctr[0] += 1
        b = bref[0]
        sk = f"scs{par}"
        sl_ = slice(par, par + 1)
        wk = f"wst{b}"
        tk = f"trow{t}"
        op("dve", lambda e: e.tensor_scalar(out=trow[t][:, 0:width],
                                            in0=wst[b][:, 0:width],
                                            scalar1=chs[:, sl_], scalar2=cls[:, sl_],
                                            op0=OP.subtract, op1=OP.is_gt),
           reads=(wk, f"chs{par}", f"cls{par}"), writes=(tk,), tag=f"{tagp}-tp")
        op("dve", lambda e: e.tensor_scalar(out=trow2[:, 0:width],
                                            in0=wst[b][:, 0:width],
                                            scalar1=chs[:, sl_], scalar2=ncls[:, sl_],
                                            op0=OP.add, op1=OP.is_lt),
           reads=(wk, f"chs{par}", f"ncls{par}"), writes=("trow2",),
           tag=f"{tagp}-tn")
        op("dve", lambda e: e.tensor_tensor(out=trow[t][:, 0:width],
                                            in0=trow[t][:, 0:width],
                                            in1=trow2[:, 0:width],
                                            op=OP.subtract),
           reads=(tk, "trow2"), writes=(tk,), tag=f"{tagp}-sub")
        op("act", lambda e: e.activation(out=trow[t][:, 0:width],
                                         in_=trow[t][:, 0:width],
                                         func=AF.Identity, scale=scs[:, sl_]),
           reads=(tk, sk), writes=(tk,), tag=f"{tagp}-sc")
        op("act", lambda e: e.dma_start(out=dst_ap, in_=trow[t][:, 0:width]),
           reads=(tk,), writes=(wkey,), dma=f"tern{t}", tag=f"{tagp}-st")

    def fc_item(i):
        par = i % 2
        po = par * 128
        p1k = f"part1{par}"
        sl_ = slice(par, par + 1)

        def run():
            b = emit_load(wfc_d[i * 128:(i + 1) * 128, :], 1024, f"ternfc{i}-ld")
            emit_r1(b, 1024, slice(po, po + 32), p1k, f"ternfc{i}-r1")
            op("dve", lambda e: e.tensor_reduce(
                out=scs[:, sl_], in_=part1[:, po:po + 32], axis=AXX, op=OP.add),
               reads=(p1k,), writes=(f"scs{par}",), tag=f"ternfc{i}-r3")
            emit_scale(par, 1024, f"ternfc{i}")
            emit_thresh([b], par, 1024, ternfc_d[i * 128:(i + 1) * 128, :],
                        "ternfc", f"ternfc{i}")
        return run

    def proj_items(i):
        par = i % 2
        po = par * 128
        p1k = f"part1{par}"
        sl_ = slice(par, par + 1)
        bufA = [None]
        bufB = [None]

        def runA():
            bufA[0] = emit_load(wproj_d[i * 128:(i + 1) * 128, 0:2048], 2048,
                                f"ternproj{i}a-ld")
            emit_r1(bufA[0], 2048, slice(po, po + 64), p1k, f"ternproj{i}a-r1")

        def runB():
            bufB[0] = emit_load(wproj_d[i * 128:(i + 1) * 128, 2048:4096], 2048,
                                f"ternproj{i}b-ld")
            emit_r1(bufB[0], 2048, slice(po + 64, po + 128), p1k,
                    f"ternproj{i}b-r1")
            op("dve", lambda e: e.tensor_reduce(
                out=part2[:, par * 4:par * 4 + 4],
                in_=part1[:, po:po + 128].rearrange("p (a b) -> p a b", b=32),
                axis=AXX, op=OP.add),
               reads=(p1k,), writes=(f"part2{par}",), tag=f"ternproj{i}-r2")
            op("dve", lambda e: e.tensor_reduce(
                out=scs[:, sl_], in_=part2[:, par * 4:par * 4 + 4],
                axis=AXX, op=OP.add),
               reads=(f"part2{par}",), writes=(f"scs{par}",), tag=f"ternproj{i}-r3")
            emit_scale(par, 4096, f"ternproj{i}")
            emit_thresh(bufA, par, 2048,
                        ternproj_d[i * 128:(i + 1) * 128, 0:2048],
                        "ternproj", f"ternproj{i}a")

        def runC():
            emit_thresh(bufB, par, 2048,
                        ternproj_d[i * 128:(i + 1) * 128, 2048:4096],
                        "ternproj", f"ternproj{i}b")
        return [runA, runB, runC]

    for i in range(32):
        items.append(fc_item(i))
    for i in range(8):
        items.extend(proj_items(i))
    return items


def script_E(B, nc, d, V, S, SG, NSG):
    op = B.op
    g = lambda n: V[n]
    (pa, pb, pt, pe_, pf) = (g("pa"), g("pb"), g("pt"), g("pe_"), g("pf"))
    wfcT, wprT = g("wfcT"), g("wprT")
    x1tm, x1bf, x1fm, r2, hr = g("x1tm"), g("x1bf"), g("x1fm"), g("r2"), g("hr")
    t1b, t2b, vtb, fbtb, uTg = g("t1b"), g("t2b"), g("vtb"), g("fbtb"), g("uTg")
    outst = g("outst")
    identb_s, gate_s = g("identb_s"), g("gate_s")
    xsum, x2sum = g("xsum"), g("x2sum")
    mstat, lnt, mm_t, vstat, rstd, b2 = (g("mstat"), g("lnt"), g("mm_t"),
                                          g("vstat"), g("rstd"), g("b2"))
    cbias = g("cbias")
    x1_d, out_d = d["x1d"], d["out"]
    ternfc_d, ternproj_d = d["ternfc"], d["ternproj"]
    vT_d, fbT_d, uTlr_d, gater_d = d["vT"], d["fbT"], d["uTlr"], d["gater"]
    NT = SG // 128
    PABANK = [("pa0", pa, 0, 0), ("pa1", pa, 512, 0), ("pe", pe_, 0, 1), ("pf", pf, 0, 1)]

    def x1block(sg):
        for tg in range(NT):
            op("sp", lambda e, sg=sg, tg=tg: e.dma_start(
                out=x1tm[:, tg * 1024:(tg + 1) * 1024],
                in_=x1_d[sg * SG + tg * 128:sg * SG + (tg + 1) * 128, :]),
               writes=(f"x1tm{tg}",), dma=f"x1ld{tg % 2}", tag=f"E{sg}-x1ld{tg}")
            op("dve", lambda e, tg=tg: e.tensor_copy(
                out=x1bf[:], in_=x1tm[:, tg * 1024:(tg + 1) * 1024]),
               reads=(f"x1tm{tg}",), writes=("x1bf",), tag=f"E{sg}-cast{tg}")
            for i in range(8):
                op("pe", lambda e, i=i: e.transpose(
                    pt[:, i * 128:(i + 1) * 128], x1bf[:, i * 128:(i + 1) * 128],
                    identb_s[:]),
                   reads=("x1bf", "identb"), writes=("pt0",), tag=f"E{sg}-t{tg}.{i}")
            op("dve", lambda e, tg=tg: e.tensor_copy(
                out=x1fm[:].rearrange("p (kc t) -> p kc t", t=SG)
                    [:, :, tg * 128:(tg + 1) * 128],
                in_=pt[:, 0:1024].rearrange("p (kc t) -> p kc t", t=128)),
               reads=("pt0",), writes=("x1fm",), tag=f"E{sg}-evt{tg}")

    x1block(0)
    for k in range(8):
        op("sp", lambda e, k=k: e.dma_start(
            out=wfcT[:, k * 4096:(k + 1) * 4096],
            in_=ternfc_d[0:4096, k * 128:(k + 1) * 128], transpose=True),
           reads=("ternfc",), writes=(f"wfcT{k}",), dma=f"wef{k%8}", tag=f"E-wfc{k}")
    for k in range(32):
        op("sp", lambda e, k=k: e.dma_start(
            out=wprT[:, k * 1024:(k + 1) * 1024],
            in_=ternproj_d[0:1024, k * 128:(k + 1) * 128], transpose=True),
           reads=("ternproj",), writes=(f"wprT{k//8}",), dma=f"wef{k%8}", tag=f"E-wpr{k}")
    op("pool", lambda e: e.dma_start(
        out=vtb[:].rearrange("p (kc j) -> p kc j", j=16),
        in_=vT_d.rearrange("(kc p) j -> p kc j", p=128)),
       writes=("vtb",), dma="lr", tag="E-vtb")
    op("pool", lambda e: e.dma_start(out=fbtb[:], in_=fbT_d),
       writes=("fbtb",), dma="lr", tag="E-fbtb")
    op("pool", lambda e: e.dma_start(out=uTg[:], in_=uTlr_d),
       writes=("uTg",), dma="lr", tag="E-uTg")
    op("sp", lambda e: e.dma_start(out=gate_s[:], in_=gater_d),
       writes=("gate",), dma="cst", tag="E-gate")
    op("dve", lambda e: e.tensor_scalar(out=uTg[:], in0=uTg[:],
                                        scalar1=gate_s[:, 0:1], scalar2=None,
                                        op0=OP.mult),
       reads=("uTg", "gate"), writes=("uTg",), tag="E-uTgs")

    for sg in range(NSG):
        if sg > 0:
            x1block(sg)
        for kc in range(8):
            op("pe", lambda e, kc=kc: e.matmul(
                pa[0:16, 0:SG], vtb[:, kc * 16:(kc + 1) * 16],
                x1fm[:, kc * SG:(kc + 1) * SG], start=(kc == 0), stop=(kc == 7)),
               reads=("vtb", "x1fm"), writes=("pa0",), tag=f"E{sg}-t1.{kc}")
        op("act", lambda e: e.copy(out=t1b[:], in_=pa[0:16, 0:SG]),
           reads=("pa0",), writes=("t1b",), tag=f"E{sg}-evt1")
        op("pe", lambda e: e.matmul(pa[0:16, 0:SG], fbtb[:], t1b[:],
                                    start=True, stop=True),
           reads=("fbtb", "t1b"), writes=("pa0",), tag=f"E{sg}-t2")
        op("act", lambda e: e.copy(out=t2b[:], in_=pa[0:16, 0:SG]),
           reads=("pa0",), writes=("t2b",), tag=f"E{sg}-evt2")
        for fg in range(32):
            bk, pp, co, _ = PABANK[fg % 4]
            for kc in range(8):
                op("pe", lambda e, fg=fg, kc=kc, pp=pp, co=co: e.matmul(
                    pp[:, co:co + SG] if SG == 512 else pp[:, co:co + SG],
                    wfcT[:, kc * 4096 + fg * 128:kc * 4096 + (fg + 1) * 128],
                    x1fm[:, kc * SG:(kc + 1) * SG],
                    start=(kc == 0), stop=False),
                   reads=(f"wfcT{kc}", "x1fm"), writes=(bk,), tag=f"E{sg}-fc{fg}.{kc}")
            op("pe", lambda e, fg=fg, pp=pp, co=co: e.matmul(
                pp[:, co:co + SG], uTg[:, fg * 128:(fg + 1) * 128], t2b[:],
                start=False, stop=True),
               reads=("uTg", "t2b"), writes=(bk,), tag=f"E{sg}-lr{fg}")
            op("act", lambda e, fg=fg, pp=pp, co=co: e.activation(
                out=hr[fg % 2][:, 0:SG], in_=pp[:, co:co + SG], func=AF.Relu),
               reads=(bk,), writes=(f"hr{fg%2}",), tag=f"E{sg}-rl{fg}")
            op("dve", lambda e, fg=fg: e.tensor_tensor(
                out=r2[:, fg * SG:(fg + 1) * SG], in0=hr[fg % 2][:, 0:SG],
                in1=hr[fg % 2][:, 0:SG], op=OP.mult),
               reads=(f"hr{fg%2}",), writes=("r2",), tag=f"E{sg}-r2{fg}")
        for tg in range(NT):
            for nh in range(2):
                for kc in range(32):
                    op("pe", lambda e, tg=tg, nh=nh, kc=kc: e.matmul(
                        pb[:, nh * 512:(nh + 1) * 512],
                        r2[:, kc * SG + tg * 128:kc * SG + (tg + 1) * 128],
                        wprT[:, kc * 1024 + nh * 512:kc * 1024 + (nh + 1) * 512],
                        start=(kc == 0), stop=(kc == 31)),
                       reads=("r2", f"wprT{kc//8}"), writes=(f"pb{nh}",),
                       tag=f"E{sg}-pj{tg}.{nh}.{kc}")
            pq = tg % 2
            op("dve", lambda e, tg=tg, pq=pq: e.scalar_tensor_tensor(
                out=outst[:], in0=pb[:], scalar=1.0,
                in1=x1tm[:, tg * 1024:(tg + 1) * 1024], op0=OP.mult, op1=OP.add,
                accum_out=xsum[:, pq:pq + 1]),
               reads=("pb0", "pb1", f"x1tm{tg}"), writes=("outst", f"xsum{pq}"),
               tag=f"E{sg}-res{tg}")
            op("act", lambda e, pq=pq: e.activation(out=x1bf[:], in_=outst[:],
                                             func=AF.Square, accum_out=x2sum[:, pq:pq + 1]),
               reads=("outst",), writes=("x1bf", f"x2sum{pq}"), tag=f"E{sg}-sq{tg}")
            op("dve", lambda e, pq=pq: e.tensor_scalar(out=mstat[:, pq:pq + 1],
                                                in0=xsum[:, pq:pq + 1],
                                                scalar1=1.0 / DM, scalar2=None,
                                                op0=OP.mult),
               reads=(f"xsum{pq}",), writes=(f"mstat{pq}",), tag=f"E{sg}-m{tg}")
            op("dve", lambda e, pq=pq: e.tensor_tensor(out=mm_t[:, pq:pq + 1],
                                                in0=mstat[:, pq:pq + 1],
                                                in1=mstat[:, pq:pq + 1], op=OP.mult),
               reads=(f"mstat{pq}",), writes=(f"mm_t{pq}",), tag=f"E{sg}-mm{tg}")
            op("dve", lambda e, pq=pq: e.scalar_tensor_tensor(
                out=vstat[:, pq:pq + 1], in0=x2sum[:, pq:pq + 1], scalar=1.0 / DM,
                in1=mm_t[:, pq:pq + 1], op0=OP.mult, op1=OP.subtract),
               reads=(f"x2sum{pq}", f"mm_t{pq}"), writes=(f"vstat{pq}",), tag=f"E{sg}-v{tg}")
            op("act", lambda e, pq=pq: e.activation(out=lnt[:, pq:pq + 1],
                                             in_=vstat[:, pq:pq + 1], func=AF.Ln,
                                             bias=cbias[:, 0:1]),
               reads=(f"vstat{pq}", "cbias"), writes=(f"lnt{pq}",), tag=f"E{sg}-lnv{tg}")
            op("act", lambda e, pq=pq: e.activation(out=rstd[:, pq:pq + 1],
                                             in_=lnt[:, pq:pq + 1], func=AF.Exp,
                                             scale=-0.5),
               reads=(f"lnt{pq}",), writes=(f"rstd{pq}",), tag=f"E{sg}-rs{tg}")
            op("dve", lambda e, pq=pq: e.scalar_tensor_tensor(
                out=b2[:, pq:pq + 1], in0=mstat[:, pq:pq + 1], scalar=-1.0,
                in1=rstd[:, pq:pq + 1], op0=OP.mult, op1=OP.mult),
               reads=(f"mstat{pq}", f"rstd{pq}"), writes=(f"b2{pq}",), tag=f"E{sg}-b2{tg}")
            op("act", lambda e, tg=tg, pq=pq: e.activation(
                out=outst[:], in_=outst[:], func=AF.Identity,
                scale=rstd[:, pq:pq + 1], bias=b2[:, pq:pq + 1]),
               reads=("outst", f"rstd{pq}", f"b2{pq}"), writes=("outst",),
               tag=f"E{sg}-app{tg}")
            op("act", lambda e, sg=sg, tg=tg: e.dma_start(
                out=out_d[sg * SG + tg * 128:sg * SG + (tg + 1) * 128, :],
                in_=outst[:]),
               reads=("outst",), dma="outst", tag=f"E{sg}-st{tg}")


def prep_in_map(inp, b, S=2048):
    f32 = np.float32
    u = np.ascontiguousarray(inp["L0_attn_u"][b], dtype=f32)[:S]
    W_in = np.asarray(inp["L0_attn_W_in"], f32)
    conv_w = np.asarray(inp["L0_attn_conv_w"], f32)
    conv_b = np.asarray(inp["L0_attn_conv_b"], f32)
    dt_bias = np.asarray(inp["L0_attn_dt_bias"], f32)
    A_log = np.asarray(inp["L0_attn_A_log"], f32)
    Dp = np.asarray(inp["L0_attn_D"], f32)
    norm_w = np.asarray(inp["L0_attn_norm_w"], f32)
    W_out = np.asarray(inp["L0_attn_W_out"], f32)
    fast_b = np.asarray(inp["L0_ffn_fast_b"], f32)
    w_fc = np.asarray(inp["L0_ffn_w_fc"], f32)
    w_proj = np.asarray(inp["L0_ffn_w_proj"], f32)
    u_lr = np.asarray(inp["L0_ffn_u"], f32)
    v_lr = np.asarray(inp["L0_ffn_v"], f32)
    gate = np.asarray(inp["L0_ffn_gate"], f32)
    m = {
        "u_tm": u,
        "u_fmT": np.ascontiguousarray(u.T),
        "winT": np.ascontiguousarray(W_in.T),
        "woutT": np.ascontiguousarray(W_out.T),
        "convw": np.ascontiguousarray(
            conv_w[:, 0, :].reshape(NFC, 128, 4).transpose(1, 0, 2).reshape(128, NFC * 4)),
        "convb": np.ascontiguousarray(conv_b.reshape(NFC, 128).T),
        "dtb": np.ascontiguousarray(np.tile(dt_bias, 4)[:, None]),
        "alog": np.ascontiguousarray(np.tile(A_log, 4)[:, None]),
        "nwpk": np.ascontiguousarray(norm_w.reshape(16, 128).T),
        "dflat": np.ascontiguousarray(np.repeat(Dp, HD)[None, :]),
        "identf": np.eye(128, dtype=f32),
        "triu": np.triu(np.ones((128, 128), f32)),
        "maskT": np.triu(np.ones((128, 128), f32)),
        "onesf": np.ones((1, 128), f32),
        "wfc": w_fc,
        "wproj": w_proj,
        "vT": np.ascontiguousarray(v_lr.T),
        "fbT": np.ascontiguousarray(fast_b.T),
        "uTlr": np.ascontiguousarray(u_lr.T),
        "gater": np.full((16, 1), gate, f32),
    }
    return m


def kernel(**inputs):
    S = inputs["L0_attn_u"].shape[1]
    nb = inputs["L0_attn_u"].shape[0]
    nc = build(S=S)
    in_maps = [prep_in_map(inputs, b, S) for b in range(nb)]
    res = run_bass_kernel_spmd(nc, in_maps, core_ids=list(range(nb)))
    return np.stack([np.asarray(r["out"], dtype=np.float32) for r in res.results])


# version 26
# speedup vs baseline: 1.0761x; 1.0062x over previous
import numpy as np
import ml_dtypes
import concourse.bass as bass
import concourse.mybir as mybir
from concourse.bass_utils import run_bass_kernel_spmd
from concourse.alu_op_type import AluOpType as OP

F32 = mybir.dt.float32
BF16 = mybir.dt.bfloat16
AF = mybir.ActivationFunctionType
AXX = mybir.AxisListType.X

DM = 1024
DI = 2048
DS = 64
NH = 32
HD = 64
CDIM = 2176
CH = 128
NFC = 17
EPS = 1e-5

ENGS = ("pe", "act", "dve", "pool", "sp")
ENG_ATTR = {"pe": "tensor", "act": "scalar", "dve": "vector", "pool": "gpsimd", "sp": "sync"}


class Builder:
    def __init__(self):
        self.ops = []
        self.eng_n = {e: 0 for e in ENGS}
        self.dma_n = {}
        self.last_w = {}
        self.readers = {}

    def _dep_waits(self, eng, reads, writes):
        waits = []
        def need(tag):
            if tag is None:
                return
            if tag[0] == "e":
                if tag[1] != eng or eng != "pe":
                    waits.append(tag)
            else:
                waits.append(tag)
        for k in reads:
            need(self.last_w.get(k))
        for k in writes:
            need(self.last_w.get(k))
            for t in self.readers.get(k, ()):
                need(t)
        return waits

    def op(self, eng, fn, reads=(), writes=(), dma=None, tag=""):
        reads = list(reads); writes = list(writes)
        waits = self._dep_waits(eng, reads, writes)
        if dma is None:
            self.eng_n[eng] += 1
            mytag = ("e", eng, self.eng_n[eng])
        else:
            k = self.dma_n.get(dma, 0)
            if k > 0:
                waits.append(("d", dma, k))
            self.dma_n[dma] = k + 1
            mytag = ("d", dma, k + 1)
        for k2 in reads:
            self.readers.setdefault(k2, []).append(mytag)
        for k2 in writes:
            self.last_w[k2] = mytag
            self.readers[k2] = []
        self.ops.append(dict(eng=eng, fn=fn, waits=waits, tag=mytag, dbg=tag))
        return mytag

    def barrier(self, tag="bar"):
        marks = {}
        for e in ENGS:
            fn, rd, wr = self.nopfns[e]
            marks[e] = self.op(e, fn, reads=rd, writes=wr, tag=f"{tag}-mark")
        dma_marks = [("d", q, self.dma_n[q]) for q in self.dma_n if self.dma_n[q] > 0]
        marks2 = {}
        for e in ENGS:
            fn, rd, wr = self.nopfns[e]
            marks2[e] = self.op(e, fn, reads=rd, writes=wr, tag=f"{tag}-wait")
            self.ops[-1]["waits"].extend(
                [marks[o] for o in ENGS if o != e] + dma_marks)
        for k in list(self.last_w):
            t = self.last_w[k]
            if t[0] == "e":
                self.last_w[k] = marks2[t[1]]
        self.readers = {k: [] for k in self.readers}

    def emit(self, nc, engines, esems, dsems):
        awaited = {e: set() for e in ENGS}
        for o in self.ops:
            for t in o["waits"]:
                if t[0] == "e":
                    awaited[t[1]].add(t[2])
        rank = {}
        inc_set = {}
        for e in ENGS:
            s = sorted(awaited[e])
            inc_set[e] = set(s)
            rank[e] = {idx: i + 1 for i, idx in enumerate(s)}
        self.final_counts = {e: len(awaited[e]) for e in ENGS}

        streams = {e: [o for o in self.ops if o["eng"] == e] for e in ENGS}
        wm = {}

        def emit_stream(e, eng_h):
            for o in streams[e]:
                for t in o["waits"]:
                    if t[0] == "e":
                        if t[2] not in rank[t[1]]:
                            continue
                        v = rank[t[1]][t[2]]
                        kk = (e, "e" + t[1])
                        if wm.get(kk, 0) >= v:
                            continue
                        wm[kk] = v
                        eng_h.wait_ge(esems[t[1]], v)
                    else:
                        v = 16 * t[2]
                        kk = (e, "d" + t[1])
                        if wm.get(kk, 0) >= v:
                            continue
                        wm[kk] = v
                        eng_h.wait_ge(dsems[t[1]], v)
                instr = o["fn"](eng_h)
                t = o["tag"]
                if t[0] == "d":
                    instr.then_inc(dsems[t[1]], 16)
                elif t[2] in inc_set[t[1]]:
                    instr.then_inc(esems[t[1]], 1)
        return emit_stream


class Arena:
    def __init__(self, nc, budget_bytes=206000):
        self.nc = nc
        self.budget = budget_bytes
        self.tiles = {}
        self.handle = None
        self.regions = {}
        self.pending = []

    def tile(self, name, shape, dtype, offset):
        per_part = int(np.prod(shape[1:])) * mybir.dt.size(dtype)
        assert offset % 32 == 0, (name, offset)
        assert offset + per_part <= self.budget, (name, offset, per_part, self.budget)
        h = self.nc.virtual_sbuf_tensor(name, list(shape), dtype)
        self.pending.append((h, offset))
        self.tiles[name] = (h, offset, per_part)
        return h

    def allocate(self):
        self.handle = self.nc.sbuf_tensor("arena", [128, self.budget], mybir.dt.uint8)
        self.handle_cm = self.handle
        h = self.handle.__enter__()
        base = self.nc.lookup_mloc(h).addr
        for hnd, off in self.pending:
            self.nc._tensor(hnd.name, hnd.shape, hnd.dtype, type="SB")
            mloc = self.nc.lookup_mloc(hnd)
            mloc.allocated = True
            mloc.addr = base + off
        return h

    def free(self):
        self.handle_cm.__exit__(None, None, None)


def ceil32(x):
    return (x + 31) // 32 * 32


def build(S=2048, budget=212832):
    NCH = S // CH
    SG = min(512, S)
    NSG = S // SG
    nc = bass.Bass("TRN2", target_bir_lowering=False)

    d = {}
    def di(name, shape, dtype=F32, kind="ExternalInput"):
        d[name] = nc.dram_tensor(name, list(shape), dtype, kind=kind).ap()
        return d[name]
    u_tm_d = di("u_tm", [S, DM])
    u_fmT_d = di("u_fmT", [DM, S])
    winT_d = di("winT", [DM, 4256])
    woutT_d = di("woutT", [DI, DM])
    convw_d = di("convw", [128, NFC * 4])
    convb_d = di("convb", [128, NFC])
    dtb_d = di("dtb", [128, 1])
    alog_d = di("alog", [128, 1])
    nwpk_d = di("nwpk", [128, 16])
    dflat_d = di("dflat", [1, DI])
    identf_d = di("identf", [128, 128])
    triu_d = di("triu", [128, 128])
    maskT_d = di("maskT", [128, 128])
    onesf_d = di("onesf", [1, 128])
    wfc_d = di("wfc", [4096, DM])
    wproj_d = di("wproj", [DM, 4096])
    vT_d = di("vT", [DM, 16])
    fbT_d = di("fbT", [16, 16])
    uTlr_d = di("uTlr", [16, 4096])
    gater_d = di("gater", [16, 1])
    out_d = di("out", [S, DM], F32, "ExternalOutput")
    aflat_d = di("acumflat", [NCH, NH * CH], F32, "Internal")
    x1_d = di("x1d", [S, DM], F32, "Internal")
    ternfc_d = di("ternfc", [4096, DM], BF16, "Internal")
    ternproj_d = di("ternproj", [DM, 4096], BF16, "Internal")
    cdf_d = di("cdf", [8, 128], F32, "Internal")

    PERS = 4096
    ar = Arena(nc, budget)
    off = [0]
    def T(name, shape, dtype, at=None):
        if at is not None:
            off[0] = at
        t = ar.tile(name, shape, dtype, off[0])
        off[0] = ceil32(off[0] + int(np.prod(shape[1:])) * mybir.dt.size(dtype))
        return t
    off[0] = budget - PERS
    identf_s = T("identf_s", [128, 128], F32)
    identb_s = T("identb_s", [128, 128], BF16)
    triu_s = T("triu_s", [128, 128], F32)
    maskT_s = T("maskT_s", [128, 128], F32)
    convw_s = T("convw_s", [128, NFC * 4], F32)
    convb_s = T("convb_s", [128, NFC], F32)
    dtb_s = T("dtb_s", [128, 1], F32)
    alog_s = T("alog_s", [128, 1], F32)
    nwpk_s = T("nwpk_s", [128, 16], F32)
    onesf_s = T("onesf_s", [1, 128], F32)
    dumact = T("dumact", [1, 8], F32)
    dumdve = T("dumdve", [1, 8], F32)
    dumpool = T("dumpool", [1, 8], F32)
    cbias = T("cbias", [128, 4], F32)
    ssum = T("ssum", [128, 2], F32)
    xsum = T("xsum", [128, 2], F32)
    x2sum = T("x2sum", [128, 2], F32)
    mstat = T("mstat", [128, 2], F32)
    lnt = T("lnt", [128, 2], F32)
    mm_t = T("mm_t", [128, 2], F32)
    vstat = T("vstat", [128, 2], F32)
    rstd = T("rstd", [128, 2], F32)
    b2 = T("b2", [128, 2], F32)
    nalog = T("nalog", [128, 1], F32)
    scs = T("scs", [128, 2], F32)
    rcs = T("rcs", [128, 1], F32)
    hscs = T("hscs", [128, 1], F32)
    gate_s = T("gate_s", [16, 1], F32)
    assert off[0] <= budget, off[0]
    PERS_USED = off[0] - (budget - PERS)

    off[0] = 0
    winT = T("winT_s", [128, 8 * 4256], BF16)
    woutT = T("woutT_s", [128, 16 * 1024], BF16)
    u_tm = T("u_tm_s", [128, DM], F32)
    u_fm = T("u_fm_s", [128, 8 * 256], BF16)
    sz = [T("sz0", [128, DI], BF16), T("sz1", [128, DI], BF16)]
    xbcraw = T("xbcraw", [128, NFC * 260], BF16)
    convtmp = [T(f"convtmp{i}", [128, 512], F32) for i in range(2)]
    xact_pair = T("xact_pair", [128, 16 * 256], BF16)
    B_s2 = T("B_s2", [64, 256], BF16)
    C_s2 = T("C_s2", [64, 256], BF16)
    B_tm2 = T("B_tm2", [128, 128], BF16)
    subl = T("subl", [128, NCH * 32], F32)
    sdol = T("sdol", [128, NCH * 32], F32)
    coefl = T("coefl", [128, NCH * 32], F32)
    crepl = T("crepl", [128, NCH * 32], F32)
    ddb = T("ddb_t", [128, 16 * CH], F32)
    lt = T("lt", [128, 16 * CH], BF16)
    gtm = T("gtm", [128, 128], BF16)
    xt = [T("xt0", [128, 16 * 64], BF16), T("xt1", [128, 16 * 64], BF16)]
    xtc = [T("xtc0", [128, 16 * 64], BF16), T("xtc1", [128, 16 * 64], BF16)]
    srun_bf = T("srun_bf", [64, DI], BF16)
    srun = T("srun", [128, 1024], F32)
    y1 = T("y1", [128, DI], F32)
    yt = T("yt", [128, DI], BF16)
    drep = T("drep", [128, DI], BF16)
    x1s = T("x1s", [128, DM], F32)
    x1n = T("x1n", [128, DM], F32)
    CEND = off[0]
    cdrow8 = T("cdrow8", [1, 128], F32, at=budget - PERS + ceil32(PERS_USED))
    ynorm = T("ynorm", [128, DI], BF16, at=ar.tiles["xt0"][1])
    acum_pk = T("acum_pk", [128, 1024], F32, at=ar.tiles["u_tm_s"][1])
    subt_pk = T("subt_pk", [128, 1024], F32, at=ar.tiles["sz0"][1])
    coeft_pk = T("coeft_pk", [128, 1024], F32, at=ar.tiles["drep"][1])
    adlh = T("adlh", [128, 256], F32, at=ar.tiles["convtmp0"][1])
    wdt_s = T("wdt_s", [128, 8 * 32], BF16, at=ar.tiles["convtmp1"][1])
    y1_off = ar.tiles["y1"][1]
    dt_pk = T("dt_pk", [128, 1024], F32, at=y1_off)
    lndt_pk = T("lndt_pk", [128, 1024], F32)
    assert off[0] <= ar.tiles["yt"][1], (off[0], ar.tiles["yt"][1])
    ad_pk = T("ad_pk", [128, 1024], F32, at=ar.tiles["srun"][1])
    assert CEND <= budget - PERS, (CEND, budget - PERS)

    off[0] = 0
    wfcT = T("wfcT", [128, 8 * 4096], BF16, at=0)
    wprT = T("wprT", [128, 32 * 1024], BF16)
    EW = off[0]
    x1tm = T("x1tm", [128, 4 * 1024], F32)
    x1bf = T("x1bf", [128, 1024], BF16)
    x1fm = T("x1fm", [128, 8 * SG], BF16)
    r2 = T("r2", [128, 32 * SG], BF16)
    hr = [T(f"hr{i}", [128, SG], BF16) for i in range(2)]
    t1b = T("t1b", [16, SG], BF16)
    t2b = T("t2b", [16, SG], BF16)
    vtb = T("vtb", [128, 8 * 16], BF16)
    fbtb = T("fbtb", [16, 16], BF16)
    uTg = T("uTg", [16, 4096], BF16)
    outst = T("outst", [128, DM], F32)
    EEND = off[0]
    assert EEND <= budget - PERS, EEND
    wst = [T("wst0", [128, 2048], F32, at=ar.tiles["ddb_t"][1]),
           T("wst1", [128, 2048], F32, at=ar.tiles["lt"][1]),
           T("wst2", [128, 2048], F32, at=ar.tiles["xbcraw"][1]),
           T("wst3", [128, 2048], F32, at=ar.tiles["xact_pair"][1])]
    trow = [T("trow0", [128, 2048], BF16, at=ar.tiles["srun_bf"][1]),
            T("trow1", [128, 2048], BF16, at=ar.tiles["x1s"][1])]
    trow2 = T("trow2", [128, 2048], BF16, at=ar.tiles["x1n"][1])
    part1 = T("part1", [128, 256], F32, at=ar.tiles["yt"][1])
    part2 = T("part2", [128, 8], F32)
    chs = T("chs", [128, 2], F32)
    cls = T("cls", [128, 2], F32)
    ncls = T("ncls", [128, 2], F32)
    assert off[0] <= ar.tiles["yt"][1] + 4096

    ar.allocate()

    B = Builder()
    import contextlib
    es = contextlib.ExitStack()
    with es:
        pa = es.enter_context(nc.psum_tensor("pa", [128, 1024], F32))
        pb = es.enter_context(nc.psum_tensor("pb", [128, 1024], F32))
        pt = es.enter_context(nc.psum_tensor("pt", [128, 2048], BF16))
        pe_ = es.enter_context(nc.psum_tensor("pe_", [128, 512], F32))
        pf = es.enter_context(nc.psum_tensor("pf", [128, 512], F32))

        B.nopfns = {
            "pe": (lambda e: e.matmul(pa[0:1, 0:8], identb_s[0:1, 0:1],
                                      identb_s[0:1, 0:8], start=True, stop=True),
                   ("identb",), ("pa0",)),
            "act": (lambda e: e.memzero(dumact[:]), ("dumact",), ("dumact",)),
            "dve": (lambda e: e.memset(dumdve[:], 0.0), ("dumdve",), ("dumdve",)),
            "pool": (lambda e: e.memset(dumpool[:], 0.0), ("dumpool",), ("dumpool",)),
            "sp": (lambda e: e.nop(), (), ()),
        }

        op = B.op

        op("pool", lambda e: e.dma_start(
            out=wdt_s[:].rearrange("p (kc c) -> p kc c", c=32),
            in_=winT_d[:, 4224:4256].rearrange("(kc p) c -> p kc c", p=128)),
           writes=("wdt",), dma="wdt", tag="ld-wdt")
        for nm, dst in [("identf", identf_s), ("triu", triu_s), ("maskT", maskT_s),
                        ("convw", convw_s), ("convb", convb_s), ("dtb", dtb_s),
                        ("alog", alog_s), ("nwpk", nwpk_s), ("onesf", onesf_s)]:
            op("sp", lambda e, a=d[nm], b=dst: e.dma_start(out=b[:], in_=a),
               writes=(nm,), dma="cst", tag=f"ld-{nm}")
        op("pool", lambda e: e.dma_start(out=identb_s[:], in_=d["identf"]),
           writes=("identb",), dma="cst2", tag="ld-identb")
        def emit_bigw():
            for kc in range(8):
                op("pool", lambda e, kc=kc: e.dma_start(
                    out=winT[:, kc * 4256:(kc + 1) * 4256],
                    in_=winT_d.rearrange("(kc p) c -> p kc c", p=128)[:, kc]),
                   writes=(f"winT{kc}",), dma="winT", tag=f"ld-winT{kc}")
            for q in range(4):
                op("pool", lambda e, q=q: e.dma_start(
                    out=woutT[:, q * 4096:(q + 1) * 4096],
                    in_=woutT_d.rearrange("(kc p) c -> p kc c", p=128)
                        [:, 4 * q:4 * (q + 1)]),
                   writes=(f"woutT{q}",), dma="woutT", tag=f"ld-woutT{q}")
        op("act", lambda e: e.activation(out=nalog[:], in_=alog_s[:], func=AF.Exp),
           reads=("alog",), writes=("nalog",), tag="expA")
        op("pool", lambda e: e.tensor_scalar(out=nalog[:], in0=nalog[:], scalar1=-1.0,
                                            scalar2=None, op0=OP.mult),
           writes=("nalog",), tag="negA")
        op("pool", lambda e: e.memset(cbias[:, 0:1], EPS), writes=("cbias",), tag="cb0")
        op("pool", lambda e: e.memset(cbias[:, 1:2], 1.0), writes=("cbias",), tag="cb1")
        op("pool", lambda e: e.memset(cbias[:, 2:3], -0.5), writes=("cbias",), tag="cb2")
        op("pool", lambda e: e.memset(cbias[:, 3:4], 0.5), writes=("cbias",), tag="cb3")

        d_items = script_D(B, nc, d, locals())
        di = [0]

        def emitD(n):
            while n > 0 and di[0] < len(d_items):
                d_items[di[0]]()
                di[0] += 1
                n -= 1

        NPAIR = NCH // 2
        op("pool", lambda e: e.memset(dt_pk[:], 0.0), writes=("dt_pk",), tag="z-dtpk")
        for c in range(NCH):
            bse = 64 * (c % 2)
            col = 128 * (c // 2)
            cp = c % 2
            pp = pe_ if cp == 0 else pf
            bk = "pe" if cp == 0 else "pf"
            op("pool", lambda e, c=c: e.dma_start(
                out=u_fm[:, 0:1024].rearrange("p (kc t) -> p kc t", t=128),
                in_=u_fmT_d[:, c * CH:(c + 1) * CH].rearrange("(kc p) t -> p kc t", p=128)),
               writes=("u_fm",), dma=f"ufm{cp}", tag=f"B-ufm{c}")
            for kc in range(8):
                op("pe", lambda e, kc=kc, pp=pp: e.matmul(
                    pp[0:32, 0:128], wdt_s[:, kc * 32:(kc + 1) * 32],
                    u_fm[:, kc * 128:(kc + 1) * 128],
                    start=(kc == 0), stop=(kc == 7)),
                   reads=("wdt", "u_fm"), writes=(bk,), tag=f"B-dt{c}.{kc}")
            op("act", lambda e, bse=bse, col=col, pp=pp: e.copy(
                out=dt_pk[bse:bse + 32, col:col + 128], in_=pp[0:32, 0:128]),
               reads=(bk,), writes=("dt_pk",), tag=f"B-ev{c}")
            if c == 1:
                emit_bigw()
            emitD(2)
        op("act", lambda e: e.activation(out=lndt_pk[:], in_=dt_pk[:], func=AF.Relu,
                                         bias=dtb_s[:, 0:1], scale=1.0),
           reads=("dt_pk", "dtb"), writes=("lndt_pk",), tag="sp-relu")
        op("act", lambda e: e.activation(out=ad_pk[:], in_=dt_pk[:], func=AF.Abs,
                                         bias=dtb_s[:, 0:1], scale=1.0),
           reads=("dt_pk", "dtb"), writes=("ad_pk",), tag="sp-abs")
        op("act", lambda e: e.activation(out=ad_pk[:], in_=ad_pk[:], func=AF.Exp,
                                         scale=-1.0),
           reads=("ad_pk",), writes=("ad_pk",), tag="sp-exp")
        op("act", lambda e: e.activation(out=ad_pk[:], in_=ad_pk[:], func=AF.Ln,
                                         bias=cbias[:, 1:2], scale=1.0),
           reads=("ad_pk", "cbias"), writes=("ad_pk",), tag="sp-ln")
        op("pool", lambda e: e.tensor_tensor(out=dt_pk[:], in0=lndt_pk[:], in1=ad_pk[:],
                                            op=OP.add),
           reads=("lndt_pk", "ad_pk"), writes=("dt_pk",), tag="sp-add")
        op("act", lambda e: e.activation(out=lndt_pk[:], in_=dt_pk[:], func=AF.Ln),
           reads=("dt_pk",), writes=("lndt_pk",), tag="lndt")
        op("pool", lambda e: e.tensor_tensor(out=ad_pk[:], in0=dt_pk[:],
                                            in1=nalog[:, 0:1].broadcast_to([128, 1024]),
                                            op=OP.mult),
           reads=("dt_pk", "nalog"), writes=("ad_pk",), tag="Ad")
        for j in range(NPAIR):
            col = 128 * j
            jp = j % 2
            ao = 128 * jp
            pp = pe_ if jp == 0 else pf
            bk = "pe" if jp == 0 else "pf"
            op("pe", lambda e, col=col, pp=pp: e.transpose(
                pp[:, 0:128], ad_pk[:, col:col + 128], identf_s[:]),
               reads=("ad_pk", "identf"), writes=(bk,), tag=f"B-tAd{j}")
            op("act", lambda e, ao=ao, pp=pp: e.copy(
                out=adlh[:, ao:ao + 128], in_=pp[:, 0:128]),
               reads=(bk,), writes=(f"adlh{jp}",), tag=f"B-evAd{j}")
            op("pe", lambda e, ao=ao, pp=pp: e.matmul(
                pp[:, 128:256], adlh[:, ao:ao + 128], triu_s[:],
                start=True, stop=True),
               reads=(f"adlh{jp}", "triu"), writes=(bk,), tag=f"B-mh{j}")
            op("act", lambda e, col=col, pp=pp: e.copy(
                out=acum_pk[:, col:col + 128], in_=pp[:, 128:256]),
               reads=(bk,), writes=("acum_pk",), tag=f"B-evh{j}")
            op("pe", lambda e, ao=ao, pp=pp: e.matmul(
                pp[:, 256:384], triu_s[:], adlh[:, ao:ao + 128],
                start=True, stop=True),
               reads=(f"adlh{jp}", "triu"), writes=(bk,), tag=f"B-ml{j}")
            for q2 in range(2):
                op("act", lambda e, j=j, q2=q2, pp=pp: e.activation(
                    out=sdol[:, (2 * j + q2) * 32:(2 * j + q2) * 32 + 32],
                    in_=pp[:, 256 + 64 * q2:256 + 64 * q2 + 32], func=AF.Exp),
                   reads=(bk,), writes=("sdol",), tag=f"B-sdo{j}.{q2}")
            emitD(1)
        op("pool", lambda e: e.tensor_tensor(out=subt_pk[:, 0:NPAIR * 128],
                                            in0=acum_pk[:, 0:NPAIR * 128],
                                            in1=lndt_pk[:, 0:NPAIR * 128],
                                            op=OP.subtract),
           reads=("acum_pk", "lndt_pk"), writes=("subt_pk",), tag="B-sub")
        for j in range(NPAIR):
            col = 128 * j
            jp = j % 2
            pp = pe_ if jp == 0 else pf
            bk = "pe" if jp == 0 else "pf"
            op("act", lambda e, col=col: e.activation(
                out=coeft_pk[:, col:col + 128], in_=subt_pk[:, col:col + 128],
                func=AF.Exp, scale=-1.0, bias=acum_pk[:, col + 127:col + 128]),
               reads=("subt_pk", "acum_pk"), writes=("coeft_pk",), tag=f"B-coef{j}")
            for q2 in range(2):
                c = 2 * j + q2
                op("act", lambda e, c=c, col=col, q2=q2: e.dma_start(
                    out=aflat_d[c:c + 1, :].rearrange("o (h l) -> (o h) l", h=32),
                    in_=acum_pk[64 * q2:64 * q2 + 32, col:col + 128]),
                   reads=("acum_pk",), writes=("aflat",), dma=f"aflat{c % 2}",
                   tag=f"B-fl{c}")
            op("pe", lambda e, col=col, pp=pp: e.transpose(
                pp[:, 0:128], subt_pk[:, col:col + 128], identf_s[:]),
               reads=("subt_pk", "identf"), writes=(bk,), tag=f"B-tsub{j}")
            for q2 in range(2):
                c = 2 * j + q2
                op("act", lambda e, c=c, q2=q2, pp=pp: e.copy(
                    out=subl[:, c * 32:c * 32 + 32],
                    in_=pp[:, 64 * q2:64 * q2 + 32]),
                   reads=(bk,), writes=("subl",), tag=f"B-evsub{c}")
            op("pe", lambda e, col=col, pp=pp: e.transpose(
                pp[:, 128:256], coeft_pk[:, col:col + 128], identf_s[:]),
               reads=("coeft_pk", "identf"), writes=(bk,), tag=f"B-tcoef{j}")
            for q2 in range(2):
                c = 2 * j + q2
                op("act", lambda e, c=c, q2=q2, pp=pp: e.copy(
                    out=coefl[:, c * 32:c * 32 + 32],
                    in_=pp[:, 128 + 64 * q2:128 + 64 * q2 + 32]),
                   reads=(bk,), writes=("coefl",), tag=f"B-evcoef{c}")
            emitD(1)
        op("act", lambda e: e.activation(
            out=coeft_pk[:, 0:NPAIR],
            in_=acum_pk[:, 0:NPAIR * 128].rearrange("p (j l) -> p j l", l=128)[:, :, 127:128],
            func=AF.Exp),
           reads=("acum_pk",), writes=("cdall",), tag="B-cdall")
        op("pe", lambda e: e.transpose(pe_[0:NPAIR, 0:128], coeft_pk[:, 0:NPAIR],
                                       identf_s[:]),
           reads=("cdall", "identf"), writes=("pe",), tag="B-tcd")
        op("act", lambda e: e.copy(out=adlh[0:NPAIR, 0:128], in_=pe_[0:NPAIR, 0:128]),
           reads=("pe",), writes=("cdsb",), tag="B-evcd")
        for j in range(NPAIR):
            jp = j % 2
            pp = pe_ if jp == 0 else pf
            bk = "pe" if jp == 0 else "pf"
            op("act", lambda e, j=j: e.dma_start(out=cdrow8[:],
                                                   in_=adlh[j:j + 1, 0:128]),
               reads=("cdsb",), writes=("cdrow8",), dma="cdf", tag=f"B-cdmv{j}")
            op("pe", lambda e, pp=pp: e.matmul(pp[:, 384:512], onesf_s[:], cdrow8[:],
                                               start=True, stop=True),
               reads=("cdrow8", "onesf"), writes=(bk,), tag=f"B-mcd{j}")
            for q2 in range(2):
                c = 2 * j + q2
                op("act", lambda e, c=c, q2=q2, pp=pp: e.copy(
                    out=crepl[:, c * 32:c * 32 + 32],
                    in_=pp[:, 384 + 64 * q2:384 + 64 * q2 + 32]),
                   reads=(bk,), writes=("crepl",), tag=f"B-evcrep{c}")
        emitD(len(d_items))
        op("dve", lambda e: e.tensor_tensor(
            out=woutT[:].rearrange("p (kc c) -> p kc c", c=1024),
            in0=woutT[:].rearrange("p (kc c) -> p kc c", c=1024),
            in1=nwpk_s[:, :, None].broadcast_to([128, 16, 1024]),
            op=OP.mult),
           reads=("woutT0", "woutT1", "woutT2", "woutT3", "nwpk"),
           writes=("woutT0", "woutT1", "woutT2", "woutT3"), tag="nwfold")

        B.barrier("pre-C")
        op("sp", lambda e: e.dma_start(out=ddb[:, 0:DI],
                                       in_=dflat_d.broadcast_to([128, DI])),
           writes=("ddb",), dma="ddb0", tag="dbcast")
        op("dve", lambda e: e.tensor_copy(out=drep[:], in_=ddb[:, 0:DI]),
           reads=("ddb",), writes=("drep",), tag="drep")
        op("dve", lambda e: e.memset(srun[:], 0.0), writes=("srun",), tag="z-srun")
        op("dve", lambda e: e.memset(
            xbcraw[:].rearrange("p (fc c) -> p fc c", c=260)[:, :, 0:4], 0.0),
           writes=('xbc0', 'xbc1', 'xbc2', 'xbc3', 'xbc4', 'xbc5', 'xbc6', 'xbc7', 'xbc8'), tag="z-halo")
        script_C(B, nc, d, locals(), NCH)
        B.barrier("C-E")
        script_E(B, nc, d, locals(), S, SG, NSG)
        B.barrier("end")

        esems = {e: es.enter_context(nc.semaphore(f"s_{e}")) for e in ENGS}
        dsems = {ch: es.enter_context(nc.semaphore(f"d_{ch}")) for ch in B.dma_n}
        block = es.enter_context(nc.Block())

        emit_stream = B.emit(nc, None, esems, dsems)

        @block.tensor
        def _(e):
            emit_stream("pe", e)

        @block.scalar
        def _(e):
            emit_stream("act", e)

        @block.vector
        def _(e):
            emit_stream("dve", e)

        @block.gpsimd
        def _(e):
            emit_stream("pool", e)

        @block.sync
        def _(e):
            emit_stream("sp", e)

    ar.free()
    return nc


def script_C(B, nc, d, V, NCH):
    op = B.op
    g = lambda n: V[n]
    (pa, pb, pt, pe_, pf) = (g("pa"), g("pb"), g("pt"), g("pe_"), g("pf"))
    winT, woutT, u_tm, u_fm = g("winT"), g("woutT"), g("u_tm"), g("u_fm")
    sz, xbcraw, convtmp = g("sz"), g("xbcraw"), g("convtmp")
    xact_pair, B_s2, C_s2, B_tm2 = g("xact_pair"), g("B_s2"), g("C_s2"), g("B_tm2")
    subl, sdol, coefl, crepl = g("subl"), g("sdol"), g("coefl"), g("crepl")
    ddb, lt, gtm = g("ddb"), g("lt"), g("gtm")
    xt, xtc = g("xt"), g("xtc")
    srun_bf, srun, y1, ynorm, yt = g("srun_bf"), g("srun"), g("y1"), g("ynorm"), g("yt")
    drep, x1s, x1n = g("drep"), g("x1s"), g("x1n")
    identb_s, identf_s, maskT_s = g("identb_s"), g("identf_s"), g("maskT_s")
    convw_s, convb_s = g("convw_s"), g("convb_s")
    ssum, xsum, x2sum = g("ssum"), g("xsum"), g("x2sum")
    mstat, lnt, mm_t, vstat, rstd, b2 = (g("mstat"), g("lnt"), g("mm_t"),
                                          g("vstat"), g("rstd"), g("b2"))
    cbias = g("cbias")
    u_fmT_d, u_tm_d, aflat_d, x1_d = d["u_fmT"], d["u_tm"], d["acumflat"], d["x1d"]
    xr3 = xbcraw[:].rearrange("p (fc c) -> p fc c", c=260)
    PABANK = [("pa0", pa, 0), ("pa1", pa, 512), ("pb0", pb, 0), ("pb1", pb, 512)]
    WK = [f"winT{kc}" for kc in range(8)]
    WOK = ["woutT0", "woutT1", "woutT2", "woutT3"]

    def inproj_xbc_pair(c0):
        for w in range(2):
            for q in range(4):
                fc0 = w * 8 + q * 2
                bk, pp, co = PABANK[q]
                for fc in (fc0, fc0 + 1):
                    cc = co + (fc - fc0) * 256
                    for kc in range(8):
                        op("pe", lambda e, fc=fc, kc=kc, pp=pp, cc=cc: e.matmul(
                            pp[:, cc:cc + 256],
                            winT[:, kc * 4256 + 2048 + fc * 128:
                                 kc * 4256 + 2048 + (fc + 1) * 128],
                            u_fm[:, kc * 256:(kc + 1) * 256],
                            start=(kc == 0), stop=(kc == 7)),
                           reads=(WK[kc], f"ufm{kc // 4}"), writes=(bk,),
                           tag=f"C{c0}-x{fc}.{kc}")
                op("act", lambda e, fc0=fc0, pp=pp, co=co: e.copy(
                    out=xr3[:, fc0:fc0 + 2, 4:260], in_=pp[:, co:co + 512]),
                   reads=(bk,), writes=(f"xbc{fc0 // 2}",), tag=f"C{c0}-xev{fc0}")
        for kc in range(8):
            op("pe", lambda e, kc=kc: e.matmul(
                pe_[:, 0:256], winT[:, kc * 4256 + 2048 + 16 * 128:
                                    kc * 4256 + 2048 + 17 * 128],
                u_fm[:, kc * 256:(kc + 1) * 256],
                start=(kc == 0), stop=(kc == 7)),
               reads=(WK[kc], f"ufm{kc // 4}"), writes=("pe",), tag=f"C{c0}-x16.{kc}")
        op("act", lambda e: e.copy(out=xr3[:, 16:17, 4:260], in_=pe_[:, 0:256]),
           reads=("pe",), writes=("xbc8",), tag=f"C{c0}-xev16")

    def inproj_z(c):
        par = c % 2
        for gi, (bk, pp, co) in enumerate(PABANK):
            for kc in range(8):
                op("pe", lambda e, gi=gi, kc=kc, pp=pp, co=co, par=par: e.matmul(
                    pp[:, co:co + 512],
                    u_fm[:, kc * 256 + par * 128:kc * 256 + par * 128 + 128],
                    winT[:, kc * 4256 + gi * 512:kc * 4256 + (gi + 1) * 512],
                    start=(kc == 0), stop=(kc == 7)),
                   reads=(WK[kc], f"ufm{kc // 4}"), writes=(bk,),
                   tag=f"C{c}-z{gi}.{kc}")
            op("act", lambda e, gi=gi, pp=pp, co=co, par=par: e.activation(
                out=sz[par][:, gi * 512:(gi + 1) * 512], in_=pp[:, co:co + 512],
                func=AF.Silu),
               reads=(bk,), writes=(f"sz{par}",), tag=f"C{c}-szev{gi}")

    def conv_pair(c0):
        for gg in range(9):
            fcs = [2 * gg, 2 * gg + 1] if gg < 8 else [16]
            ck = f"convtmp{gg % 2}"
            for j in range(4):
                for fc in fcs:
                    q = (fc % 2) * 256
                    if j == 0:
                        op("dve", lambda e, fc=fc, q=q, gg=gg: e.scalar_tensor_tensor(
                            out=convtmp[gg % 2][:, q:q + 256], in0=xr3[:, fc, 1:257],
                            scalar=convw_s[:, fc * 4:fc * 4 + 1],
                            in1=convb_s[:, fc:fc + 1].broadcast_to([128, 256]),
                            op0=OP.mult, op1=OP.add),
                           reads=(f"xbc{gg}", "convw", "convb"), writes=(ck,),
                           tag=f"C{c0}-cv{fc}.0")
                    else:
                        op("dve", lambda e, fc=fc, j=j, q=q, gg=gg: e.scalar_tensor_tensor(
                            out=convtmp[gg % 2][:, q:q + 256],
                            in0=xr3[:, fc, 1 + j:257 + j],
                            scalar=convw_s[:, fc * 4 + j:fc * 4 + j + 1],
                            in1=convtmp[gg % 2][:, q:q + 256], op0=OP.mult, op1=OP.add),
                           reads=(f"xbc{gg}", "convw", ck), writes=(ck,),
                           tag=f"C{c0}-cv{fc}.{j}")
            if gg < 8:
                op("act", lambda e, gg=gg: e.activation(
                    out=xact_pair[:, gg * 512:(gg + 1) * 512], in_=convtmp[gg % 2][:],
                    func=AF.Silu),
                   reads=(ck,), writes=(f"xact{gg}",), tag=f"C{c0}-cs{gg}")
            else:
                op("act", lambda e: e.activation(
                    out=B_s2[:], in_=convtmp[0][0:64, 0:256], func=AF.Silu),
                   reads=("convtmp0",), writes=("Bs",), tag=f"C{c0}-csB")
                op("act", lambda e: e.activation(
                    out=C_s2[:], in_=convtmp[0][64:128, 0:256], func=AF.Silu),
                   reads=("convtmp0",), writes=("Cs",), tag=f"C{c0}-csC")
        op("dve", lambda e: e.tensor_copy(out=xr3[:, :, 1:4], in_=xr3[:, :, 257:260]),
           reads=('xbc0', 'xbc1', 'xbc2', 'xbc3', 'xbc4', 'xbc5', 'xbc6', 'xbc7', 'xbc8'), writes=('xbc0', 'xbc1', 'xbc2', 'xbc3', 'xbc4', 'xbc5', 'xbc6', 'xbc7', 'xbc8'), tag=f"C{c0}-halo")

    def ufm_load(c):
        for hf in range(2):
            op("pool", lambda e, c=c, hf=hf: e.dma_start(
                out=u_fm[:, hf * 1024:(hf + 1) * 1024]
                    .rearrange("p (kc t) -> p kc t", t=256),
                in_=u_fmT_d[hf * 512:(hf + 1) * 512, c * CH:(c + 2) * CH]
                    .rearrange("(kc p) t -> p kc t", p=128)),
               writes=(f"ufm{hf}",), dma=f"ufm{hf}", tag=f"C{c}-ufm{hf}")

    def decay_stage(c):
        op("sp", lambda e, c=c: e.dma_start(
            out=ddb[:], in_=aflat_d[c:c + 1, 0:2048].broadcast_to([128, 2048])),
           reads=("aflat",), writes=("ddb",), dma="ddb0", tag=f"C{c}-db0")
        for hh in range(2):
            op("dve", lambda e, c=c, hh=hh: e.tensor_tensor(
                out=ddb[:].rearrange("p (h l) -> p h l", l=CH),
                in0=ddb[:].rearrange("p (h l) -> p h l", l=CH),
                in1=subl[:, c * 32 + hh * 16:c * 32 + (hh + 1) * 16][:, :, None]
                    .broadcast_to([128, 16, CH]),
                op=OP.subtract),
               reads=("ddb", "subl"), writes=("ddb",), tag=f"C{c}-sub{hh}")
            if hh == 0:
                op("act", lambda e: e.activation(out=lt[:], in_=ddb[:],
                                                 func=AF.Exp),
                   reads=("ddb",), writes=("lt",), tag=f"C{c}-exp0")
                op("sp", lambda e, c=c: e.dma_start(
                    out=ddb[:],
                    in_=aflat_d[c:c + 1, 2048:4096].broadcast_to([128, 2048])),
                   reads=("aflat",), writes=("ddb",), dma="ddb1", tag=f"C{c}-db1")
            else:
                op("act", lambda e: e.activation(out=ddb[:], in_=ddb[:],
                                                 func=AF.Exp),
                   reads=("ddb",), writes=("ddb",), tag=f"C{c}-exp1")

    def chunk_stage(c):
        par = c % 2
        op("sp", lambda e, c=c: e.dma_start(
            out=u_tm[:], in_=u_tm_d[c * CH:(c + 1) * CH, :]),
           writes=("u_tm",), dma="utm", tag=f"C{c}-utm")

        op("pe", lambda e, par=par: e.transpose(
            pt[:, 0:64], B_s2[:, par * 128:par * 128 + 128][0:64, :]
            if False else B_s2[0:64, par * 128:par * 128 + 128],
            identb_s[0:64, 0:64]),
           reads=("Bs", "identb"), writes=("pt0",), tag=f"C{c}-tB")
        op("dve", lambda e, par=par: e.tensor_copy(
            out=B_tm2[:, par * 64:par * 64 + 64], in_=pt[:, 0:64]),
           reads=("pt0",), writes=("Btm",), tag=f"C{c}-evB")
        op("pe", lambda e, par=par: e.matmul(
            pe_[:, 128:256], B_s2[0:64, par * 128:par * 128 + 128],
            C_s2[0:64, par * 128:par * 128 + 128], start=True, stop=True),
           reads=("Bs", "Cs"), writes=("pe",), tag=f"C{c}-GT")
        op("dve", lambda e: e.tensor_tensor(out=gtm[:], in0=pe_[:, 128:256],
                                            in1=maskT_s[:], op=OP.mult),
           reads=("pe", "maskT"), writes=("gtm",), tag=f"C{c}-GTm")
        op("act", lambda e: e.copy(out=srun_bf[:, 0:1024], in_=srun[0:64, :]),
           reads=("srun",), writes=("srunbf",), tag=f"C{c}-stc0")
        op("act", lambda e: e.copy(out=srun_bf[:, 1024:2048], in_=srun[64:128, :]),
           reads=("srun",), writes=("srunbf",), tag=f"C{c}-stc1")

        for hh in range(2):
            ptk = f"pt{hh}"
            ptco = hh * 1024
            for i in range(8):
                op("pe", lambda e, i=i, hh=hh, par=par, ptco=ptco: e.transpose(
                    pt[:, ptco + i * 128:ptco + (i + 1) * 128],
                    xact_pair[:, (hh * 8 + i) * 256 + par * 128:
                              (hh * 8 + i) * 256 + par * 128 + 128],
                    identb_s[:]),
                   reads=(f"xact{(hh * 8 + i) // 2}", "identb"), writes=(ptk,),
                   tag=f"C{c}-tx{hh}.{i}")
            op("act", lambda e, ptco=ptco, hh=hh: e.copy(
                out=xt[hh][:], in_=pt[:, ptco:ptco + 1024]),
               reads=(ptk,), writes=(f"xt{hh}",), tag=f"C{c}-evx{hh}")
            op("dve", lambda e, hh=hh: e.tensor_tensor(
                out=xtc[hh][:], in0=xt[hh][:],
                in1=drep[:, hh * 1024:(hh + 1) * 1024], op=OP.mult),
               reads=(f"xt{hh}", "drep"), writes=(f"xtc{hh}",), tag=f"C{c}-xtd{hh}")
            for j in range(2):
                op("pe", lambda e, j=j, hh=hh, par=par: e.matmul(
                    pa[:, j * 512:(j + 1) * 512],
                    C_s2[0:64, par * 128:par * 128 + 128],
                    srun_bf[:, hh * 1024 + j * 512:hh * 1024 + (j + 1) * 512],
                    start=True, stop=True),
                   reads=("Cs", "srunbf"), writes=(f"pa{j}",), tag=f"C{c}-yo{hh}.{j}")
            if hh == 0:
                op("dve", lambda e: e.scalar_tensor_tensor(
                    out=lt[:].rearrange("p (h l) -> p h l", l=CH),
                    in0=lt[:].rearrange("p (h l) -> p h l", l=CH), scalar=1e30,
                    in1=gtm[:, None, :].broadcast_to([128, 16, CH]),
                    op0=OP.min, op1=OP.mult),
                   reads=("lt", "gtm"), writes=("lt",), tag=f"C{c}-stt{hh}")
            else:
                op("dve", lambda e: e.scalar_tensor_tensor(
                    out=lt[:].rearrange("p (h l) -> p h l", l=CH),
                    in0=ddb[:].rearrange("p (h l) -> p h l", l=CH), scalar=1e30,
                    in1=gtm[:, None, :].broadcast_to([128, 16, CH]),
                    op0=OP.min, op1=OP.mult),
                   reads=("ddb", "gtm"), writes=("lt",), tag=f"C{c}-stt{hh}")
            for jb in range(2):
                op("pe", lambda e, hh=hh, jb=jb: e.matmul(
                    pb[:, jb * 512:(jb + 1) * 512], identb_s[:],
                    xtc[hh][:, jb * 512:(jb + 1) * 512],
                    start=True, stop=False),
                   reads=("identb", f"xtc{hh}"), writes=(f"pb{jb}",),
                   tag=f"C{c}-ydx{hh}.{jb}")
            for h in range(16):
                bk = "pb0" if h < 8 else "pb1"
                co = (h % 8) * 64 + (h // 8) * 512
                op("pe", lambda e, h=h, co=co, hh=hh: e.matmul(
                    pb[:, co:co + 64], lt[:, h * 128:(h + 1) * 128],
                    xt[hh][:, h * 64:(h + 1) * 64], start=False, stop=True),
                   reads=("lt", f"xt{hh}"), writes=(bk,), tag=f"C{c}-ydg{hh}.{h}")
            op("dve", lambda e, c=c, hh=hh: e.tensor_tensor(
                out=xtc[hh][:].rearrange("p (h q) -> p h q", q=64),
                in0=xt[hh][:].rearrange("p (h q) -> p h q", q=64),
                in1=coefl[:, c * 32 + hh * 16:c * 32 + (hh + 1) * 16][:, :, None]
                    .broadcast_to([128, 16, 64]),
                op=OP.mult),
               reads=(f"xt{hh}", "coefl"), writes=(f"xtc{hh}",), tag=f"C{c}-xtc{hh}")
            for j, (bk2, pp2) in enumerate([("pe", pe_), ("pf", pf)]):
                op("pe", lambda e, j=j, par=par, pp2=pp2, hh=hh: e.matmul(
                    pp2[0:64, 0:512], B_tm2[:, par * 64:par * 64 + 64],
                    xtc[hh][:, j * 512:(j + 1) * 512], start=True, stop=True),
                   reads=("Btm", f"xtc{hh}"), writes=(bk2,), tag=f"C{c}-sc{hh}.{j}")
            op("dve", lambda e, c=c, hh=hh: e.tensor_tensor(
                out=y1[:, hh * 1024:(hh + 1) * 1024].rearrange("p (h q) -> p h q", q=64),
                in0=pa[:].rearrange("p (h q) -> p h q", q=64),
                in1=sdol[:, c * 32 + hh * 16:c * 32 + (hh + 1) * 16][:, :, None]
                    .broadcast_to([128, 16, 64]),
                op=OP.mult),
               reads=("pa0", "pa1", "sdol"), writes=("y1",), tag=f"C{c}-y1{hh}")
            op("dve", lambda e, hh=hh: e.tensor_tensor(
                out=y1[:, hh * 1024:(hh + 1) * 1024],
                in0=y1[:, hh * 1024:(hh + 1) * 1024], in1=pb[:], op=OP.add),
               reads=("pb0", "pb1", "y1"), writes=("y1",), tag=f"C{c}-y2{hh}")
            op("dve", lambda e, c=c, hh=hh: e.tensor_tensor(
                out=srun[hh * 64:(hh + 1) * 64, :].rearrange("p (h q) -> p h q", q=64),
                in0=srun[hh * 64:(hh + 1) * 64, :].rearrange("p (h q) -> p h q", q=64),
                in1=crepl[hh * 64:(hh + 1) * 64,
                          c * 32 + hh * 16:c * 32 + (hh + 1) * 16][:, :, None]
                    .broadcast_to([64, 16, 64]),
                op=OP.mult),
               reads=("srun", "crepl", "srunbf"), writes=("srun",), tag=f"C{c}-sd{hh}")
            op("dve", lambda e, hh=hh: e.tensor_tensor(
                out=srun[hh * 64:(hh + 1) * 64, 0:512],
                in0=srun[hh * 64:(hh + 1) * 64, 0:512], in1=pe_[0:64, 0:512],
                op=OP.add),
               reads=("srun", "pe"), writes=("srun",), tag=f"C{c}-sa{hh}.0")
            op("dve", lambda e, hh=hh: e.tensor_tensor(
                out=srun[hh * 64:(hh + 1) * 64, 512:1024],
                in0=srun[hh * 64:(hh + 1) * 64, 512:1024], in1=pf[0:64, 0:512],
                op=OP.add),
               reads=("srun", "pf"), writes=("srun",), tag=f"C{c}-sa{hh}.1")

        op("dve", lambda e, par=par: e.tensor_tensor(out=y1[:], in0=y1[:],
                                                     in1=sz[par][:], op=OP.mult),
           reads=("y1", f"sz{par}"), writes=("y1",), tag=f"C{c}-gate")
        op("act", lambda e, par=par: e.activation(out=ynorm[:], in_=y1[:], func=AF.Square,
                                         accum_out=ssum[:, par:par + 1]),
           reads=("y1",), writes=("xt0", "xt1", f"ssum{par}"), tag=f"C{c}-sq")
        op("act", lambda e, par=par: e.activation(out=lnt[:, par:par + 1],
                                         in_=ssum[:, par:par + 1], func=AF.Ln,
                                         scale=1.0 / DI, bias=cbias[:, 0:1]),
           reads=(f"ssum{par}", "cbias"), writes=(f"lnt{par}",), tag=f"C{c}-ln")
        op("act", lambda e, par=par: e.activation(out=rstd[:, par:par + 1],
                                         in_=lnt[:, par:par + 1], func=AF.Exp,
                                         scale=-0.5),
           reads=(f"lnt{par}",), writes=(f"rstd{par}",), tag=f"C{c}-rs")
        op("dve", lambda e, par=par: e.tensor_scalar(out=ynorm[:], in0=y1[:],
                                            scalar1=rstd[:, par:par + 1], scalar2=None,
                                            op0=OP.mult),
           reads=("y1", f"rstd{par}"), writes=("xt0", "xt1"), tag=f"C{c}-yn")
        for i in range(16):
            ptk = "pt0" if i < 8 else "pt1"
            op("pe", lambda e, i=i: e.transpose(
                pt[:, i * 128:(i + 1) * 128], ynorm[:, i * 128:(i + 1) * 128],
                identb_s[:]),
               reads=("xt0" if i < 8 else "xt1", "identb"), writes=(ptk,),
               tag=f"C{c}-ty{i}")
        op("act", lambda e: e.copy(out=yt[:, 0:1024], in_=pt[:, 0:1024]),
           reads=("pt0",), writes=("yt",), tag=f"C{c}-evy0")
        op("act", lambda e: e.copy(out=yt[:, 1024:2048], in_=pt[:, 1024:2048]),
           reads=("pt1",), writes=("yt",), tag=f"C{c}-evy1")
        for nh in range(2):
            for kc in range(16):
                op("pe", lambda e, nh=nh, kc=kc: e.matmul(
                    pa[:, nh * 512:(nh + 1) * 512], yt[:, kc * 128:(kc + 1) * 128],
                    woutT[:, kc * 1024 + nh * 512:kc * 1024 + (nh + 1) * 512],
                    start=(kc == 0), stop=(kc == 15)),
                   reads=("yt", WOK[kc // 4]), writes=(f"pa{nh}",), tag=f"C{c}-op{nh}.{kc}")
        op("dve", lambda e, par=par: e.scalar_tensor_tensor(
            out=x1s[:], in0=pa[:], scalar=1.0, in1=u_tm[:], op0=OP.mult, op1=OP.add,
            accum_out=xsum[:, par:par + 1]),
           reads=("pa0", "pa1", "u_tm"), writes=("x1s", f"xsum{par}"), tag=f"C{c}-res")
        op("act", lambda e, par=par: e.activation(out=ynorm[:, 0:1024], in_=x1s[:],
                                         func=AF.Square, accum_out=x2sum[:, par:par + 1]),
           reads=("x1s",), writes=("xt0", f"x2sum{par}"), tag=f"C{c}-sq1")
        op("dve", lambda e, par=par: e.tensor_scalar(out=mstat[:, par:par + 1],
                                            in0=xsum[:, par:par + 1],
                                            scalar1=1.0 / DM, scalar2=None, op0=OP.mult),
           reads=(f"xsum{par}",), writes=(f"mstat{par}",), tag=f"C{c}-m")
        op("dve", lambda e, par=par: e.tensor_tensor(out=mm_t[:, par:par + 1],
                                            in0=mstat[:, par:par + 1],
                                            in1=mstat[:, par:par + 1], op=OP.mult),
           reads=(f"mstat{par}",), writes=(f"mm_t{par}",), tag=f"C{c}-mm")
        op("dve", lambda e, par=par: e.scalar_tensor_tensor(
            out=vstat[:, par:par + 1], in0=x2sum[:, par:par + 1], scalar=1.0 / DM,
            in1=mm_t[:, par:par + 1], op0=OP.mult, op1=OP.subtract),
           reads=(f"x2sum{par}", f"mm_t{par}"), writes=(f"vstat{par}",), tag=f"C{c}-v")
        op("act", lambda e, par=par: e.activation(out=lnt[:, par:par + 1],
                                         in_=vstat[:, par:par + 1], func=AF.Ln,
                                         bias=cbias[:, 0:1]),
           reads=(f"vstat{par}", "cbias"), writes=(f"lnt{par}",), tag=f"C{c}-lnv")
        op("act", lambda e, par=par: e.activation(out=rstd[:, par:par + 1],
                                         in_=lnt[:, par:par + 1], func=AF.Exp,
                                         scale=-0.5),
           reads=(f"lnt{par}",), writes=(f"rstd{par}",), tag=f"C{c}-rs1")
        op("dve", lambda e, par=par: e.scalar_tensor_tensor(
            out=b2[:, par:par + 1], in0=mstat[:, par:par + 1], scalar=-1.0,
            in1=rstd[:, par:par + 1], op0=OP.mult, op1=OP.mult),
           reads=(f"mstat{par}", f"rstd{par}"), writes=(f"b2{par}",), tag=f"C{c}-b2")
        op("act", lambda e, par=par: e.activation(out=x1n[:], in_=x1s[:], func=AF.Identity,
                                         scale=rstd[:, par:par + 1], bias=b2[:, par:par + 1]),
           reads=("x1s", f"rstd{par}", f"b2{par}"), writes=("x1n",), tag=f"C{c}-x1n")
        op("act", lambda e, c=c: e.dma_start(out=x1_d[c * CH:(c + 1) * CH, :],
                                              in_=x1n[:]),
           reads=("x1n",), writes=("x1d",), dma="x1st", tag=f"C{c}-x1st")

    ufm_load(0)
    inproj_xbc_pair(0)
    inproj_z(0)
    inproj_z(1)
    for jj in range(NCH // 2):
        c0 = 2 * jj
        conv_pair(c0)
        decay_stage(c0)
        if c0 + 2 < NCH:
            ufm_load(c0 + 2)
            inproj_xbc_pair(c0 + 2)
        chunk_stage(c0)
        if c0 + 2 < NCH:
            inproj_z(c0 + 2)
        decay_stage(c0 + 1)
        chunk_stage(c0 + 1)
        if c0 + 2 < NCH:
            inproj_z(c0 + 3)


def script_D(B, nc, d, V):
    op = B.op
    g = lambda n: V[n]
    wst, trow, trow2 = g("wst"), g("trow"), g("trow2")
    part1, part2 = g("part1"), g("part2")
    scs, chs, cls, ncls = g("scs"), g("chs"), g("cls"), g("ncls")
    wfc_d, wproj_d = d["wfc"], d["wproj"]
    ternfc_d, ternproj_d = d["ternfc"], d["ternproj"]
    items = []
    kctr = [0]
    tctr = [0]

    def emit_load(src_ap, width, tag):
        b = kctr[0] % 4
        kctr[0] += 1
        op("sp", lambda e: e.dma_start(out=wst[b][:, 0:width], in_=src_ap),
           writes=(f"wst{b}",), dma=f"wst{b}", tag=tag)
        return b

    def emit_r1(b, width, p1sl, p1k, tag, eng="dve"):
        op(eng, lambda e: e.tensor_reduce(
            out=part1[:, p1sl],
            in_=wst[b][:, 0:width].rearrange("p (a b) -> p a b", b=32),
            axis=AXX, op=OP.add, apply_absolute_value=True),
           reads=(f"wst{b}",), writes=(p1k,), tag=tag)

    def emit_scale(par, width, tagp):
        sk = f"scs{par}"
        sl_ = slice(par, par + 1)
        op("dve", lambda e: e.tensor_scalar(out=scs[:, sl_], in0=scs[:, sl_],
                                            scalar1=1.0 / width, scalar2=1e-5,
                                            op0=OP.mult, op1=OP.max),
           reads=(sk,), writes=(sk,), tag=f"{tagp}-mc")
        op("dve", lambda e: e.tensor_scalar(out=chs[:, sl_], in0=scs[:, sl_],
                                            scalar1=0.5, scalar2=None, op0=OP.mult),
           reads=(sk,), writes=(f"chs{par}",), tag=f"{tagp}-ch")
        op("dve", lambda e: e.tensor_scalar(out=cls[:, sl_], in0=scs[:, sl_],
                                            scalar1=2.0 ** -25, scalar2=None,
                                            op0=OP.mult),
           reads=(sk,), writes=(f"cls{par}",), tag=f"{tagp}-cl")
        op("dve", lambda e: e.tensor_scalar(out=ncls[:, sl_], in0=cls[:, sl_],
                                            scalar1=-1.0, scalar2=None, op0=OP.mult),
           reads=(f"cls{par}",), writes=(f"ncls{par}",), tag=f"{tagp}-ncl")

    def emit_thresh(bref, par, width, dst_ap, wkey, tagp):
        t = tctr[0] % 2
        tctr[0] += 1
        b = bref[0]
        sk = f"scs{par}"
        sl_ = slice(par, par + 1)
        wk = f"wst{b}"
        tk = f"trow{t}"
        op("dve", lambda e: e.tensor_scalar(out=trow[t][:, 0:width],
                                            in0=wst[b][:, 0:width],
                                            scalar1=chs[:, sl_], scalar2=cls[:, sl_],
                                            op0=OP.subtract, op1=OP.is_gt),
           reads=(wk, f"chs{par}", f"cls{par}"), writes=(tk,), tag=f"{tagp}-tp")
        op("dve", lambda e: e.tensor_scalar(out=trow2[:, 0:width],
                                            in0=wst[b][:, 0:width],
                                            scalar1=chs[:, sl_], scalar2=ncls[:, sl_],
                                            op0=OP.add, op1=OP.is_lt),
           reads=(wk, f"chs{par}", f"ncls{par}"), writes=("trow2",),
           tag=f"{tagp}-tn")
        op("dve", lambda e: e.tensor_tensor(out=trow[t][:, 0:width],
                                            in0=trow[t][:, 0:width],
                                            in1=trow2[:, 0:width],
                                            op=OP.subtract),
           reads=(tk, "trow2"), writes=(tk,), tag=f"{tagp}-sub")
        op("act", lambda e: e.activation(out=trow[t][:, 0:width],
                                         in_=trow[t][:, 0:width],
                                         func=AF.Identity, scale=scs[:, sl_]),
           reads=(tk, sk), writes=(tk,), tag=f"{tagp}-sc")
        op("act", lambda e: e.dma_start(out=dst_ap, in_=trow[t][:, 0:width]),
           reads=(tk,), writes=(wkey,), dma=f"tern{t}", tag=f"{tagp}-st")

    def fc_item(i):
        par = i % 2
        po = par * 128
        p1k = f"part1{par}"
        sl_ = slice(par, par + 1)

        def run():
            b = emit_load(wfc_d[i * 128:(i + 1) * 128, :], 1024, f"ternfc{i}-ld")
            emit_r1(b, 1024, slice(po, po + 32), p1k, f"ternfc{i}-r1")
            op("dve", lambda e: e.tensor_reduce(
                out=scs[:, sl_], in_=part1[:, po:po + 32], axis=AXX, op=OP.add),
               reads=(p1k,), writes=(f"scs{par}",), tag=f"ternfc{i}-r3")
            emit_scale(par, 1024, f"ternfc{i}")
            emit_thresh([b], par, 1024, ternfc_d[i * 128:(i + 1) * 128, :],
                        "ternfc", f"ternfc{i}")
        return run

    def proj_items(i):
        par = i % 2
        po = par * 128
        p1k = f"part1{par}"
        sl_ = slice(par, par + 1)
        bufA = [None]
        bufB = [None]

        def runA():
            bufA[0] = emit_load(wproj_d[i * 128:(i + 1) * 128, 0:2048], 2048,
                                f"ternproj{i}a-ld")
            emit_r1(bufA[0], 2048, slice(po, po + 64), p1k, f"ternproj{i}a-r1")

        def runB():
            bufB[0] = emit_load(wproj_d[i * 128:(i + 1) * 128, 2048:4096], 2048,
                                f"ternproj{i}b-ld")
            emit_r1(bufB[0], 2048, slice(po + 64, po + 128), p1k,
                    f"ternproj{i}b-r1")
            op("dve", lambda e: e.tensor_reduce(
                out=part2[:, par * 4:par * 4 + 4],
                in_=part1[:, po:po + 128].rearrange("p (a b) -> p a b", b=32),
                axis=AXX, op=OP.add),
               reads=(p1k,), writes=(f"part2{par}",), tag=f"ternproj{i}-r2")
            op("dve", lambda e: e.tensor_reduce(
                out=scs[:, sl_], in_=part2[:, par * 4:par * 4 + 4],
                axis=AXX, op=OP.add),
               reads=(f"part2{par}",), writes=(f"scs{par}",), tag=f"ternproj{i}-r3")
            emit_scale(par, 4096, f"ternproj{i}")
            emit_thresh(bufA, par, 2048,
                        ternproj_d[i * 128:(i + 1) * 128, 0:2048],
                        "ternproj", f"ternproj{i}a")

        def runC():
            emit_thresh(bufB, par, 2048,
                        ternproj_d[i * 128:(i + 1) * 128, 2048:4096],
                        "ternproj", f"ternproj{i}b")
        return [runA, runB, runC]

    for i in range(32):
        items.append(fc_item(i))
    for i in range(8):
        items.extend(proj_items(i))
    return items


def script_E(B, nc, d, V, S, SG, NSG):
    op = B.op
    g = lambda n: V[n]
    (pa, pb, pt, pe_, pf) = (g("pa"), g("pb"), g("pt"), g("pe_"), g("pf"))
    wfcT, wprT = g("wfcT"), g("wprT")
    x1tm, x1bf, x1fm, r2, hr = g("x1tm"), g("x1bf"), g("x1fm"), g("r2"), g("hr")
    t1b, t2b, vtb, fbtb, uTg = g("t1b"), g("t2b"), g("vtb"), g("fbtb"), g("uTg")
    outst = g("outst")
    identb_s, gate_s = g("identb_s"), g("gate_s")
    xsum, x2sum = g("xsum"), g("x2sum")
    mstat, lnt, mm_t, vstat, rstd, b2 = (g("mstat"), g("lnt"), g("mm_t"),
                                          g("vstat"), g("rstd"), g("b2"))
    cbias = g("cbias")
    x1_d, out_d = d["x1d"], d["out"]
    ternfc_d, ternproj_d = d["ternfc"], d["ternproj"]
    vT_d, fbT_d, uTlr_d, gater_d = d["vT"], d["fbT"], d["uTlr"], d["gater"]
    NT = SG // 128
    PABANK = [("pa0", pa, 0, 0), ("pa1", pa, 512, 0), ("pe", pe_, 0, 1), ("pf", pf, 0, 1)]

    def x1block(sg):
        for tg in range(NT):
            op("sp", lambda e, sg=sg, tg=tg: e.dma_start(
                out=x1tm[:, tg * 1024:(tg + 1) * 1024],
                in_=x1_d[sg * SG + tg * 128:sg * SG + (tg + 1) * 128, :]),
               writes=(f"x1tm{tg}",), dma=f"x1ld{tg % 2}", tag=f"E{sg}-x1ld{tg}")
            op("dve", lambda e, tg=tg: e.tensor_copy(
                out=x1bf[:], in_=x1tm[:, tg * 1024:(tg + 1) * 1024]),
               reads=(f"x1tm{tg}",), writes=("x1bf",), tag=f"E{sg}-cast{tg}")
            for i in range(8):
                op("pe", lambda e, i=i: e.transpose(
                    pt[:, i * 128:(i + 1) * 128], x1bf[:, i * 128:(i + 1) * 128],
                    identb_s[:]),
                   reads=("x1bf", "identb"), writes=("pt0",), tag=f"E{sg}-t{tg}.{i}")
            op("dve", lambda e, tg=tg: e.tensor_copy(
                out=x1fm[:].rearrange("p (kc t) -> p kc t", t=SG)
                    [:, :, tg * 128:(tg + 1) * 128],
                in_=pt[:, 0:1024].rearrange("p (kc t) -> p kc t", t=128)),
               reads=("pt0",), writes=("x1fm",), tag=f"E{sg}-evt{tg}")

    x1block(0)
    for k in range(8):
        op("sp", lambda e, k=k: e.dma_start(
            out=wfcT[:, k * 4096:(k + 1) * 4096],
            in_=ternfc_d[0:4096, k * 128:(k + 1) * 128], transpose=True),
           reads=("ternfc",), writes=(f"wfcT{k}",), dma=f"wef{k%8}", tag=f"E-wfc{k}")
    for k in range(32):
        op("sp", lambda e, k=k: e.dma_start(
            out=wprT[:, k * 1024:(k + 1) * 1024],
            in_=ternproj_d[0:1024, k * 128:(k + 1) * 128], transpose=True),
           reads=("ternproj",), writes=(f"wprT{k//8}",), dma=f"wef{k%8}", tag=f"E-wpr{k}")
    op("pool", lambda e: e.dma_start(
        out=vtb[:].rearrange("p (kc j) -> p kc j", j=16),
        in_=vT_d.rearrange("(kc p) j -> p kc j", p=128)),
       writes=("vtb",), dma="lr", tag="E-vtb")
    op("pool", lambda e: e.dma_start(out=fbtb[:], in_=fbT_d),
       writes=("fbtb",), dma="lr", tag="E-fbtb")
    op("pool", lambda e: e.dma_start(out=uTg[:], in_=uTlr_d),
       writes=("uTg",), dma="lr", tag="E-uTg")
    op("sp", lambda e: e.dma_start(out=gate_s[:], in_=gater_d),
       writes=("gate",), dma="cst", tag="E-gate")
    op("dve", lambda e: e.tensor_scalar(out=uTg[:], in0=uTg[:],
                                        scalar1=gate_s[:, 0:1], scalar2=None,
                                        op0=OP.mult),
       reads=("uTg", "gate"), writes=("uTg",), tag="E-uTgs")

    for sg in range(NSG):
        if sg > 0:
            x1block(sg)
        for kc in range(8):
            op("pe", lambda e, kc=kc: e.matmul(
                pe_[0:16, 0:SG], vtb[:, kc * 16:(kc + 1) * 16],
                x1fm[:, kc * SG:(kc + 1) * SG], start=(kc == 0), stop=(kc == 7)),
               reads=("vtb", "x1fm"), writes=("pe",), tag=f"E{sg}-t1.{kc}")
        op("act", lambda e: e.copy(out=t1b[:], in_=pe_[0:16, 0:SG]),
           reads=("pe",), writes=("t1b",), tag=f"E{sg}-evt1")
        op("pe", lambda e: e.matmul(pf[0:16, 0:SG], fbtb[:], t1b[:],
                                    start=True, stop=True),
           reads=("fbtb", "t1b"), writes=("pf",), tag=f"E{sg}-t2")
        op("act", lambda e: e.copy(out=t2b[:], in_=pf[0:16, 0:SG]),
           reads=("pf",), writes=("t2b",), tag=f"E{sg}-evt2")
        for fg in range(32):
            bk, pp, co, _ = PABANK[fg % 4]
            for kc in range(8):
                op("pe", lambda e, fg=fg, kc=kc, pp=pp, co=co: e.matmul(
                    pp[:, co:co + SG] if SG == 512 else pp[:, co:co + SG],
                    wfcT[:, kc * 4096 + fg * 128:kc * 4096 + (fg + 1) * 128],
                    x1fm[:, kc * SG:(kc + 1) * SG],
                    start=(kc == 0), stop=False),
                   reads=(f"wfcT{kc}", "x1fm"), writes=(bk,), tag=f"E{sg}-fc{fg}.{kc}")
            op("pe", lambda e, fg=fg, pp=pp, co=co: e.matmul(
                pp[:, co:co + SG], uTg[:, fg * 128:(fg + 1) * 128], t2b[:],
                start=False, stop=True),
               reads=("uTg", "t2b"), writes=(bk,), tag=f"E{sg}-lr{fg}")
            op("act", lambda e, fg=fg, pp=pp, co=co: e.activation(
                out=hr[fg % 2][:, 0:SG], in_=pp[:, co:co + SG], func=AF.Relu),
               reads=(bk,), writes=(f"hr{fg%2}",), tag=f"E{sg}-rl{fg}")
            op("dve", lambda e, fg=fg: e.tensor_tensor(
                out=r2[:, fg * SG:(fg + 1) * SG], in0=hr[fg % 2][:, 0:SG],
                in1=hr[fg % 2][:, 0:SG], op=OP.mult),
               reads=(f"hr{fg%2}",), writes=("r2",), tag=f"E{sg}-r2{fg}")
        for tg in range(NT):
            for nh in range(2):
                for kc in range(32):
                    op("pe", lambda e, tg=tg, nh=nh, kc=kc: e.matmul(
                        pb[:, nh * 512:(nh + 1) * 512],
                        r2[:, kc * SG + tg * 128:kc * SG + (tg + 1) * 128],
                        wprT[:, kc * 1024 + nh * 512:kc * 1024 + (nh + 1) * 512],
                        start=(kc == 0), stop=(kc == 31)),
                       reads=("r2", f"wprT{kc//8}"), writes=(f"pb{nh}",),
                       tag=f"E{sg}-pj{tg}.{nh}.{kc}")
            pq = tg % 2
            op("dve", lambda e, tg=tg, pq=pq: e.scalar_tensor_tensor(
                out=outst[:], in0=pb[:], scalar=1.0,
                in1=x1tm[:, tg * 1024:(tg + 1) * 1024], op0=OP.mult, op1=OP.add,
                accum_out=xsum[:, pq:pq + 1]),
               reads=("pb0", "pb1", f"x1tm{tg}"), writes=("outst", f"xsum{pq}"),
               tag=f"E{sg}-res{tg}")
            op("act", lambda e, pq=pq: e.activation(out=x1bf[:], in_=outst[:],
                                             func=AF.Square, accum_out=x2sum[:, pq:pq + 1]),
               reads=("outst",), writes=("x1bf", f"x2sum{pq}"), tag=f"E{sg}-sq{tg}")
            op("dve", lambda e, pq=pq: e.tensor_scalar(out=mstat[:, pq:pq + 1],
                                                in0=xsum[:, pq:pq + 1],
                                                scalar1=1.0 / DM, scalar2=None,
                                                op0=OP.mult),
               reads=(f"xsum{pq}",), writes=(f"mstat{pq}",), tag=f"E{sg}-m{tg}")
            op("dve", lambda e, pq=pq: e.tensor_tensor(out=mm_t[:, pq:pq + 1],
                                                in0=mstat[:, pq:pq + 1],
                                                in1=mstat[:, pq:pq + 1], op=OP.mult),
               reads=(f"mstat{pq}",), writes=(f"mm_t{pq}",), tag=f"E{sg}-mm{tg}")
            op("dve", lambda e, pq=pq: e.scalar_tensor_tensor(
                out=vstat[:, pq:pq + 1], in0=x2sum[:, pq:pq + 1], scalar=1.0 / DM,
                in1=mm_t[:, pq:pq + 1], op0=OP.mult, op1=OP.subtract),
               reads=(f"x2sum{pq}", f"mm_t{pq}"), writes=(f"vstat{pq}",), tag=f"E{sg}-v{tg}")
            op("act", lambda e, pq=pq: e.activation(out=lnt[:, pq:pq + 1],
                                             in_=vstat[:, pq:pq + 1], func=AF.Ln,
                                             bias=cbias[:, 0:1]),
               reads=(f"vstat{pq}", "cbias"), writes=(f"lnt{pq}",), tag=f"E{sg}-lnv{tg}")
            op("act", lambda e, pq=pq: e.activation(out=rstd[:, pq:pq + 1],
                                             in_=lnt[:, pq:pq + 1], func=AF.Exp,
                                             scale=-0.5),
               reads=(f"lnt{pq}",), writes=(f"rstd{pq}",), tag=f"E{sg}-rs{tg}")
            op("dve", lambda e, pq=pq: e.scalar_tensor_tensor(
                out=b2[:, pq:pq + 1], in0=mstat[:, pq:pq + 1], scalar=-1.0,
                in1=rstd[:, pq:pq + 1], op0=OP.mult, op1=OP.mult),
               reads=(f"mstat{pq}", f"rstd{pq}"), writes=(f"b2{pq}",), tag=f"E{sg}-b2{tg}")
            op("act", lambda e, tg=tg, pq=pq: e.activation(
                out=outst[:], in_=outst[:], func=AF.Identity,
                scale=rstd[:, pq:pq + 1], bias=b2[:, pq:pq + 1]),
               reads=("outst", f"rstd{pq}", f"b2{pq}"), writes=("outst",),
               tag=f"E{sg}-app{tg}")
            op("act", lambda e, sg=sg, tg=tg: e.dma_start(
                out=out_d[sg * SG + tg * 128:sg * SG + (tg + 1) * 128, :],
                in_=outst[:]),
               reads=("outst",), dma="outst", tag=f"E{sg}-st{tg}")


def prep_in_map(inp, b, S=2048):
    f32 = np.float32
    u = np.ascontiguousarray(inp["L0_attn_u"][b], dtype=f32)[:S]
    W_in = np.asarray(inp["L0_attn_W_in"], f32)
    conv_w = np.asarray(inp["L0_attn_conv_w"], f32)
    conv_b = np.asarray(inp["L0_attn_conv_b"], f32)
    dt_bias = np.asarray(inp["L0_attn_dt_bias"], f32)
    A_log = np.asarray(inp["L0_attn_A_log"], f32)
    Dp = np.asarray(inp["L0_attn_D"], f32)
    norm_w = np.asarray(inp["L0_attn_norm_w"], f32)
    W_out = np.asarray(inp["L0_attn_W_out"], f32)
    fast_b = np.asarray(inp["L0_ffn_fast_b"], f32)
    w_fc = np.asarray(inp["L0_ffn_w_fc"], f32)
    w_proj = np.asarray(inp["L0_ffn_w_proj"], f32)
    u_lr = np.asarray(inp["L0_ffn_u"], f32)
    v_lr = np.asarray(inp["L0_ffn_v"], f32)
    gate = np.asarray(inp["L0_ffn_gate"], f32)
    m = {
        "u_tm": u,
        "u_fmT": np.ascontiguousarray(u.T),
        "winT": np.ascontiguousarray(W_in.T),
        "woutT": np.ascontiguousarray(W_out.T),
        "convw": np.ascontiguousarray(
            conv_w[:, 0, :].reshape(NFC, 128, 4).transpose(1, 0, 2).reshape(128, NFC * 4)),
        "convb": np.ascontiguousarray(conv_b.reshape(NFC, 128).T),
        "dtb": np.ascontiguousarray(np.tile(dt_bias, 4)[:, None]),
        "alog": np.ascontiguousarray(np.tile(A_log, 4)[:, None]),
        "nwpk": np.ascontiguousarray(norm_w.reshape(16, 128).T),
        "dflat": np.ascontiguousarray(np.repeat(Dp, HD)[None, :]),
        "identf": np.eye(128, dtype=f32),
        "triu": np.triu(np.ones((128, 128), f32)),
        "maskT": np.triu(np.ones((128, 128), f32)),
        "onesf": np.ones((1, 128), f32),
        "wfc": w_fc,
        "wproj": w_proj,
        "vT": np.ascontiguousarray(v_lr.T),
        "fbT": np.ascontiguousarray(fast_b.T),
        "uTlr": np.ascontiguousarray(u_lr.T),
        "gater": np.full((16, 1), gate, f32),
    }
    return m


def kernel(**inputs):
    S = inputs["L0_attn_u"].shape[1]
    nb = inputs["L0_attn_u"].shape[0]
    nc = build(S=S)
    in_maps = [prep_in_map(inputs, b, S) for b in range(nb)]
    res = run_bass_kernel_spmd(nc, in_maps, core_ids=list(range(nb)))
    return np.stack([np.asarray(r["out"], dtype=np.float32) for r in res.results])
